# version 03
# baseline (speedup 1.0000x reference)
import contextlib
import dataclasses

import ml_dtypes
import numpy as np

import concourse.bass as bass
import concourse.mybir as mybir
from concourse.bass_utils import run_bass_kernel_spmd

F32 = mybir.dt.float32
BF16 = mybir.dt.bfloat16
I16 = mybir.dt.int16
AF = mybir.ActivationFunctionType
OP = mybir.AluOpType
BF16NP = ml_dtypes.bfloat16

C = 1024
T = 2048
NCC = 8
NTT = 16
N = 64
FF = 4096
NE = 8
HEADS = 16
DH = 64
NCH = 16

LN_EPS = 1e-5
ATT_EPS = 1e-6


def _sl(ap_, offset, dims):
    return dataclasses.replace(ap_, offset=int(offset),
                               ap=mybir.VecI64Pair([[int(a), int(b)] for a, b in dims]))


class Sem:
    __slots__ = ("h", "n")

    def __init__(self, h):
        self.h = h
        self.n = 0

    def inc(self, inst, k=1):
        inst.then_inc(self.h, k)
        self.n += k
        return self.n


class Ctx:
    def __init__(self, nc, block, stack):
        self.nc = nc
        self.block = block
        self.stack = stack
        self.bar = self.sem("BAR")
        self._bar_pend = []
        self.t = {}
        self.s = {}
        self.sm = {}

    def sem(self, name):
        return Sem(self.stack.enter_context(self.nc.semaphore(name)))

    def sec(self, eng_name, fn):
        base = self.bar.n

        def wrapped(eng):
            eng.wait_ge(self.bar.h, base)
            fin = fn(eng)
            if fin is not None:
                sem, val = fin
                w = eng.wait_ge(sem.h, val)
                self._bar_pend.append(w)
        getattr(self.block, eng_name)(wrapped)

    def barrier(self):
        for inst in self._bar_pend:
            inst.then_inc(self.bar.h, 1)
            self.bar.n += 1
        self._bar_pend = []

    def chain(self, eng, sem, inst):
        sem.inc(inst)
        eng.wait_ge(sem.h, sem.n)




TRI0 = 1664


def build_program(meta):
    nc = bass.Bass("TRN2", target_bir_lowering=False)
    t = {}

    def din(name, shape, dt=F32):
        t[name] = nc.dram_tensor(name, list(shape), dt, kind="ExternalInput").ap()

    def dout(name, shape, dt=F32):
        t[name] = nc.dram_tensor(name, list(shape), dt, kind="ExternalOutput").ap()

    def dint(name, shape, dt=F32):
        t[name] = nc.dram_tensor(name, list(shape), dt).ap()

    din("xT", [C, T])
    din("xg", [T, C])
    din("wg", [128, 16])
    din("sel", [16 * NE * meta["nchunk"] * 128, 128], BF16)
    din("mixT", [N, C])
    din("stau", [1, N + 1])
    din("pp0", [C, 16])
    din("prow", [1, 3 * C])
    din("fcb", [FF, 2])
    din("fc1T", [C, FF])
    din("fc2T", [FF, C])
    din("W1e", [C, FF])
    din("W2e", [FF, C])
    din("WqT", [C, C]); din("WkT", [C, C]); din("WvT", [C, C]); din("WoT", [C, C])
    din("WrT", [C, C]); din("WkkT", [C, C]); din("WvvT", [C, C])
    din("consts", [1, 4096])
    din("cconst", [128, 16])
    din("idn", [128, 128])
    din("jdn", [128, 128])
    din("tri", [128, 128])
    dout("yT", [C, T])
    dint("kpad", [C, 256], BF16)
    dint("ag_in", [T, C])
    dint("ag_out", [NE * T, C])
    dint("acc_moe", [T, C])
    dint("xspill", [C, T])

    with contextlib.ExitStack() as st:
        def sb(name, shape, dt):
            return st.enter_context(nc.sbuf_tensor(name, shape, dt))

        S = dict(
            R=sb("R", [128, 16384], F32),
            BA=sb("BA", [128, 16384], BF16),
            BB=sb("BB", [128, 17664], BF16),
            BC=sb("BC", [128, 16384], BF16),
            W=sb("W", [128, 8192], BF16),
            RW=sb("RW", [128, 3312], F32),
            SC=sb("SC", [64, 1024], F32),

            CN=sb("CN", [128, 288], F32),
            CB=sb("CB", [128, 3652], BF16),
            SM=sb("SM", [128, 192], F32),
        )
        block = st.enter_context(nc.Block())
        ctx = Ctx(nc, block, st)
        ctx.t, ctx.s = t, S
        for nm in ("LD", "GD", "WD", "MM", "VE", "SE", "GE", "ST", "CCS", "TZ", "TA", "TB", "WA2", "WB2"):
            ctx.sm[nm] = ctx.sem(nm)
        emit(ctx, meta)
        build_program.dbg = dict(S=S, t=t)
    return nc


CONSTS0 = 2048


def crow(S, j, b=0, p=1):
    return S["CN"][b:b + p, 270 + j:270 + j + 1]


def emit(ctx, meta):
    nc, t, S = ctx.nc, ctx.t, ctx.s
    phases = meta.get("phases", 99)
    tap = meta.get("tap", None)
    ST = ctx.sm["ST"]

    emit_p0(ctx, meta)
    ctx.barrier()
    if phases >= 1 and not meta.get("skip_moe", False):
        emit_p1(ctx, meta)
        ctx.barrier()
    if phases >= 2:
        emit_p2(ctx, meta)
        ctx.barrier()
    if phases >= 3:
        emit_p3(ctx, meta)
        ctx.barrier()
    if phases >= 4 and not meta.get("skip_moe", False):
        emit_p4(ctx, meta)
        ctx.barrier()
    if phases >= 5:
        emit_p5(ctx, meta)
        ctx.barrier()

    def pf_sync(eng):
        i = eng.dma_start(
            out=_sl(t["yT"], 0, [[T, 128], [128 * T, NCC], [1, T]]),
            in_=_sl(S["R"].ap(), 0, [[16384, 128], [T, NCC], [1, T]]))
        ST.inc(i, 16)
        return (ST, ST.n)
    ctx.sec("sync", pf_sync)
    ctx.barrier()


def emit_p0(ctx, meta):
    nc, t, S = ctx.nc, ctx.t, ctx.s
    LD, GD, VE, SE, GE = (ctx.sm[k] for k in ("LD", "GD", "VE", "SE", "GE"))

    def p0_sync(eng):
        i = eng.dma_start(out=S["CN"][:, 0:128], in_=t["idn"][:, :])
        LD.inc(i, 16)
        for cc in range(NCC):
            i = eng.dma_start(out=S["SM"][:, 16 * cc:16 * cc + 16],
                              in_=t["pp0"][cc * 128:(cc + 1) * 128, :])
            LD.inc(i, 16)
        for fc in range(FF // 128):
            i = eng.dma_start(out=S["SM"][:, 128 + 2 * fc:128 + 2 * fc + 2],
                              in_=t["fcb"][fc * 128:(fc + 1) * 128, :])
            LD.inc(i, 16)
        i = eng.dma_start(out=S["RW"][0:1, 0:2064], in_=t["consts"][:, 0:2064])
        LD.inc(i, 16)
        i = eng.dma_start(out=S["CN"][:, 270:286], in_=t["cconst"][:, :])
        LD.inc(i, 16)
        i = eng.dma_start(out=S["RW"][32:33, 2112:2112 + N + 1], in_=t["stau"][:, :])
        LD.inc(i, 16)
        return (LD, LD.n)
    ctx.sec("sync", p0_sync)

    def p0_vec(eng):
        eng.memset(S["CN"][0:1, 140:268], 1.0)
        eng.memset(S["CN"][32:33, 140:268], 1.0)
        eng.memset(S["CN"][64:65, 140:268], 1.0)
        eng.memset(S["CN"][:, 129:130], 1.0)
        eng.memset(S["CB"][:, 128:129], 1.0)
        i = eng.memset(S["W"][:, :], 0.0)
        VE.inc(i)
        return (VE, VE.n)
    ctx.sec("vector", p0_vec)

    def p0_gp(eng):
        i = eng.iota(S["CN"][:, 128:129], pattern=[[0, 1]], base=0,
                     channel_multiplier=1, allow_small_or_imprecise_dtypes=True)
        GE.inc(i)
        i = eng.dma_start(out=S["CB"][:, 2240:2240 + 128], in_=t["tri"][:, :])
        GD.inc(i, 16)
        i = eng.dma_start(out=S["CB"][:, 3524:3524 + 128], in_=t["jdn"][:, :])
        GD.inc(i, 16)
        eng.wait_ge(ctx.sm["VE"].h, ctx.sm["VE"].n)
        Wf = S["W"].ap().bitcast(F32)
        for j in range(8):
            i = eng.dma_start(out=t["acc_moe"][j * 256:(j + 1) * 256, :],
                              in_=_sl(Wf, 0, [[4096, 128], [0, 2], [1, 1024]]))
            GD.inc(i, 16)
        return (GD, GD.n)
    ctx.sec("gpsimd", p0_gp)

    def p0_sc(eng):
        eng.wait_ge(LD.h, LD.n)
        i = eng.copy(out=S["CB"][:, 0:128], in_=S["CN"][:, 0:128])
        SE.inc(i)
        return (SE, SE.n)
    ctx.sec("scalar", p0_sc)


def emit_p1(ctx, meta):
    nc, t, S = ctx.nc, ctx.t, ctx.s
    LD, GD, MM, VE, SE, GE = (ctx.sm[k] for k in ("LD", "GD", "MM", "VE", "SE", "GE"))
    WA2, WB2, CCS = ctx.sm["WA2"], ctx.sm["WB2"], ctx.sm["CCS"]
    R, BA, BB, BC, W, RW, CN, CB, SM = (
        S[k] for k in ("R", "BA", "BB", "BC", "W", "RW", "CN", "CB", "SM"))
    onesF = CN[0:1, 140:268]
    WCOL = 3290
    BNC = 3270
    B2R = 2064

    def sy1(eng):
        i = eng.dma_start(
            out=_sl(R.ap(), 0, [[16384, 128], [1024, 16], [1, 1024]]),
            in_=_sl(t["xg"], 0, [[1024, 128], [131072, 16], [1, 1024]]))
        LD.inc(i, 16)
        i = eng.dma_start(out=RW[:, WCOL:WCOL + 16], in_=t["wg"][:, :])
        LD.inc(i, 16)
        i = eng.dma_start(out=RW[0:1, B2R:B2R + C], in_=t["prow"][:, C:2 * C])
        LD.inc(i, 16)
        return (LD, LD.n)
    ctx.sec("sync", sy1)
    ctx.barrier()

    _lnk = meta.get("p1_ln_k", 16)
    _lnm = meta.get("p1_ln_mode", "full")

    def vLN(eng):
        last = None
        for i_ in range(_lnk):
            last = eng.reduce_sum(RW[:, BNC:BNC + 1], R[:, i_ * 1024:(i_ + 1) * 1024],
                                  axis=mybir.AxisListType.X)
            ctx.chain(eng, VE, last)
            last = eng.tensor_scalar_mul(RW[:, BNC + 1:BNC + 2], RW[:, BNC:BNC + 1],
                                         -1.0 / 1024.0)
            VE.inc(last)
            eng.wait_ge(SE.h, ctx._p1_se_base + 2 * i_ + 2)
            last = eng.reciprocal(RW[:, BNC + 4:BNC + 5], RW[:, BNC + 3:BNC + 4])
            ctx.chain(eng, VE, last)
            if _lnm == "nots":
                last = eng.tensor_copy(out=BA[:, i_ * 1024:(i_ + 1) * 1024],
                                       in_=R[:, i_ * 1024:(i_ + 1) * 1024])
            else:
                last = eng.tensor_scalar(BA[:, i_ * 1024:(i_ + 1) * 1024],
                                         R[:, i_ * 1024:(i_ + 1) * 1024],
                                         RW[:, BNC + 1:BNC + 2],
                                         RW[:, BNC + 4:BNC + 5],
                                         OP.add, OP.mult)
            VE.inc(last)
        return (VE, VE.n)

    def sLN(eng):
        last = None
        for i_ in range(_lnk):
            eng.wait_ge(VE.h, ctx._p1_ve_base + i_ * 4 + 2)
            if _lnm == "nosq":
                last = eng.activation(RW[:, BNC + 2:BNC + 3], RW[:, BNC:BNC + 1],
                                      AF.Copy, bias=0.0)
            else:
                last = eng.activation(BA[:, i_ * 1024:(i_ + 1) * 1024],
                                      R[:, i_ * 1024:(i_ + 1) * 1024],
                                      AF.Square, bias=RW[:, BNC + 1:BNC + 2],
                                      accum_out=RW[:, BNC + 2:BNC + 3])
            ctx.chain(eng, SE, last)
            if _lnm == "nosqrt":
                last = eng.activation(RW[:, BNC + 3:BNC + 4], RW[:, BNC + 2:BNC + 3],
                                      AF.Copy, bias=0.0)
            else:
                last = eng.activation(RW[:, BNC + 3:BNC + 4], RW[:, BNC + 2:BNC + 3],
                                      AF.Sqrt, scale=crow(S, 2, 0, 128),
                                      bias=crow(S, 3, 0, 128))
            SE.inc(last)
        return (SE, SE.n)

    ctx._p1_ve_base = ctx.sm["VE"].n
    ctx._p1_se_base = ctx.sm["SE"].n
    if meta.get("p1_stage", 9) < 1:
        return
    ctx.sec("vector", vLN)
    ctx.sec("scalar", sLN)
    ctx.barrier()
    if meta.get("p1_stage", 9) < 2:
        return

    _tsw = 1024
    with nc.psum_tensor("psT1", [128, 2 * _tsw], BF16) as psT1:
        jobs = [(i_, cc) for i_ in range(16) for cc in range(NCC)]
        jobs = jobs[meta.get("p1_tr_lo", 0):meta.get("p1_tr_k", len(jobs))]
        mm_tr = MM.n
        se_tr = SE.n

        def tT(eng):
            last = None
            for j, (i_, cc) in enumerate(jobs):
                if j >= 2:
                    eng.wait_ge(SE.h, se_tr + j - 1)
                last = eng.transpose(psT1[:, (j % 2) * _tsw:(j % 2) * _tsw + 128],
                                     BA[:, i_ * 1024 + cc * 128: i_ * 1024 + cc * 128 + 128],
                                     CB[:, 0:128])
                MM.inc(last)
            return (MM, MM.n)
        ctx.sec("tensor", tT)

        def sT(eng):
            last = None
            for j, (i_, cc) in enumerate(jobs):
                eng.wait_ge(MM.h, mm_tr + j + 1)
                if meta.get("p1_tr_ev") == "vector":
                    last = eng.tensor_copy(
                        out=BC[:, cc * T + i_ * 128: cc * T + i_ * 128 + 128],
                        in_=psT1[:, (j % 2) * _tsw:(j % 2) * _tsw + 128])
                else:
                    last = eng.copy(
                        out=BC[:, cc * T + i_ * 128: cc * T + i_ * 128 + 128],
                        in_=psT1[:, (j % 2) * _tsw:(j % 2) * _tsw + 128])
                SE.inc(last)
            return (SE, SE.n)
        ctx.sec("vector" if meta.get("p1_tr_ev") == "vector" else "scalar", sT)
        ctx.barrier()
    if meta.get("p1_stage", 9) < 3:
        return

    with nc.psum_tensor("psM1", [128, 1024], F32) as psM1, \
         nc.psum_tensor("psM2", [128, 2048], F32) as psM2:
        mm0 = MM.n
        se0 = SE.n
        ve0 = VE.n
        wa0, wb0 = WA2.n, WB2.n

        def mmq(Q):
            return mm0 + Q * 48

        def gpW(eng):
            for Q in range(4):
                for e2 in range(2):
                    ge = Q * 2 + e2
                    if ge >= 1:
                        eng.wait_ge(MM.h, mmq((ge - 1) // 2) + ((ge - 1) % 2) * 16 + 16)
                        if e2 == 0:
                            eng.wait_ge(MM.h, mmq(Q))
                    i = eng.dma_start(
                        out=_sl(W.ap(), 0, [[8192, 128], [512, 8], [1, 512]]),
                        in_=_sl(t["W1e"], 512 * ge, [[4096, 128], [4096 * 128, 8], [1, 512]]))
                    WA2.inc(i, 16)
                eng.wait_ge(MM.h, mmq(Q) + 32)
                i = eng.dma_start(
                    out=_sl(W.ap(), 0, [[8192, 128], [1024, 8], [1, 1024]]),
                    in_=_sl(t["W2e"], 1024 * Q * 1024,
                            [[1024, 128], [128 * 1024, 8], [1, 1024]]))
                WB2.inc(i, 16)
            return (WB2, WB2.n)
        ctx.sec("gpsimd", gpW)

        def tM(eng):
            last = None
            for Q in range(4):
                for e2 in range(2):
                    ge = Q * 2 + e2
                    eng.wait_ge(WA2.h, wa0 + 16 * (ge + 1))
                    for j, (sq, fj) in enumerate([(a, b) for a in range(4) for b in range(4)]):
                        gj = ge * 16 + j
                        if gj >= 2:
                            eng.wait_ge(SE.h, se0 + gj - 1)
                        for cc in range(NCC):
                            last = eng.matmul(
                                psM1[:, (j % 2) * 512:(j % 2) * 512 + 512],
                                W[:, cc * 512 + fj * 128: cc * 512 + fj * 128 + 128],
                                BC[:, cc * T + sq * 512: cc * T + (sq + 1) * 512],
                                start=(cc == 0), stop=(cc == NCC - 1))
                        MM.inc(last)
                eng.wait_ge(WB2.h, wb0 + 16 * (Q + 1))
                eng.wait_ge(SE.h, se0 + (Q + 1) * 32)
                for st in range(16):
                    gs = Q * 16 + st
                    if gs >= 2:
                        eng.wait_ge(VE.h, ve0 + gs - 1)
                    addb2 = (Q == 0)
                    for fj8 in range(8):
                        lastf = (fj8 == 7) and not addb2
                        for hv in range(2):
                            last = eng.matmul(
                                psM2[:, (st % 2) * 1024 + hv * 512:(st % 2) * 1024 + (hv + 1) * 512],
                                BB[:, fj8 * 2048 + st * 128: fj8 * 2048 + st * 128 + 128],
                                W[:, fj8 * 1024 + hv * 512: fj8 * 1024 + (hv + 1) * 512],
                                start=(fj8 == 0), stop=lastf)
                    if addb2:
                        for hv in range(2):
                            last = eng.matmul(
                                psM2[:, (st % 2) * 1024 + hv * 512:(st % 2) * 1024 + (hv + 1) * 512],
                                onesF, RW[0:1, B2R + hv * 512:B2R + (hv + 1) * 512],
                                start=False, stop=True)
                    MM.inc(last)
            return (MM, MM.n)
        ctx.sec("tensor", tM)

        def sM(eng):
            last = None
            for Q in range(4):
                for e2 in range(2):
                    ge = Q * 2 + e2
                    for j, (sq, fj) in enumerate([(a, b) for a in range(4) for b in range(4)]):
                        eng.wait_ge(MM.h, mmq(Q) + e2 * 16 + j + 1)
                        fl = e2 * 4 + fj
                        fg = Q * 8 + fl
                        last = eng.activation(
                            BB[:, fl * 2048 + sq * 512: fl * 2048 + (sq + 1) * 512],
                            psM1[:, (j % 2) * 512:(j % 2) * 512 + 512],
                            AF.Gelu_apprx_tanh, bias=SM[:, 129 + 2 * fg:129 + 2 * fg + 1])
                        SE.inc(last)
            return (SE, SE.n)
        ctx.sec("scalar", sM)

        def vM(eng):
            last = None
            for Q in range(4):
                for st in range(16):
                    eng.wait_ge(MM.h, mmq(Q) + 32 + st + 1)
                    dst = R[:, st * 1024:(st + 1) * 1024]
                    srcp = psM2[:, (st % 2) * 1024:(st % 2) * 1024 + 1024]
                    if Q == 0:
                        last = eng.tensor_copy(out=dst, in_=srcp)
                    else:
                        last = eng.tensor_add(dst, dst, srcp)
                    VE.inc(last)
            return (VE, VE.n)
        ctx.sec("vector", vM)
        ctx.barrier()
    if meta.get("p1_stage", 9) < 4:
        return

    def sW(eng):
        last = None
        for st in range(16):
            last = eng.activation(R[:, st * 1024:(st + 1) * 1024],
                                  R[:, st * 1024:(st + 1) * 1024],
                                  AF.Copy, scale=RW[:, WCOL + st:WCOL + st + 1])
            SE.inc(last)
        return (SE, SE.n)
    ctx.sec("scalar", sW)
    ctx.barrier()

    def syA(eng):
        i = eng.dma_start(
            out=_sl(t["ag_in"], 0, [[1024, 128], [131072, 16], [1, 1024]]),
            in_=_sl(R.ap(), 0, [[16384, 128], [1024, 16], [1, 1024]]))
        LD.inc(i, 16)
        return (LD, LD.n)
    ctx.sec("sync", syA)

    def gpA(eng):
        eng.wait_ge(LD.h, LD.n)
        i = eng.collective_compute(
            "AllGather", mybir.AluOpType.bypass,
            replica_groups=[list(range(8))],
            ins=[t["ag_in"]], outs=[t["ag_out"]])
        CCS.inc(i)
        return None
    ctx.sec("gpsimd", gpA)


def emit_p3(ctx, meta):
    nc, t, S = ctx.nc, ctx.t, ctx.s
    LD, GD, MM, VE, SE = (ctx.sm[k] for k in ("LD", "GD", "MM", "VE", "SE"))
    WA2, WB2 = ctx.sm["WA2"], ctx.sm["WB2"]
    R, BA, BB, BC, W, RW, CN, CB, SM = (
        S[k] for k in ("R", "BA", "BB", "BC", "W", "RW", "CN", "CB", "SM"))
    ones_b = CB[:, 128:129]
    onesF32 = CN[32:33, 140:268]
    onesF64 = CN[64:65, 140:268]
    Rv = _sl(R.ap(), 0, [[16384, 128], [T, NCC], [1, T]])
    BCv = _sl(BC.ap(), 0, [[16384, 128], [T, NCC], [1, T]])

    def brc(ps, w):
        return _sl(ps.ap(), 0, [[2048, 128], [0, NCC], [1, w]])

    with nc.psum_tensor("psQ3", [1, 512], F32) as psQ, \
         nc.psum_tensor("psREP3", [128, 2048], F32) as psREP:

        def s1(eng):
            i = eng.copy(out=BC[:, :], in_=R[:, :])
            SE.inc(i)
            return (SE, SE.n)
        ctx.sec("scalar", s1)
        if meta.get("p5_ln", 9) < 2:
            return

        se_cast = SE.n
        mm_q = MM.n
        se_q = SE.n

        def t1(eng):
            eng.wait_ge(SE.h, se_cast)
            last = None
            for q in range(4):
                if q >= 1:
                    eng.wait_ge(SE.h, se_q + q)
                for cc in range(NCC):
                    last = eng.matmul(psQ[0:1, :], ones_b,
                                      BC[:, cc * T + q * 512: cc * T + (q + 1) * 512],
                                      start=(cc == 0), stop=(cc == NCC - 1))
                MM.inc(last)
            return (MM, MM.n)
        ctx.sec("tensor", t1)

        def s2(eng):
            last = None
            for q in range(4):
                eng.wait_ge(MM.h, mm_q + q + 1)
                last = eng.activation(RW[32:33, q * 512:(q + 1) * 512], psQ[0:1, :],
                                      AF.Copy, scale=crow(S, 2, 32))
                SE.inc(last)
            return (SE, SE.n)
        ctx.sec("scalar", s2)
        if meta.get("p5_ln", 9) < 3:
            return

        se_mu = SE.n

        def t2(eng):
            eng.wait_ge(SE.h, se_mu)
            last = None
            for q in range(4):
                last = eng.matmul(psREP[:, q * 512:(q + 1) * 512], onesF32,
                                  RW[32:33, q * 512:(q + 1) * 512], start=True, stop=True)
            MM.inc(last)
            return (MM, MM.n)
        ctx.sec("tensor", t2)

        mm_murep = MM.n

        def v1(eng):
            eng.wait_ge(MM.h, mm_murep)
            i = eng.tensor_sub(BCv, Rv, brc(psREP, 2048))
            VE.inc(i)
            return (VE, VE.n)
        ctx.sec("vector", v1)
        if meta.get("p5_ln", 9) < 4:
            return

        ve_xc = VE.n

        def s3(eng):
            eng.wait_ge(VE.h, ve_xc)
            i = eng.activation(BA[:, :], BC[:, :], AF.Square)
            SE.inc(i)
            return (SE, SE.n)
        ctx.sec("scalar", s3)

        se_sq = SE.n
        mm_q2 = MM.n
        se_q2 = SE.n

        def t3(eng):
            eng.wait_ge(SE.h, se_sq)
            last = None
            for q in range(4):
                if q >= 1:
                    eng.wait_ge(SE.h, se_q2 + q)
                for cc in range(NCC):
                    last = eng.matmul(psQ[0:1, :], ones_b,
                                      BA[:, cc * T + q * 512: cc * T + (q + 1) * 512],
                                      start=(cc == 0), stop=(cc == NCC - 1))
                MM.inc(last)
            return (MM, MM.n)
        ctx.sec("tensor", t3)

        def s4(eng):
            last = None
            for q in range(4):
                eng.wait_ge(MM.h, mm_q2 + q + 1)
                last = eng.activation(RW[96:97, q * 512:(q + 1) * 512], psQ[0:1, :],
                                      AF.Copy, scale=crow(S, 2, 96))
                SE.inc(last)
            eng.wait_ge(SE.h, SE.n)
            last = eng.activation(RW[64:65, 0:2048], RW[96:97, 0:2048], AF.Sqrt,
                                  bias=crow(S, 3, 64))
            SE.inc(last)
            return (SE, SE.n)
        ctx.sec("scalar", s4)
        if meta.get("p5_ln", 9) < 5:
            return

        se_s = SE.n

        def v2(eng):
            eng.wait_ge(SE.h, se_s)
            i = eng.reciprocal(RW[64:65, 0:2048], RW[64:65, 0:2048])
            VE.inc(i)
            return (VE, VE.n)
        ctx.sec("vector", v2)

        ve_rs = VE.n

        def t4(eng):
            eng.wait_ge(VE.h, ve_rs)
            last = None
            for q in range(4):
                last = eng.matmul(psREP[:, q * 512:(q + 1) * 512], onesF64,
                                  RW[64:65, q * 512:(q + 1) * 512], start=True, stop=True)
            MM.inc(last)
            return (MM, MM.n)
        ctx.sec("tensor", t4)

        mm_rsrep = MM.n

        def v3(eng):
            eng.wait_ge(MM.h, mm_rsrep)
            i = eng.tensor_mul(BCv, BCv, brc(psREP, 2048))
            VE.inc(i)
            return (VE, VE.n)
        ctx.sec("vector", v3)

        ve_norm = VE.n

        def s5(eng):
            eng.wait_ge(VE.h, ve_norm)
            last = None
            for cc in range(NCC):
                last = eng.activation(BA[:, cc * T:(cc + 1) * T], BC[:, cc * T:(cc + 1) * T],
                                      AF.Copy, scale=SM[:, 16 * cc + 3:16 * cc + 4])
                SE.inc(last)
            return (SE, SE.n)
        ctx.sec("scalar", s5)

        se_g = SE.n

        def v4(eng):
            eng.wait_ge(SE.h, se_g)
            last = None
            for cc in range(NCC):
                last = eng.tensor_scalar_add(BA[:, cc * T:(cc + 1) * T],
                                             BA[:, cc * T:(cc + 1) * T],
                                             SM[:, 16 * cc + 4:16 * cc + 5])
                VE.inc(last)
            return (VE, VE.n)
        ctx.sec("vector", v4)

        ctx.barrier()

    with nc.psum_tensor("psF1", [128, 1024], F32) as psF1, \
         nc.psum_tensor("psF2", [128, 1024], F32) as psF2:
        NE8 = 8
        mm0 = MM.n
        se0 = SE.n
        ve0 = VE.n
        wa0, wb0 = WA2.n, WB2.n

        def mm_counts(e):
            return mm0 + e * 48

        def gpW(eng):
            for e in range(NE8):
                if e >= 1:
                    eng.wait_ge(MM.h, mm_counts(e - 1) + 16)
                i = eng.dma_start(
                    out=_sl(W.ap(), 0, [[8192, 128], [512, 8], [1, 512]]),
                    in_=_sl(t["fc1T"], 512 * e, [[4096, 128], [4096 * 128, 8], [1, 512]]))
                WA2.inc(i, 16)
                if e >= 1:
                    eng.wait_ge(MM.h, mm_counts(e - 1) + 48)
                i = eng.dma_start(
                    out=_sl(W.ap(), 4096, [[8192, 128], [1024, 4], [1, 1024]]),
                    in_=_sl(t["fc2T"], 512 * e * 1024, [[1024, 128], [128 * 1024, 4], [1, 1024]]))
                WB2.inc(i, 16)
            return (WB2, WB2.n)
        ctx.sec("gpsimd", gpW)

        def tF(eng):
            last = None
            for e in range(NE8):
                eng.wait_ge(WA2.h, wa0 + 16 * (e + 1))
                for j, (tq, fj) in enumerate([(a, b) for a in range(4) for b in range(4)]):
                    gj = e * 16 + j
                    if gj >= 2:
                        eng.wait_ge(SE.h, se0 + gj - 1)
                    for cc in range(NCC):
                        last = eng.matmul(
                            psF1[:, (j % 2) * 512:(j % 2) * 512 + 512],
                            W[:, cc * 512 + fj * 128: cc * 512 + fj * 128 + 128],
                            BA[:, cc * T + tq * 512: cc * T + (tq + 1) * 512],
                            start=(cc == 0), stop=(cc == NCC - 1))
                    MM.inc(last)
                eng.wait_ge(WB2.h, wb0 + 16 * (e + 1))
                for j, (tq, cc) in enumerate([(a, b) for a in range(4) for b in range(NCC)]):
                    jj = e * 32 + j
                    if jj >= 2:
                        eng.wait_ge(VE.h, ve0 + jj - 1)
                    eng.wait_ge(SE.h, se0 + e * 16 + tq * 4 + 4)
                    for fj in range(4):
                        last = eng.matmul(
                            psF2[:, (j % 2) * 512:(j % 2) * 512 + 512],
                            W[:, 4096 + fj * 1024 + cc * 128: 4096 + fj * 1024 + cc * 128 + 128],
                            BB[:, fj * 2048 + tq * 512: fj * 2048 + (tq + 1) * 512],
                            start=(fj == 0), stop=(fj == 3))
                    MM.inc(last)
            return (MM, MM.n)
        ctx.sec("tensor", tF)

        def sF(eng):
            last = None
            for e in range(NE8):
                for j, (tq, fj) in enumerate([(a, b) for a in range(4) for b in range(4)]):
                    eng.wait_ge(MM.h, mm_counts(e) + j + 1)
                    fcg = 4 * e + fj
                    last = eng.activation(
                        BB[:, fj * 2048 + tq * 512: fj * 2048 + (tq + 1) * 512],
                        psF1[:, (j % 2) * 512:(j % 2) * 512 + 512],
                        AF.Gelu, bias=SM[:, 128 + 2 * fcg:128 + 2 * fcg + 1])
                    SE.inc(last)
            return (SE, SE.n)
        ctx.sec("scalar", sF)

        def vF(eng):
            last = None
            for e in range(NE8):
                for j, (tq, cc) in enumerate([(a, b) for a in range(4) for b in range(NCC)]):
                    eng.wait_ge(MM.h, mm_counts(e) + 16 + j + 1)
                    dst = R[:, cc * T + tq * 512: cc * T + (tq + 1) * 512]
                    if e == 0:
                        last = eng.scalar_tensor_tensor(
                            dst, psF2[:, (j % 2) * 512:(j % 2) * 512 + 512],
                            SM[:, 16 * cc + 5:16 * cc + 6], dst, OP.add, OP.add)
                    else:
                        last = eng.tensor_add(
                            dst, dst, psF2[:, (j % 2) * 512:(j % 2) * 512 + 512])
                    VE.inc(last)
            return (VE, VE.n)
        ctx.sec("vector", vF)

        ctx.barrier()

    def gpX(eng):
        i = eng.dma_start(out=_sl(R.ap(), 0, [[16384, 128], [T, NCC], [1, T]]),
                          in_=_sl(t["xT"], 0, [[T, 128], [128 * T, NCC], [1, T]]),
                          accum_op=OP.add)
        GD.inc(i, 16)
        return (GD, GD.n)
    ctx.sec("gpsimd", gpX)
    ctx.barrier()


def emit_p4(ctx, meta):
    nc, t, S = ctx.nc, ctx.t, ctx.s
    LD, GD, MM, VE, SE, CCS = (ctx.sm[k] for k in ("LD", "GD", "MM", "VE", "SE", "CCS"))
    TA, TB2 = ctx.sm["TA"], ctx.sm["TB"]
    WA2, WB2 = ctx.sm["WA2"], ctx.sm["WB2"]
    R, BA, BB, BC, W, CN, CB = (S[k] for k in ("R", "BA", "BB", "BC", "W", "CN", "CB"))
    NCH4 = meta["nchunk"]
    rowstart = meta["rowstart"]
    mm0 = MM.n
    ve0 = VE.n
    ld0 = LD.n
    mmT_pred = mm0 + 16 * NE * NCH4

    wdb = (2 * NE * NCH4 * 128 <= 8192)
    wsz = NE * NCH4 * 128

    def syS(eng):
        last = None
        per = NE * NCH4 * 128
        for ti in range(16):
            if wdb:
                if ti >= 2:
                    eng.wait_ge(MM.h, mm0 + (ti - 1) * NE * NCH4)
            elif ti >= 1:
                eng.wait_ge(MM.h, mm0 + ti * NE * NCH4)
            last = eng.dma_start(
                out=_sl(W.ap(), (ti % 2) * wsz if wdb else 0,
                        [[8192, 128], [128, NE * NCH4], [1, 128]]),
                in_=_sl(t["sel"], ti * per * 128,
                        [[128, 128], [128 * 128, NE * NCH4], [1, 128]]))
            (TA if ti % 2 == 0 else TB2).inc(last, 16)
        eng.wait_ge(TA.h, TA.n)
        return (TB2, TB2.n)
    ta0 = ctx.sm["TA"].n
    tb0 = ctx.sm["TB"].n
    ctx.sec("sync", syS)

    wa0, wb0 = WA2.n, WB2.n

    def gpS(eng):
        eng.wait_ge(CCS.h, CCS.n)
        last = None
        for ti in range(16):
            for e in range(NE):
                for ch in range(NCH4):
                    j = (ti * NE + e) * NCH4 + ch
                    if j >= 2:
                        eng.wait_ge(MM.h, mm0 + j - 1)
                    r0 = e * 2048 + int(rowstart[e][ti][ch])
                    last = eng.dma_start(
                        out=_sl(BC.ap(), (j % 2) * 1024, [[16384, 128], [1, 1024]]),
                        in_=t["ag_out"][r0:r0 + 128, :])
                    (WA2 if j % 2 == 0 else WB2).inc(last, 16)
        return (WA2 if j % 2 == 0 else WB2,
                (WA2 if j % 2 == 0 else WB2).n)
    ctx.sec("gpsimd", gpS)

    with nc.psum_tensor("psC4", [128, 2048], F32) as psC, \
         nc.psum_tensor("psT4", [128, 2048], BF16) as psT4:

        def tS(eng):
            last = None
            for ti in range(16):
                sem = TA if ti % 2 == 0 else TB2
                base = ta0 if ti % 2 == 0 else tb0
                eng.wait_ge(sem.h, base + 16 * (ti // 2 + 1))
                if ti >= 2:
                    eng.wait_ge(VE.h, ve0 + ti - 1)
                for e in range(NE):
                    for ch in range(NCH4):
                        j = (ti * NE + e) * NCH4 + ch
                        eng.wait_ge((WA2 if j % 2 == 0 else WB2).h,
                                    (wa0 if j % 2 == 0 else wb0) + 16 * (j // 2 + 1))
                        wof = (ti % 2) * wsz if wdb else 0
                        for hv in range(2):
                            last = eng.matmul(
                                psC[:, (ti % 2) * 1024 + hv * 512:(ti % 2) * 1024 + (hv + 1) * 512],
                                W[:, wof + ((e * NCH4) + ch) * 128:wof + ((e * NCH4) + ch) * 128 + 128],
                                _sl(BC.ap(), (j % 2) * 1024 + hv * 512, [[16384, 128], [1, 512]]),
                                start=(e == 0 and ch == 0), stop=(e == NE - 1 and ch == NCH4 - 1))
                        MM.inc(last)
            return (MM, MM.n)
        ctx.sec("tensor", tS)

        def vS(eng):
            last = None
            for ti in range(16):
                eng.wait_ge(MM.h, mm0 + (ti * NE + NE - 1) * NCH4 + NCH4)
                last = eng.tensor_copy(
                    out=_sl(BB.ap(), ti * 1024, [[17664, 128], [1, 1024]]),
                    in_=psC[:, (ti % 2) * 1024:(ti % 2) * 1024 + 1024])
                VE.inc(last)
            return (VE, VE.n)
        ctx.sec("vector", vS)

        mmT = MM.n
        veT = VE.n

        def tT(eng):
            last = None
            for ti in range(16):
                eng.wait_ge(VE.h, ve0 + ti + 1)
                for cc in range(NCC):
                    j = ti * NCC + cc
                    if j >= 2:
                        eng.wait_ge(VE.h, veT + j - 1)
                    last = eng.transpose(
                        psT4[:, (j % 2) * 1024:(j % 2) * 1024 + 128],
                        _sl(BB.ap(), ti * 1024 + cc * 128, [[17664, 128], [1, 128]]),
                        CB[:, 0:128])
                    MM.inc(last)
            return (MM, MM.n)
        ctx.sec("tensor", tT)

        def vT(eng):
            last = None
            for ti in range(16):
                for cc in range(NCC):
                    j = ti * NCC + cc
                    eng.wait_ge(MM.h, mmT + j + 1)
                    dst = R[:, cc * T + ti * 128: cc * T + ti * 128 + 128]
                    last = eng.tensor_add(dst, dst,
                                          psT4[:, (j % 2) * 1024:(j % 2) * 1024 + 128])
                    VE.inc(last)
            return (VE, VE.n)
        ctx.sec("vector", vT)
    ctx.barrier()


def emit_p5(ctx, meta):
    nc, t, S = ctx.nc, ctx.t, ctx.s
    LD, GD, MM, VE, SE = (ctx.sm[k] for k in ("LD", "GD", "MM", "VE", "SE"))
    WA2, WB2 = ctx.sm["WA2"], ctx.sm["WB2"]
    R, BA, BB, BC, W, RW, CN, CB, SM = (
        S[k] for k in ("R", "BA", "BB", "BC", "W", "RW", "CN", "CB", "SM"))
    ones_b = CB[:, 128:129]
    onesF32 = CN[32:33, 140:268]
    onesF64 = CN[64:65, 140:268]
    Rv = _sl(R.ap(), 0, [[16384, 128], [T, NCC], [1, T]])
    BAv = _sl(BA.ap(), 0, [[16384, 128], [T, NCC], [1, T]])
    BCv = _sl(BC.ap(), 0, [[16384, 128], [T, NCC], [1, T]])
    Rb = R.ap().bitcast(BF16)

    def brc(ps, w):
        return _sl(ps.ap(), 0, [[2048, 128], [0, NCC], [1, w]])

    with nc.psum_tensor("psQ5", [1, 512], F32) as psQ, \
         nc.psum_tensor("psREP5", [128, 2048], F32) as psREP:

        def s1(eng):
            i = eng.copy(out=BC[:, :], in_=R[:, :])
            SE.inc(i)
            return (SE, SE.n)
        ctx.sec("scalar", s1)
        if meta.get("p5_ln", 9) < 2:
            return

        se_cast = SE.n
        mm_q = MM.n
        se_q = SE.n

        def t1(eng):
            eng.wait_ge(SE.h, se_cast)
            last = None
            for q in range(4):
                if q >= 1:
                    eng.wait_ge(SE.h, se_q + q)
                for cc in range(NCC):
                    last = eng.matmul(psQ[0:1, :], ones_b,
                                      BC[:, cc * T + q * 512: cc * T + (q + 1) * 512],
                                      start=(cc == 0), stop=(cc == NCC - 1))
                MM.inc(last)
            return (MM, MM.n)
        ctx.sec("tensor", t1)

        def s2(eng):
            last = None
            for q in range(4):
                eng.wait_ge(MM.h, mm_q + q + 1)
                last = eng.activation(RW[32:33, q * 512:(q + 1) * 512], psQ[0:1, :],
                                      AF.Copy, scale=crow(S, 2, 32))
                SE.inc(last)
            return (SE, SE.n)
        ctx.sec("scalar", s2)
        if meta.get("p5_ln", 9) < 3:
            return

        se_mu = SE.n

        def t2(eng):
            eng.wait_ge(SE.h, se_mu)
            last = None
            for q in range(4):
                last = eng.matmul(psREP[:, q * 512:(q + 1) * 512], onesF32,
                                  RW[32:33, q * 512:(q + 1) * 512], start=True, stop=True)
            MM.inc(last)
            return (MM, MM.n)
        ctx.sec("tensor", t2)

        mm_murep = MM.n

        def v1(eng):
            eng.wait_ge(MM.h, mm_murep)
            i = eng.tensor_sub(BCv, Rv, brc(psREP, 2048))
            VE.inc(i)
            return (VE, VE.n)
        ctx.sec("vector", v1)
        if meta.get("p5_ln", 9) < 4:
            return

        ve_xc = VE.n

        def s3(eng):
            eng.wait_ge(VE.h, ve_xc)
            i = eng.activation(BA[:, :], BC[:, :], AF.Square)
            SE.inc(i)
            return (SE, SE.n)
        ctx.sec("scalar", s3)

        se_sq = SE.n
        mm_q2 = MM.n
        se_q2 = SE.n

        def t3(eng):
            eng.wait_ge(SE.h, se_sq)
            last = None
            for q in range(4):
                if q >= 1:
                    eng.wait_ge(SE.h, se_q2 + q)
                for cc in range(NCC):
                    last = eng.matmul(psQ[0:1, :], ones_b,
                                      BA[:, cc * T + q * 512: cc * T + (q + 1) * 512],
                                      start=(cc == 0), stop=(cc == NCC - 1))
                MM.inc(last)
            return (MM, MM.n)
        ctx.sec("tensor", t3)

        def s4(eng):
            last = None
            for q in range(4):
                eng.wait_ge(MM.h, mm_q2 + q + 1)
                last = eng.activation(RW[96:97, q * 512:(q + 1) * 512], psQ[0:1, :],
                                      AF.Copy, scale=crow(S, 2, 96))
                SE.inc(last)
            eng.wait_ge(SE.h, SE.n)
            last = eng.activation(RW[64:65, 0:2048], RW[96:97, 0:2048], AF.Sqrt,
                                  bias=crow(S, 3, 64))
            SE.inc(last)
            return (SE, SE.n)
        ctx.sec("scalar", s4)
        if meta.get("p5_ln", 9) < 5:
            return

        se_s = SE.n

        def v2(eng):
            eng.wait_ge(SE.h, se_s)
            i = eng.reciprocal(RW[64:65, 0:2048], RW[64:65, 0:2048])
            VE.inc(i)
            return (VE, VE.n)
        ctx.sec("vector", v2)

        ve_rs = VE.n

        def t4(eng):
            eng.wait_ge(VE.h, ve_rs)
            last = None
            for q in range(4):
                last = eng.matmul(psREP[:, q * 512:(q + 1) * 512], onesF64,
                                  RW[64:65, q * 512:(q + 1) * 512], start=True, stop=True)
            MM.inc(last)
            return (MM, MM.n)
        ctx.sec("tensor", t4)

        mm_rsrep = MM.n

        def v3(eng):
            eng.wait_ge(MM.h, mm_rsrep)
            i = eng.tensor_mul(BAv, BCv, brc(psREP, 2048))
            VE.inc(i)
            return (VE, VE.n)
        ctx.sec("vector", v3)
        if meta.get("p5_ln", 9) < 6:
            return

        def syS(eng):
            i = eng.dma_start(
                out=_sl(t["xspill"], 0, [[T, 128], [128 * T, NCC], [1, T]]),
                in_=_sl(R.ap(), 0, [[16384, 128], [T, NCC], [1, T]]))
            LD.inc(i, 16)
            return (LD, LD.n)
        ctx.sec("sync", syS)
        ctx.barrier()
        if meta.get("p5_stage", 99) < 2:
            return

    def proj(eng, src, psum, evac_engine_jobs, mm_base, ve_or_se, dst_cb):
        pass

    with nc.psum_tensor("psP5", [128, 1024], F32) as psF:
        def vxr(eng):
            last = None
            for cc in range(NCC):
                eng.tensor_sub(BC[:, cc * T + 1: cc * T + T],
                               BA[:, cc * T + 1: cc * T + T],
                               BA[:, cc * T: cc * T + T - 1])
                last = eng.tensor_copy(out=BC[:, cc * T: cc * T + 1],
                                       in_=BA[:, cc * T: cc * T + 1])
                VE.inc(last)
            return (VE, VE.n)
        ctx.sec("vector", vxr)

        ve_d = VE.n

        def sxr(eng):
            eng.wait_ge(VE.h, ve_d)
            last = None
            for cc in range(NCC):
                last = eng.activation(BC[:, cc * T:(cc + 1) * T], BC[:, cc * T:(cc + 1) * T],
                                      AF.Copy, scale=SM[:, 16 * cc + 6:16 * cc + 7])
                SE.inc(last)
            return (SE, SE.n)
        ctx.sec("scalar", sxr)

        se_m = SE.n

        def vxr2(eng):
            eng.wait_ge(SE.h, se_m)
            last = None
            for cc in range(NCC):
                last = eng.tensor_add(BC[:, cc * T + 1: cc * T + T],
                                      BC[:, cc * T + 1: cc * T + T],
                                      BA[:, cc * T: cc * T + T - 1])
                VE.inc(last)
            return (VE, VE.n)
        ctx.sec("vector", vxr2)
        ctx.barrier()

        def gpw1(eng):
            i = eng.dma_start(out=_sl(W.ap(), 0, [[8192, 128], [1024, 8], [1, 1024]]),
                              in_=_sl(t["WrT"], 0, [[1024, 128], [128 * 1024, 8], [1, 1024]]))
            WA2.inc(i, 16)
            return (WA2, WA2.n)
        ctx.sec("gpsimd", gpw1)

        mm_r = MM.n
        se_r = SE.n
        wa_r = WA2.n

        def tr(eng):
            eng.wait_ge(WA2.h, wa_r)
            last = None
            for j, (dc, tq) in enumerate([(a, b) for a in range(NCC) for b in range(4)]):
                if j >= 2:
                    eng.wait_ge(SE.h, se_r + j - 1)
                for cc in range(NCC):
                    last = eng.matmul(psF[:, (j % 2) * 512:(j % 2) * 512 + 512],
                                      W[:, cc * 1024 + dc * 128: cc * 1024 + dc * 128 + 128],
                                      BC[:, cc * T + tq * 512: cc * T + (tq + 1) * 512],
                                      start=(cc == 0), stop=(cc == NCC - 1))
                MM.inc(last)
            return (MM, MM.n)
        ctx.sec("tensor", tr)

        def sr(eng):
            last = None
            for j, (dc, tq) in enumerate([(a, b) for a in range(NCC) for b in range(4)]):
                eng.wait_ge(MM.h, mm_r + j + 1)
                last = eng.activation(BB[:, dc * T + tq * 512: dc * T + (tq + 1) * 512],
                                      psF[:, (j % 2) * 512:(j % 2) * 512 + 512],
                                      AF.Sigmoid)
                SE.inc(last)
            return (SE, SE.n)
        ctx.sec("scalar", sr)
        ctx.barrier()

        def vxk(eng):
            last = None
            for cc in range(NCC):
                eng.tensor_sub(BC[:, cc * T + 1: cc * T + T],
                               BA[:, cc * T + 1: cc * T + T],
                               BA[:, cc * T: cc * T + T - 1])
                last = eng.tensor_copy(out=BC[:, cc * T: cc * T + 1],
                                       in_=BA[:, cc * T: cc * T + 1])
                VE.inc(last)
            return (VE, VE.n)
        ctx.sec("vector", vxk)

        ve_d2 = VE.n

        def sxk(eng):
            eng.wait_ge(VE.h, ve_d2)
            last = None
            for cc in range(NCC):
                last = eng.activation(BC[:, cc * T:(cc + 1) * T], BC[:, cc * T:(cc + 1) * T],
                                      AF.Copy, scale=SM[:, 16 * cc + 7:16 * cc + 8])
                SE.inc(last)
            return (SE, SE.n)
        ctx.sec("scalar", sxk)

        se_m2 = SE.n

        def vxk2(eng):
            eng.wait_ge(SE.h, se_m2)
            last = None
            for cc in range(NCC):
                last = eng.tensor_add(BC[:, cc * T + 1: cc * T + T],
                                      BC[:, cc * T + 1: cc * T + T],
                                      BA[:, cc * T: cc * T + T - 1])
                VE.inc(last)
            return (VE, VE.n)
        ctx.sec("vector", vxk2)
        ctx.barrier()

        def gpw2(eng):
            i = eng.dma_start(out=_sl(W.ap(), 0, [[8192, 128], [1024, 8], [1, 1024]]),
                              in_=_sl(t["WkkT"], 0, [[1024, 128], [128 * 1024, 8], [1, 1024]]))
            WA2.inc(i, 16)
            return (WA2, WA2.n)
        ctx.sec("gpsimd", gpw2)

        mm_k = MM.n
        ve_k = VE.n
        se_k = SE.n
        wa_k = WA2.n

        def tk(eng):
            eng.wait_ge(WA2.h, wa_k)
            last = None
            for j, (dc, tq) in enumerate([(a, b) for a in range(NCC) for b in range(4)]):
                if j >= 2:
                    eng.wait_ge(VE.h, ve_k + j - 1)
                for cc in range(NCC):
                    last = eng.matmul(psF[:, (j % 2) * 512:(j % 2) * 512 + 512],
                                      W[:, cc * 1024 + dc * 128: cc * 1024 + dc * 128 + 128],
                                      BC[:, cc * T + tq * 512: cc * T + (tq + 1) * 512],
                                      start=(cc == 0), stop=(cc == NCC - 1))
                MM.inc(last)
            return (MM, MM.n)
        ctx.sec("tensor", tk)

        def vk(eng):
            last = None
            for j, (dc, tq) in enumerate([(a, b) for a in range(NCC) for b in range(4)]):
                eng.wait_ge(MM.h, mm_k + j + 1)
                last = eng.tensor_scalar_max(
                    _sl(Rb, dc * T + tq * 512, [[32768, 128], [1, 512]]),
                    psF[:, (j % 2) * 512:(j % 2) * 512 + 512], 0.0)
                VE.inc(last)
            return (VE, VE.n)
        ctx.sec("vector", vk)

        ve_relu = VE.n

        def sk(eng):
            eng.wait_ge(VE.h, ve_relu)
            last = None
            for dc in range(NCC):
                last = eng.activation(_sl(Rb, dc * T, [[32768, 128], [1, T]]),
                                      _sl(Rb, dc * T, [[32768, 128], [1, T]]),
                                      AF.Square)
                SE.inc(last)
            return (SE, SE.n)
        ctx.sec("scalar", sk)
        ctx.barrier()

        def gpw3(eng):
            i = eng.dma_start(out=_sl(W.ap(), 0, [[8192, 128], [1024, 8], [1, 1024]]),
                              in_=_sl(t["WvvT"], 0, [[1024, 128], [128 * 1024, 8], [1, 1024]]))
            WA2.inc(i, 16)
            return (WA2, WA2.n)
        ctx.sec("gpsimd", gpw3)

        mm_v = MM.n
        ve_v = VE.n
        wa_v = WA2.n

        def tv(eng):
            eng.wait_ge(WA2.h, wa_v)
            last = None
            for j, (dc, tq) in enumerate([(a, b) for a in range(NCC) for b in range(4)]):
                if j >= 2:
                    eng.wait_ge(VE.h, ve_v + j - 1)
                for cc in range(NCC):
                    last = eng.matmul(psF[:, (j % 2) * 512:(j % 2) * 512 + 512],
                                      W[:, cc * 1024 + dc * 128: cc * 1024 + dc * 128 + 128],
                                      _sl(Rb, cc * T + tq * 512, [[32768, 128], [1, 512]]),
                                      start=(cc == 0), stop=(cc == NCC - 1))
                MM.inc(last)
            return (MM, MM.n)
        ctx.sec("tensor", tv)

        def vv(eng):
            last = None
            for j, (dc, tq) in enumerate([(a, b) for a in range(NCC) for b in range(4)]):
                eng.wait_ge(MM.h, mm_v + j + 1)
                last = eng.tensor_mul(
                    _sl(Rb, 16384 + dc * T + tq * 512, [[32768, 128], [1, 512]]),
                    psF[:, (j % 2) * 512:(j % 2) * 512 + 512],
                    BB[:, dc * T + tq * 512: dc * T + (tq + 1) * 512])
                VE.inc(last)
            return (VE, VE.n)
        ctx.sec("vector", vv)
        ctx.barrier()

        def gpacc(eng):
            i = eng.dma_start(
                out=_sl(t["xspill"], 0, [[T, 128], [128 * T, NCC], [1, T]]),
                in_=_sl(Rb, 16384, [[32768, 128], [T, NCC], [1, T]]),
                accum_op=OP.add)
            GD.inc(i, 16)
            return (GD, GD.n)
        ctx.sec("gpsimd", gpacc)
    ctx.barrier()

    if meta.get("p5_stage", 99) < 3:
        return
    emit_p5_attn(ctx, meta)


def emit_p5_attn(ctx, meta):
    nc, t, S = ctx.nc, ctx.t, ctx.s
    LD, GD, MM, VE, SE = (ctx.sm[k] for k in ("LD", "GD", "MM", "VE", "SE"))
    WA2, WB2 = ctx.sm["WA2"], ctx.sm["WB2"]
    R, BA, BB, BC, W, RW, CN, CB, SM = (
        S[k] for k in ("R", "BA", "BB", "BC", "W", "RW", "CN", "CB", "SM"))
    Rb = R.ap().bitcast(BF16)
    onesF32 = CN[32:33, 140:268]
    V0 = 1040
    STG = 192
    TRI = 2240
    TRIH = 2368
    ONER = 2880
    PREF = 3392
    HFC = 3522
    RZC = 16640

    with nc.psum_tensor("psA5", [128, 2048], F32) as psF:
        mm0 = MM.n
        se0 = SE.n
        ve0 = VE.n
        wa0 = WA2.n

        def gpw(eng):
            for wi, wn in enumerate(["WqT", "WkT", "WvT"]):
                if wi >= 1:
                    eng.wait_ge(MM.h, mm0 + wi * 32)
                i = eng.dma_start(
                    out=_sl(W.ap(), 0, [[8192, 128], [1024, 8], [1, 1024]]),
                    in_=_sl(t[wn], 0, [[1024, 128], [128 * 1024, 8], [1, 1024]]))
                WA2.inc(i, 16)
            return (WA2, WA2.n)
        ctx.sec("gpsimd", gpw)

        def tqk(eng):
            last = None
            for wi in range(2):
                eng.wait_ge(WA2.h, wa0 + 16 * (wi + 1))
                for j, (dc, tq) in enumerate([(a_, b_) for a_ in range(NCC) for b_ in range(4)]):
                    gj = wi * 32 + j
                    if gj >= 2:
                        eng.wait_ge(SE.h, se0 + gj - 1)
                    for cc in range(NCC):
                        last = eng.matmul(psF[:, (j % 2) * 512:(j % 2) * 512 + 512],
                                          W[:, cc * 1024 + dc * 128: cc * 1024 + dc * 128 + 128],
                                          BA[:, cc * T + tq * 512: cc * T + (tq + 1) * 512],
                                          start=(cc == 0), stop=(cc == NCC - 1))
                    MM.inc(last)
            eng.wait_ge(WA2.h, wa0 + 48)
            for kt in range(16):
                if kt >= 2:
                    eng.wait_ge(SE.h, se0 + 64 + kt - 1)
                else:
                    eng.wait_ge(SE.h, se0 + 64)
                for cc in range(NCC):
                    for hv in range(2):
                        last = eng.matmul(
                            psF[:, (kt % 2) * 1024 + hv * 512:(kt % 2) * 1024 + (hv + 1) * 512],
                            BA[:, cc * T + kt * 128: cc * T + kt * 128 + 128],
                            W[:, cc * 1024 + hv * 512: cc * 1024 + (hv + 1) * 512],
                            start=(cc == 0), stop=(cc == NCC - 1))
                MM.inc(last)
            return (MM, MM.n)
        ctx.sec("tensor", tqk)

        def sqk(eng):
            last = None
            for wi in range(2):
                for j, (dc, tq) in enumerate([(a_, b_) for a_ in range(NCC) for b_ in range(4)]):
                    gj = wi * 32 + j
                    eng.wait_ge(MM.h, mm0 + gj + 1)
                    srcp = psF[:, (j % 2) * 512:(j % 2) * 512 + 512]
                    if wi == 0:
                        last = eng.activation(
                            BC[:, dc * T + tq * 512: dc * T + (tq + 1) * 512],
                            srcp, AF.Copy, scale=crow(S, 7, 0, 128))
                    else:
                        last = eng.activation(
                            _sl(Rb, dc * T + tq * 512, [[32768, 128], [1, 512]]),
                            srcp, AF.Copy)
                    SE.inc(last)
            for kt in range(16):
                eng.wait_ge(MM.h, mm0 + 64 + kt + 1)
                last = eng.activation(
                    _sl(BB.ap(), kt * V0, [[17664, 128], [65, 16], [1, 64]]),
                    _sl(psF.ap(), (kt % 2) * 1024, [[2048, 128], [64, 16], [1, 64]]),
                    AF.Copy)
                SE.inc(last)
            return (SE, SE.n)
        ctx.sec("scalar", sqk)

        def vpre(eng):
            last = None
            for kt in range(16):
                last = eng.memset(_sl(BB.ap(), kt * V0 + 64, [[17664, 128], [65, 16], [1, 1]]), 1.0)
            eng.memset(CB[0:1, ONER:ONER + 512], 1.0)
            last = eng.memset(CB[:, HFC:HFC + 1], 0.5)
            VE.inc(last)
            return (VE, VE.n)
        ctx.sec("vector", vpre)

        def spre(eng):
            i = eng.activation(CB[:, TRIH:TRIH + 128], CB[:, TRI:TRI + 128],
                               AF.Copy, scale=crow(S, 4, 0, 128))
            SE.inc(i)
            return (SE, SE.n)
        ctx.sec("scalar", spre)

        def vpre2(eng):
            i = eng.memset(CB[:, TRIH + 128:TRIH + 512], 0.5)
            VE.inc(i)
            return (VE, VE.n)
        ctx.sec("vector", vpre2)
        ctx.barrier()
    if meta.get("p5_stage", 99) < 4:
        return

    def build_sched():
        ev = []
        mm = se = ve = 0
        ktg = 0
        st_se_hist = {}
        av_mm_hist = {}
        for hp in range(8):
            for qc in range(4):
                e = dict(hp=hp, qc=qc, kts=[])
                if qc >= 1:
                    mm += 1; e["mm_psP"] = mm
                    ve += 1; e["ve_pref"] = ve
                for kt in range(4 * qc + 4):
                    w0 = max(0, kt * 128 - qc * 512)
                    d = kt >= 4 * qc
                    mm += 1; st_mm = mm
                    se += 1; st_se = se
                    tri_ve = None
                    if d:
                        ve += 1; tri_ve = ve
                    mm += 1; av_mm = mm
                    e["kts"].append(dict(kt=kt, ktg=ktg, w0=w0, d=d, st_mm=st_mm,
                                         st_se=st_se, tri_ve=tri_ve, av_mm=av_mm))
                    st_se_hist[ktg] = st_se
                    av_mm_hist[ktg] = av_mm
                    ktg += 1
                ve += 2; e["ve_rz"] = ve
                mm += 1; e["mm_rz"] = mm
                ve += 1; e["ve_rzc"] = ve
                ve += 1; e["ve_on"] = ve
                ev.append(e)
        return ev, st_se_hist, av_mm_hist

    sched, st_se_hist, av_mm_hist = build_sched()
    with nc.psum_tensor("psATT", [128, 3072], F32) as psA, \
         nc.psum_tensor("psO5", [128, 1024], F32) as psO:
        mmA = MM.n
        seA = SE.n
        veA = VE.n

        def slotS(ktg, hh):
            return (ktg % 2) * 1024 + hh * 512

        def stg(ktg, hh):
            return STG + ((ktg % 2) * 2 + hh) * 512

        def tATT(eng):
            last = None
            prev_pref_ve = None
            prev_on_ve = None
            prev_rzc_ve = None
            for e in sched:
                hp, qc = e["hp"], e["qc"]
                if qc >= 1:
                    if prev_pref_ve is not None:
                        eng.wait_ge(VE.h, veA + prev_pref_ve)
                    if prev_rzc_ve is not None:
                        eng.wait_ge(VE.h, veA + prev_rzc_ve)
                    for hh in range(2):
                        for kt in range(4 * qc):
                            last = eng.matmul(
                                psA[0:1, 2048 + hh * 512:2048 + hh * 512 + 65],
                                CB[:, HFC:HFC + 1],
                                _sl(BB.ap(), kt * V0 + (2 * hp + hh) * 65,
                                    [[17664, 128], [1, 65]]),
                                start=(kt == 0), stop=(kt == 4 * qc - 1))
                    MM.inc(last)
                    prev_pref_ve = e["ve_pref"]
                first = [True, True]
                if qc >= 1:
                    eng.wait_ge(VE.h, veA + e["ve_pref"])
                    if prev_on_ve is not None:
                        eng.wait_ge(VE.h, veA + prev_on_ve)
                    for hh in range(2):
                        last = eng.matmul(
                            psO[0:65, hh * 512 + 0:hh * 512 + 512],
                            CB[0:1, PREF + hh * 65:PREF + hh * 65 + 65],
                            CB[0:1, ONER:ONER + 512],
                            start=True, stop=False)
                        first[hh] = False
                elif prev_on_ve is not None:
                    eng.wait_ge(VE.h, veA + prev_on_ve)
                nkt = len(e["kts"])
                for ek in e["kts"]:
                    kt, ktg, w0, d = ek["kt"], ek["ktg"], ek["w0"], ek["d"]
                    ww = 512 - w0
                    if ktg >= 2:
                        eng.wait_ge(SE.h, seA + st_se_hist[ktg - 2])
                    for hh in range(2):
                        last = eng.matmul(
                            psA[:, slotS(ktg, hh) + w0:slotS(ktg, hh) + 512],
                            _sl(Rb, hp * T + kt * 128 + (hh * 64) * 32768,
                                [[32768, 64], [1, 128]]),
                            _sl(BC.ap(), hp * T + qc * 512 + w0 + (hh * 64) * 16384,
                                [[16384, 64], [1, ww]]),
                            tile_position=(hh * 64, 0),
                            start=True, stop=True)
                    MM.inc(last)
                    eng.wait_ge(SE.h, seA + ek["st_se"])
                    if d:
                        eng.wait_ge(VE.h, veA + ek["tri_ve"])
                    laststop = (kt == nkt - 1)
                    for hh in range(2):
                        vsl = _sl(BB.ap(), kt * V0 + (2 * hp + hh) * 65,
                                  [[17664, 128], [1, 65]])
                        last = eng.matmul(
                            psO[0:65, hh * 512 + w0:hh * 512 + 512],
                            vsl, CB[:, stg(ktg, hh) + w0:stg(ktg, hh) + 512],
                            start=first[hh], stop=laststop and not d)
                        first[hh] = False
                        if d:
                            last = eng.matmul(
                                psO[0:65, hh * 512 + w0:hh * 512 + 512],
                                vsl, CB[:, TRIH:TRIH + (512 - w0)],
                                start=False, stop=laststop)
                    MM.inc(last)
                eng.wait_ge(VE.h, veA + e["ve_rz"])
                for hv in range(2):
                    last = eng.matmul(psA[0:64, 2048 + hv * 512:2048 + (hv + 1) * 512],
                                      onesF32[0:1, 0:64],
                                      RW[32:33, 512 + hv * 512:512 + (hv + 1) * 512],
                                      start=True, stop=True)
                MM.inc(last)
                prev_rzc_ve = e["ve_rzc"]
            return (MM, MM.n)
        ctx.sec("tensor", tATT)

        def sATT(eng):
            last = None
            for e in sched:
                for ek in e["kts"]:
                    ktg, w0 = ek["ktg"], ek["w0"]
                    eng.wait_ge(MM.h, mmA + ek["st_mm"])
                    if ktg >= 2:
                        eng.wait_ge(MM.h, mmA + av_mm_hist[ktg - 2])
                    for hh in range(2):
                        last = eng.activation(
                            CB[:, stg(ktg, hh) + w0:stg(ktg, hh) + 512],
                            psA[:, slotS(ktg, hh) + w0:slotS(ktg, hh) + 512],
                            AF.Square, scale=crow(S, 5, 0, 128),
                            bias=crow(S, 5, 0, 128))
                    SE.inc(last)
            return (SE, SE.n)
        ctx.sec("scalar", sATT)

        def vATT(eng):
            last = None
            for e in sched:
                hp, qc = e["hp"], e["qc"]
                if qc >= 1:
                    eng.wait_ge(MM.h, mmA + e["mm_psP"])
                    eng.tensor_copy(out=CB[0:1, PREF:PREF + 65],
                                    in_=psA[0:1, 2048:2048 + 65])
                    last = eng.tensor_copy(out=CB[0:1, PREF + 65:PREF + 130],
                                           in_=psA[0:1, 2048 + 512:2048 + 512 + 65])
                    VE.inc(last)
                for ek in e["kts"]:
                    if ek["d"]:
                        ktg, w0 = ek["ktg"], ek["w0"]
                        eng.wait_ge(SE.h, seA + ek["st_se"])
                        for hh in range(2):
                            sl_ = CB[:, stg(ktg, hh) + w0:stg(ktg, hh) + w0 + 128]
                            last = eng.tensor_mul(sl_, sl_, CB[:, TRI:TRI + 128])
                        VE.inc(last)
                eng.wait_ge(MM.h, mmA + e["kts"][-1]["av_mm"])
                i = eng.tensor_scalar(RW[32:33, 512:1536], psO[64:65, 0:1024],
                                      ATT_EPS, None, OP.add)
                ctx.chain(eng, VE, i)
                i = eng.reciprocal(RW[32:33, 512:1536], RW[32:33, 512:1536])
                VE.inc(i)
                eng.wait_ge(MM.h, mmA + e["mm_rz"])
                i = eng.tensor_copy(out=BB[0:64, RZC:RZC + 1024],
                                    in_=psA[0:64, 2048:3072])
                ctx.chain(eng, VE, i)
                for hh in range(2):
                    last = eng.tensor_mul(
                        _sl(BA.ap(), hp * T + qc * 512 + (hh * 64) * 16384,
                            [[16384, 64], [1, 512]]),
                        psO[0:64, hh * 512:hh * 512 + 512],
                        BB[0:64, RZC + hh * 512:RZC + hh * 512 + 512])
                VE.inc(last)
            return (VE, VE.n)
        ctx.sec("vector", vATT)
        ctx.barrier()
    if meta.get("p5_stage", 99) < 5:
        return

    with nc.psum_tensor("psW5", [128, 1024], F32) as psF:
        def gpwo(eng):
            i = eng.dma_start(
                out=_sl(W.ap(), 0, [[8192, 128], [1024, 8], [1, 1024]]),
                in_=_sl(t["WoT"], 0, [[1024, 128], [128 * 1024, 8], [1, 1024]]))
            WA2.inc(i, 16)
            return (WA2, WA2.n)
        ctx.sec("gpsimd", gpwo)

        mmW = MM.n
        seW = SE.n
        waW = WA2.n

        def two(eng):
            eng.wait_ge(WA2.h, waW)
            last = None
            for j, (dc, tq) in enumerate([(a_, b_) for a_ in range(NCC) for b_ in range(4)]):
                if j >= 2:
                    eng.wait_ge(SE.h, seW + j - 1)
                for cc in range(NCC):
                    last = eng.matmul(psF[:, (j % 2) * 512:(j % 2) * 512 + 512],
                                      W[:, cc * 1024 + dc * 128: cc * 1024 + dc * 128 + 128],
                                      BA[:, cc * T + tq * 512: cc * T + (tq + 1) * 512],
                                      start=(cc == 0), stop=(cc == NCC - 1))
                MM.inc(last)
            return (MM, MM.n)
        ctx.sec("tensor", two)

        def swo(eng):
            last = None
            for j, (dc, tq) in enumerate([(a_, b_) for a_ in range(NCC) for b_ in range(4)]):
                eng.wait_ge(MM.h, mmW + j + 1)
                last = eng.activation(BC[:, dc * T + tq * 512: dc * T + (tq + 1) * 512],
                                      psF[:, (j % 2) * 512:(j % 2) * 512 + 512], AF.Copy)
                SE.inc(last)
            return (SE, SE.n)
        ctx.sec("scalar", swo)
        ctx.barrier()

    def syR(eng):
        i = eng.dma_start(out=_sl(R.ap(), 0, [[16384, 128], [T, NCC], [1, T]]),
                          in_=_sl(t["xspill"], 0, [[T, 128], [128 * T, NCC], [1, T]]))
        LD.inc(i, 16)
        return (LD, LD.n)
    ctx.sec("sync", syR)
    ctx.barrier()

    def vfin(eng):
        last = None
        for cc in range(NCC):
            last = eng.tensor_add(R[:, cc * T:(cc + 1) * T], R[:, cc * T:(cc + 1) * T],
                                  BC[:, cc * T:(cc + 1) * T])
            VE.inc(last)
        return (VE, VE.n)
    ctx.sec("vector", vfin)
    ctx.barrier()


def emit_p2(ctx, meta):
    nc, t, S = ctx.nc, ctx.t, ctx.s
    LD, GD, MM, VE, SE, TZ = (ctx.sm[k] for k in ("LD", "GD", "MM", "VE", "SE", "TZ"))
    R, BA, BB, BC, W, RW, SC, CN, CB, SM = (
        S[k] for k in ("R", "BA", "BB", "BC", "W", "RW", "SC", "CN", "CB", "SM"))
    ones_b = CB[:, 128:129]
    onesF = CN[0:1, 140:268]
    onesF32 = CN[32:33, 140:268]
    onesF64 = CN[64:65, 140:268]

    Rv = _sl(R.ap(), 0, [[16384, 128], [T, NCC], [1, T]])
    BAv = _sl(BA.ap(), 0, [[16384, 128], [T, NCC], [1, T]])

    def brc(ps, w):
        return _sl(ps.ap(), 0, [[2048, 128], [0, NCC], [1, w]])

    def sy0(eng):
        i = eng.dma_start(out=_sl(R.ap(), 0, [[16384, 128], [T, NCC], [1, T]]),
                          in_=_sl(t["xT"], 0, [[T, 128], [128 * T, NCC], [1, T]]))
        LD.inc(i, 16)
        return (LD, LD.n)
    ctx.sec("sync", sy0)

    def gp0(eng):
        i = eng.dma_start(out=BC[0:64, 0:1024], in_=t["mixT"][:, :])
        GD.inc(i, 16)
        i = eng.dma_start(out=BC[64:65, 0:1024], in_=t["prow"][:, 0:C])
        GD.inc(i, 16)
        return (GD, GD.n)
    ctx.sec("gpsimd", gp0)

    def syw1(eng):
        i = eng.dma_start(out=RW[64:65, 2112:2112 + C], in_=t["prow"][:, 2 * C:3 * C])
        LD.inc(i, 16)
        return (LD, LD.n)
    ctx.sec("sync", syw1)
    ctx.barrier()

    with nc.psum_tensor("psQ", [1, 512], F32) as psQ, \
         nc.psum_tensor("psREP", [128, 2048], F32) as psREP, \
         nc.psum_tensor("psTR", [128, 2048], BF16) as psTR, \
         nc.psum_tensor("psSML", [128, 256], F32) as psSML:

        def s1(eng):
            i = eng.copy(out=BA[:, :], in_=R[:, :])
            SE.inc(i)
            return (SE, SE.n)
        ctx.sec("scalar", s1)

        se_cast = SE.n
        mm_q_base = MM.n
        se_q_base = SE.n

        def stat_pass(src):
            pass

        def t1(eng):
            eng.wait_ge(SE.h, se_cast)
            last = None
            for q in range(4):
                if q >= 1:
                    eng.wait_ge(SE.h, se_q_base + q)
                for cc in range(NCC):
                    last = eng.matmul(psQ[0:1, :], ones_b,
                                      BA[:, cc * T + q * 512: cc * T + (q + 1) * 512],
                                      start=(cc == 0), stop=(cc == NCC - 1))
                MM.inc(last)
            return (MM, MM.n)
        ctx.sec("tensor", t1)

        def s2(eng):
            last = None
            for q in range(4):
                eng.wait_ge(MM.h, mm_q_base + q + 1)
                last = eng.activation(RW[32:33, q * 512:(q + 1) * 512], psQ[0:1, :],
                                      AF.Copy, scale=crow(S, 2, 32))
                SE.inc(last)
            return (SE, SE.n)
        ctx.sec("scalar", s2)
        if meta.get("p5_ln", 9) < 3:
            return

        se_mu = SE.n

        def t2(eng):
            eng.wait_ge(SE.h, se_mu)
            last = None
            for q in range(4):
                last = eng.matmul(psREP[:, q * 512:(q + 1) * 512], onesF32,
                                  RW[32:33, q * 512:(q + 1) * 512], start=True, stop=True)
            MM.inc(last)
            return (MM, MM.n)
        ctx.sec("tensor", t2)

        mm_murep = MM.n

        def v1(eng):
            eng.wait_ge(MM.h, mm_murep)
            i = eng.tensor_sub(Rv, Rv, brc(psREP, 2048))
            VE.inc(i)
            return (VE, VE.n)
        ctx.sec("vector", v1)

        ve_center = VE.n

        def s3(eng):
            eng.wait_ge(VE.h, ve_center)
            i = eng.activation(BA[:, :], R[:, :], AF.Square)
            SE.inc(i)
            return (SE, SE.n)
        ctx.sec("scalar", s3)

        se_sq = SE.n
        mm_q2_base = MM.n
        se_q2_base = SE.n

        def t3(eng):
            eng.wait_ge(SE.h, se_sq)
            last = None
            for q in range(4):
                if q >= 1:
                    eng.wait_ge(SE.h, se_q2_base + q)
                for cc in range(NCC):
                    last = eng.matmul(psQ[0:1, :], ones_b,
                                      BA[:, cc * T + q * 512: cc * T + (q + 1) * 512],
                                      start=(cc == 0), stop=(cc == NCC - 1))
                MM.inc(last)
            return (MM, MM.n)
        ctx.sec("tensor", t3)

        def s4(eng):
            last = None
            for q in range(4):
                eng.wait_ge(MM.h, mm_q2_base + q + 1)
                last = eng.activation(RW[96:97, q * 512:(q + 1) * 512], psQ[0:1, :],
                                      AF.Copy, scale=crow(S, 2, 96))
                SE.inc(last)
            eng.wait_ge(SE.h, SE.n)
            last = eng.activation(RW[64:65, 0:2048], RW[96:97, 0:2048], AF.Sqrt,
                                  bias=crow(S, 3, 64))
            SE.inc(last)
            return (SE, SE.n)
        ctx.sec("scalar", s4)
        if meta.get("p5_ln", 9) < 5:
            return

        se_s = SE.n

        def v2(eng):
            eng.wait_ge(SE.h, se_s)
            i = eng.reciprocal(RW[64:65, 0:2048], RW[64:65, 0:2048])
            VE.inc(i)
            return (VE, VE.n)
        ctx.sec("vector", v2)

        ve_rs = VE.n

        def s5(eng):
            eng.wait_ge(VE.h, ve_rs)
            i = eng.activation(RW[32:33, 0:2048], RW[64:65, 0:2048], AF.Square)
            SE.inc(i)
            return (SE, SE.n)
        ctx.sec("scalar", s5)

        se_rssq = SE.n

        def v3(eng):
            eng.wait_ge(SE.h, se_rssq)
            i = eng.tensor_scalar(RW[32:33, 0:2048], RW[32:33, 0:2048],
                                  -LN_EPS, 1.0, OP.mult, OP.add)
            VE.inc(i)
            return (VE, VE.n)
        ctx.sec("vector", v3)

        ve_varv = VE.n

        def s6(eng):
            eng.wait_ge(VE.h, ve_varv)
            i = eng.activation(RW[32:33, 0:2048], RW[32:33, 0:2048], AF.Sqrt,
                               bias=crow(S, 3, 32))
            SE.inc(i)
            return (SE, SE.n)
        ctx.sec("scalar", s6)

        se_s2 = SE.n

        def v4(eng):
            eng.wait_ge(SE.h, se_s2)
            i = eng.reciprocal(RW[32:33, 0:2048], RW[32:33, 0:2048])
            VE.inc(i)
            return (VE, VE.n)
        ctx.sec("vector", v4)

        ve_sv = VE.n

        def t4(eng):
            eng.wait_ge(VE.h, ve_sv)
            last = None
            for q in range(4):
                last = eng.matmul(psREP[:, q * 512:(q + 1) * 512], onesF64,
                                  RW[64:65, q * 512:(q + 1) * 512], start=True, stop=True)
            MM.inc(last)
            return (MM, MM.n)
        ctx.sec("tensor", t4)

        mm_rsrep = MM.n

        def v5a(eng):
            eng.wait_ge(MM.h, mm_rsrep)
            i = eng.tensor_mul(BAv, Rv, brc(psREP, 2048))
            VE.inc(i)
            return (VE, VE.n)
        ctx.sec("vector", v5a)

        ve_v1 = VE.n

        def t4b(eng):
            eng.wait_ge(VE.h, ve_v1)
            last = None
            for q in range(4):
                last = eng.matmul(psREP[:, q * 512:(q + 1) * 512], onesF32,
                                  RW[32:33, q * 512:(q + 1) * 512], start=True, stop=True)
            MM.inc(last)
            return (MM, MM.n)
        ctx.sec("tensor", t4b)

        mm_rs2rep = MM.n

        def v5(eng):
            eng.wait_ge(MM.h, mm_rs2rep)
            i = eng.tensor_mul(BAv, BAv, brc(psREP, 2048))
            VE.inc(i)
            return (VE, VE.n)
        ctx.sec("vector", v5)
        if meta.get("p2_stage", 99) < 3:
            return

        ve_v = VE.n

        jobs = [(i_, cc) for i_ in range(NTT) for cc in range(NCC)]
        mm_tr = MM.n
        se_tr = SE.n

        def t5(eng):
            eng.wait_ge(VE.h, ve_v)
            last = None
            for j, (i_, cc) in enumerate(jobs):
                if j >= 2:
                    eng.wait_ge(SE.h, se_tr + j - 1)
                last = eng.transpose(psTR[:, (j % 2) * 1024:(j % 2) * 1024 + 128],
                                     BA[:, cc * T + i_ * 128: cc * T + i_ * 128 + 128],
                                     CB[:, 0:128])
                MM.inc(last)
            return (MM, MM.n)
        ctx.sec("tensor", t5)

        def s7(eng):
            last = None
            for j, (i_, cc) in enumerate(jobs):
                eng.wait_ge(MM.h, mm_tr + j + 1)
                last = eng.copy(
                    out=BB[:, i_ * 1024 + cc * 128: i_ * 1024 + cc * 128 + 128],
                    in_=psTR[:, (j % 2) * 1024:(j % 2) * 1024 + 128])
                SE.inc(last)
            return (SE, SE.n)
        ctx.sec("scalar", s7)

        ctx.barrier()
        if meta.get("p2_stage", 99) < 4:
            return

        def s8(eng):
            i = eng.activation(RW[32:33, 2112:2112 + N + 1],
                               RW[32:33, 2112:2112 + N + 1], AF.Exp)
            ctx.chain(eng, SE, i)
            i = eng.activation(RW[32:33, 2112:2112 + N + 1],
                               RW[32:33, 2112:2112 + N + 1], AF.Ln,
                               bias=crow(S, 0, 32))
            SE.inc(i)
            return (SE, SE.n)
        ctx.sec("scalar", s8)

        se_tau = SE.n

        def t6(eng):
            eng.wait_ge(SE.h, se_tau)
            i = eng.matmul(psSML[0:1, 0:N], RW[32:33, 2112 + N:2112 + N + 1],
                           RW[32:33, 2112:2112 + N],
                           start=True, stop=True)
            ctx.chain(eng, MM, i)
            i = eng.matmul(psSML[0:64, 128:129], RW[32:33, 2112:2112 + N],
                           RW[32:33, 2112 + N:2112 + N + 1],
                           start=True, stop=True)
            MM.inc(i)
            return (MM, MM.n)
        ctx.sec("tensor", t6)

        mm_lam = MM.n

        def v6(eng):
            eng.wait_ge(MM.h, mm_lam)
            eng.tensor_copy(out=RW[32:33, 2048:2048 + N], in_=psSML[0:1, 0:N])
            i = eng.tensor_copy(out=CN[0:64, 130:131], in_=psSML[0:64, 128:129])
            ctx.chain(eng, VE, i)
            eng.tensor_scalar_mul(CN[0:64, 131:132], CN[0:64, 130:131], -1.0)
            eng.tensor_scalar_mul(CN[0:64, 132:133], CN[0:64, 130:131], -128.0)
            eng.tensor_scalar_mul(CN[0:64, 138:139], CN[0:64, 130:131], -127.0)
            i = eng.tensor_scalar(CN[:, 137:138], CN[:, 128:129], -127.0, None, OP.add)
            i = eng.tensor_scalar(RW[32:33, 0:2048], RW[0:1, 0:2048], 1.0, None, OP.add)
            VE.inc(i)
            return (VE, VE.n)
        ctx.sec("vector", v6)

        ve_lam = VE.n
        mm_er_base = MM.n

        def t7(eng):
            eng.wait_ge(VE.h, ve_lam)
            eng.matmul(psSML[0:64, 0:128], onesF[0:1, 0:64], RW[0:1, 0:128],
                       start=True, stop=True)
            i = eng.matmul(psREP[0:128, 0:64], onesF32, RW[32:33, 2048:2048 + N],
                           start=True, stop=True)
            for h in range(2):
                i = eng.matmul(psREP[0:64, 512 + h * 512:512 + (h + 1) * 512],
                               onesF64[0:1, 0:64],
                               RW[64:65, 2112 + h * 512:2112 + (h + 1) * 512],
                               start=True, stop=True)
            MM.inc(i)
            return (MM, MM.n)
        ctx.sec("tensor", t7)

        mm_emat = MM.n

        def s9(eng):
            eng.wait_ge(MM.h, mm_emat)
            eng.activation(CN[0:64, 133:134], CN[0:64, 131:132], AF.Exp)
            eng.activation(CN[0:64, 134:135], CN[0:64, 132:133], AF.Exp)
            eng.activation(CB[0:64, 192:320], psSML[0:64, 0:128], AF.Exp,
                           scale=CN[0:64, 130:131], bias=CN[0:64, 138:139])
            eng.activation(CB[0:64, 320:448], psSML[0:64, 0:128], AF.Exp,
                           scale=CN[0:64, 130:131], bias=CN[0:64, 132:133])
            i = eng.activation(CB[:, 448:512], psREP[0:128, 0:64], AF.Exp,
                               scale=CN[:, 137:138])
            SE.inc(i)
            return (SE, SE.n)
        ctx.sec("scalar", s9)
        if meta.get("p2_stage", 99) < 5:
            return

        se_emat = SE.n

        def v7(eng):
            eng.wait_ge(SE.h, se_emat)
            eng.tensor_mul(BC[0:64, 5120:6144], BC[0:64, 0:1024],
                           psREP[0:64, 512:512 + 1024])
            i = eng.tensor_scalar(CN[0:64, 135:136], CN[0:64, 133:134], -1.0, 1.0,
                                  OP.mult, OP.add)
            ctx.chain(eng, VE, i)
            i = eng.reciprocal(CN[0:64, 135:136], CN[0:64, 135:136])
            ctx.chain(eng, VE, i)
            i = eng.tensor_scalar_mul(CN[0:64, 136:137], CN[0:64, 135:136], -1.0)
            VE.inc(i)
            return (VE, VE.n)
        ctx.sec("vector", v7)

        ve_r1a = VE.n

        def t8(eng):
            eng.wait_ge(VE.h, ve_r1a)
            last = None
            for q in range(4):
                last = eng.matmul(psREP[0:64, q * 512:(q + 1) * 512], onesF32[0:1, 0:64],
                                  RW[32:33, q * 512:(q + 1) * 512], start=True, stop=True)
            MM.inc(last)
            return (MM, MM.n)
        ctx.sec("tensor", t8)

        mm_grep = MM.n

        def s10(eng):
            eng.wait_ge(MM.h, mm_grep)
            i = eng.activation(BC[0:64, 3072:5120], psREP[0:64, 0:2048], AF.Exp,
                               scale=CN[0:64, 131:132])
            SE.inc(i)
            return (SE, SE.n)
        ctx.sec("scalar", s10)

        se_w = SE.n

        def v8(eng):
            eng.wait_ge(SE.h, se_w)
            i = eng.tensor_scalar(BC[0:64, 1024:3072], BC[0:64, 3072:5120],
                                  CN[0:64, 136:137], CN[0:64, 135:136],
                                  OP.mult, OP.add)
            ctx.chain(eng, VE, i)
            i = eng.memset(BC[64:65, 1024:3072], 1.0)
            for cc in range(NCC):
                i = eng.tensor_mul(SM[:, 16 * cc + 8:16 * cc + 9],
                                   SM[:, 16 * cc + 2:16 * cc + 3],
                                   SM[:, 16 * cc + 0:16 * cc + 1])
            VE.inc(i)
            return (VE, VE.n)
        ctx.sec("vector", v8)
        if meta.get("p2_stage", 99) < 6:
            return

        ve_G = VE.n

        mm_kp = MM.n
        se_kp = SE.n
        ve_kp = VE.n
        ld_kp = LD.n

        def vz0(eng):
            i = eng.memset(CB[:, 512 + 128:512 + 256] if False else CB[:, 1536:1664], 0.0)
            VE.inc(i)
            return (VE, VE.n)
        ctx.sec("vector", vz0)

        ve_z0 = VE.n

        def t9(eng):
            eng.wait_ge(VE.h, ve_G)
            last = None
            for cc in range(NCC):
                if cc >= 1:
                    eng.wait_ge(SE.h, se_kp + cc)
                last = eng.matmul(psSML[:, (cc % 2) * 128:(cc % 2) * 128 + 128],
                                  BC[0:64, cc * 128:(cc + 1) * 128],
                                  CB[0:64, 192:320], start=True, stop=True)
                MM.inc(last)
            return (MM, MM.n)
        ctx.sec("tensor", t9)

        def s11(eng):
            last = None
            for cc in range(NCC):
                eng.wait_ge(MM.h, mm_kp + cc + 1)
                if cc >= 1:
                    eng.wait_ge(VE.h, ve_z0 + cc)
                    eng.wait_ge(LD.h, ld_kp + 16 * cc)
                last = eng.activation(CB[:, 512:640],
                                      psSML[:, (cc % 2) * 128:(cc % 2) * 128 + 128],
                                      AF.Copy, scale=SM[:, 16 * cc:16 * cc + 1])
                SE.inc(last)
            return (SE, SE.n)
        ctx.sec("scalar", s11)

        def v9(eng):
            last = None
            for cc in range(NCC):
                eng.wait_ge(SE.h, se_kp + cc + 1)
                last = eng.tensor_tensor(CB[:, 639:640], CB[:, 639:640],
                                         SM[:, 16 * cc + 8:16 * cc + 9], OP.add)
                VE.inc(last)
            return (VE, VE.n)
        ctx.sec("vector", v9)

        def sy1(eng):
            last = None
            for cc in range(NCC):
                eng.wait_ge(VE.h, ve_z0 + cc + 1)
                last = eng.dma_start(out=t["kpad"][cc * 128:(cc + 1) * 128, 0:128],
                                     in_=CB[:, 512:640])
                LD.inc(last, 16)
            eng.wait_ge(VE.h, ve_z0)
            last = eng.dma_start(
                out=_sl(t["kpad"], 128, [[256, 128], [128 * 256, 8], [1, 128]]),
                in_=_sl(CB.ap(), 1536, [[3652, 128], [0, 8], [1, 128]]))
            LD.inc(last, 16)
            return (LD, LD.n)
        ctx.sec("sync", sy1)

        ctx.barrier()

    if meta.get("p2_stage", 99) < 7:
        return
    emit_p2b(ctx, meta)


def emit_p2b(ctx, meta):
    nc, t, S = ctx.nc, ctx.t, ctx.s
    LD, GD, MM, VE, SE, TZ = (ctx.sm[k] for k in ("LD", "GD", "MM", "VE", "SE", "TZ"))
    R, BA, BB, BC, W, RW, SC, CN, CB, SM = (
        S[k] for k in ("R", "BA", "BB", "BC", "W", "RW", "SC", "CN", "CB", "SM"))
    onesF = CN[0:1, 140:268]
    onesF64 = CN[64:65, 140:268]

    with nc.psum_tensor("psSL", [64, 2048], F32) as psSL, \
         nc.psum_tensor("psEX", [128, 2048], F32) as psEX:

        def vz(eng):
            eng.memset(SC[:, :], 0.0)
            i = eng.memset(BA[:, :], 0.0)
            VE.inc(i)
            return (VE, VE.n)
        ctx.sec("vector", vz)

        ve0 = VE.n
        mm0 = MM.n

        def t10(eng):
            eng.wait_ge(VE.h, ve0)
            last = None
            for i_ in range(NCH):
                if i_ >= 2:
                    tgt = 2 if i_ - 2 == 0 else 3 * (i_ - 2) + 1
                    eng.wait_ge(VE.h, ve0 + tgt)
                for h in range(2):
                    last = eng.matmul(
                        psSL[:, (i_ % 2) * 1024 + h * 512:(i_ % 2) * 1024 + (h + 1) * 512],
                        CB[:, 448:512],
                        BB[:, i_ * 1024 + h * 512: i_ * 1024 + (h + 1) * 512],
                        start=True, stop=True)
                MM.inc(last)
                if i_ >= 1:
                    eng.wait_ge(VE.h, ve0 + 3 * i_)
                    if i_ >= 3:
                        eng.wait_ge(VE.h, ve0 + 3 * (i_ - 2) + 2)
                    for h in range(2):
                        last = eng.matmul(
                            psEX[:, (i_ % 2) * 1024 + h * 512:(i_ % 2) * 1024 + (h + 1) * 512],
                            CB[0:64, 320:448],
                            BC[0:64, 3072 + (i_ % 2) * 1024 + h * 512:
                               3072 + (i_ % 2) * 1024 + (h + 1) * 512],
                            start=True, stop=True)
                    MM.inc(last)
            return (MM, MM.n)
        ctx.sec("tensor", t10)

        def v10(eng):
            last = None
            for i_ in range(NCH):
                if i_ == 0:
                    i = eng.engine_nop()
                    VE.inc(i)
                    eng.wait_ge(MM.h, mm0 + 1)
                    last = eng.scalar_tensor_tensor(
                        SC[:, :], SC[:, :], CN[0:64, 134:135],
                        psSL[:, 0:1024], OP.mult, OP.add)
                    VE.inc(last)
                else:
                    eng.wait_ge(VE.h, ve0 + (2 if i_ == 1 else 3 * (i_ - 1) + 1))
                    if i_ >= 3:
                        eng.wait_ge(MM.h, mm0 + 2 * (i_ - 2) + 1)
                    last = eng.tensor_mul(
                        BC[0:64, 3072 + (i_ % 2) * 1024:3072 + (i_ % 2) * 1024 + 1024],
                        BC[0:64, 5120:6144], SC[:, :])
                    VE.inc(last)
                    eng.wait_ge(MM.h, mm0 + 2 * i_)
                    last = eng.scalar_tensor_tensor(
                        SC[:, :], SC[:, :], CN[0:64, 134:135],
                        psSL[:, (i_ % 2) * 1024:(i_ % 2) * 1024 + 1024],
                        OP.mult, OP.add)
                    VE.inc(last)
                    eng.wait_ge(MM.h, mm0 + 2 * i_ + 1)
                    last = eng.tensor_add(
                        BA[:, i_ * 1024:(i_ + 1) * 1024],
                        BA[:, i_ * 1024:(i_ + 1) * 1024],
                        psEX[:, (i_ % 2) * 1024:(i_ % 2) * 1024 + 1024])
                    VE.inc(last)
            return (VE, VE.n)
        ctx.sec("vector", v10)

        ctx.barrier()
    if meta.get("p2_stage", 99) < 8:
        return

    with nc.psum_tensor("psI", [128, 4096], F32) as psI:
        TA, TB2 = ctx.sm["TA"], ctx.sm["TB"]
        tpsem = [TA, TB2]
        mm1 = MM.n
        ve1 = VE.n
        tz0 = TZ.n
        ta0, tb0 = TA.n, TB2.n

        def syT(eng):
            last = None
            for g in range(32):
                if g >= 2:
                    eng.wait_ge(TZ.h, tz0 + g - 1)
                c0 = g * 32
                src = _sl(t["kpad"], c0 * 256, [[1, 128], [256, 32], [1, 128]])
                dst = _sl(W.ap(), (g % 2) * 4096, [[8192, 128], [128, 32], [1, 128]])
                last = eng.dma_start(out=dst, in_=src)
                tpsem[g % 2].inc(last, 16)
            eng.wait_ge(TA.h, TA.n)
            return (TB2, TB2.n)
        ctx.sec("sync", syT)

        _imode = meta.get("p2_intra", "full")

        def t11(eng):
            last = None
            base = [ta0, tb0]
            for g in range(32):
                eng.wait_ge(tpsem[g % 2].h, base[g % 2] + 16 * (g // 2 + 1))
                r = g // 4
                if g >= 8:
                    eng.wait_ge(VE.h, ve1 + (r - 1) * NCH)
                if _imode == "dma":
                    last = eng.wait_ge(MM.h, 0)
                    TZ.inc(last)
                    continue
                for cl in range(32):
                    c = g * 32 + cl
                    rhs = _sl(BB.ap(), c, [[17664, 128], [1024, 16]])
                    cloc = (g % 4) * 32 + cl
                    cb_ = (r % 2) * 2048 + (cloc % 4) * 512 + (cloc // 4) * 16
                    last = eng.matmul(
                        psI[:, cb_:cb_ + 16],
                        W[:, (g % 2) * 4096 + cl * 128:(g % 2) * 4096 + cl * 128 + 128],
                        rhs, start=True, stop=True)
                    if _imode == "ser" and cl < 31:
                        ctx.chain(eng, MM, last)
                TZ.inc(last)
            return (TZ, TZ.n)
        ctx.sec("tensor", t11)

        def v11(eng):
            last = None
            for r in range(8):
                eng.wait_ge(TZ.h, tz0 + 4 * (r + 1))
                for i_ in range(NCH):
                    src = _sl(psI.ap(), (r % 2) * 2048 + i_,
                              [[4096, 128], [512, 4], [16, 32]])
                    dstc = _sl(BA.ap(), i_ * 1024 + r * 128,
                               [[16384, 128], [1, 4], [4, 32]])
                    last = eng.tensor_add(dstc, dstc, src)
                    VE.inc(last)
            return (VE, VE.n)
        ctx.sec("vector", v11)

        ctx.barrier()
    if meta.get("p2_stage", 99) < 9:
        return

    with nc.psum_tensor("psREP2", [128, 2048], F32) as psREP2:
        def t12(eng):
            last = None
            for q in range(4):
                last = eng.matmul(psREP2[:, q * 512:(q + 1) * 512], onesF64,
                                  RW[64:65, q * 512:(q + 1) * 512], start=True, stop=True)
            MM.inc(last)
            return (MM, MM.n)
        ctx.sec("tensor", t12)

        mm_rs = MM.n

        def v12(eng):
            eng.wait_ge(MM.h, mm_rs)
            Rv = _sl(R.ap(), 0, [[16384, 128], [T, NCC], [1, T]])
            i = eng.tensor_mul(Rv, Rv,
                               _sl(psREP2.ap(), 0, [[2048, 128], [0, NCC], [1, T]]))
            VE.inc(i)
            return (VE, VE.n)
        ctx.sec("vector", v12)
        ctx.barrier()

    with nc.psum_tensor("psTR2", [128, 2048], BF16) as psTR2, \
         nc.psum_tensor("psKD", [128, 2048], F32) as psKD:
        jobs2 = [(i_, cc) for i_ in range(NTT) for cc in range(NCC)]
        mm_tb = MM.n
        ve_tb = VE.n

        def t13(eng):
            last = None
            for j, (i_, cc) in enumerate(jobs2):
                if j >= 2:
                    eng.wait_ge(VE.h, ve_tb + j - 1)
                last = eng.transpose(psTR2[:, (j % 2) * 1024:(j % 2) * 1024 + 128],
                                     BA[:, i_ * 1024 + cc * 128: i_ * 1024 + cc * 128 + 128],
                                     CB[:, 3524:3524 + 128])
                MM.inc(last)
            return (MM, MM.n)
        ctx.sec("tensor", t13)

        def v13(eng):
            last = None
            for j, (i_, cc) in enumerate(jobs2):
                eng.wait_ge(MM.h, mm_tb + j + 1)
                dst = R[:, cc * T + i_ * 128: cc * T + i_ * 128 + 128]
                last = eng.tensor_add(dst, dst,
                                      psTR2[:, (j % 2) * 1024:(j % 2) * 1024 + 128])
                VE.inc(last)
            return (VE, VE.n)
        ctx.sec("vector", v13)

        mm_kd = MM.n
        ve_after_tb = ve_tb + len(jobs2)

        def t14(eng):
            eng.wait_ge(VE.h, ve_after_tb)
            last = None
            for cc in range(NCC):
                if cc >= 1:
                    eng.wait_ge(VE.h, ve_after_tb + cc)
                for q in range(4):
                    last = eng.matmul(psKD[:, q * 512:(q + 1) * 512],
                                      BC[0:65, cc * 128:(cc + 1) * 128],
                                      BC[0:65, 1024 + q * 512:1024 + (q + 1) * 512],
                                      start=True, stop=True)
                MM.inc(last)
            return (MM, MM.n)
        ctx.sec("tensor", t14)

        def v14(eng):
            last = None
            for cc in range(NCC):
                eng.wait_ge(MM.h, mm_kd + cc + 1)
                last = eng.scalar_tensor_tensor(
                    R[:, cc * T:(cc + 1) * T], psKD[:, 0:2048],
                    SM[:, 16 * cc + 1:16 * cc + 2], R[:, cc * T:(cc + 1) * T],
                    OP.mult, OP.add)
                VE.inc(last)
            return (VE, VE.n)
        ctx.sec("vector", v14)



def _plan(inputs):
    top_idx = np.asarray(inputs["L0_ffn_top_idx"])
    tok = top_idx.T.astype(np.int64)
    T_ = T

    perms, tok_sorted = [], []
    for e in range(NE):
        p = np.argsort(tok[e] % T_, kind="stable")
        perms.append(p)
        tok_sorted.append(tok[e][p])

    nch_need = np.zeros((NE, 16), np.int64)
    chunk_starts = [[None] * 16 for _ in range(NE)]
    for e in range(NE):
        loc = tok_sorted[e] % T_
        for ti in range(16):
            a_ = int(np.searchsorted(loc, ti * 128, side="left"))
            b_ = int(np.searchsorted(loc, ti * 128 + 128, side="left"))
            c0 = a_ // 128
            c1 = max((b_ + 127) // 128, c0 + 1)
            nch_need[e, ti] = c1 - c0
            chunk_starts[e][ti] = c0
    nchunk = int(nch_need.max())
    grid = [[None] * nchunk for _ in range(16)]
    rowstart = np.zeros((NE, 16, nchunk), np.int64)
    for e in range(NE):
        for ti in range(16):
            for ch in range(nchunk):
                s = min(chunk_starts[e][ti] + ch, 2048 // 128 - 1) * 128
                rowstart[e, ti, ch] = s

    meta = dict(nchunk=nchunk, rowstart=rowstart)
    plans = dict(perms=perms, tok_sorted=tok_sorted, rowstart=rowstart,
                 nchunk=nchunk)
    return meta, plans


def _build_sel(plans, core):
    NCH4 = plans["nchunk"]
    sel = np.zeros((16 * NE * NCH4 * 128, 128), np.float32)
    for ti in range(16):
        for e in range(NE):
            ts = plans["tok_sorted"][e]
            for ch in range(NCH4):
                s = plans["rowstart"][e, ti, ch]
                if ch > 0 and s == plans["rowstart"][e, ti, ch - 1]:
                    continue
                blk = (ti * NE + e) * NCH4 + ch
                rows = ts[s:s + 128]
                tl = rows - (core * T + ti * 128)
                jj = np.nonzero((tl >= 0) & (tl < 128))[0]
                sel[blk * 128 + jj, tl[jj]] = 1.0
    return sel.astype(BF16NP)


def _consts_row():
    c = np.zeros(4096, np.float32)
    c[0:2048] = np.arange(2048, dtype=np.float32)
    c[2048 + 0] = 1.0
    c[2048 + 1] = -1.0
    c[2048 + 2] = 1.0 / 1024.0
    c[2048 + 3] = LN_EPS
    c[2048 + 4] = 0.5
    c[2048 + 5] = 1.0 / np.sqrt(2.0)
    c[2048 + 6] = ATT_EPS
    c[2048 + 7] = DH ** -0.5
    return c[None, :]


def make_in_maps(inputs, meta, plans):
    f32 = lambda a: np.ascontiguousarray(np.asarray(a), dtype=np.float32)
    x = f32(inputs["L0_attn_x"])
    xhead = np.ascontiguousarray(x[0:4].reshape(4 * T, C))
    cwbase = f32(inputs["L0_ffn_combine_weights"])
    tokT = np.asarray(inputs["L0_ffn_top_idx"]).T.astype(np.int64)

    pp0 = np.zeros((C, 16), np.float32)
    pp0[:, 0] = f32(inputs["L0_attn_norm1_w"])
    pp0[:, 1] = f32(inputs["L0_attn_norm1_b"])
    pp0[:, 2] = f32(inputs["L0_attn_D"])
    pp0[:, 3] = f32(inputs["L0_attn_norm2_w"])
    pp0[:, 4] = f32(inputs["L0_attn_norm2_b"])
    pp0[:, 5] = f32(inputs["L0_attn_fc2_b"])
    pp0[:, 6] = f32(inputs["L1_ffn_mu_r"])
    pp0[:, 7] = f32(inputs["L1_ffn_mu_k"])

    stau = np.zeros((1, N + 1), np.float32)
    stau[0, 0:N] = f32(inputs["L0_attn_log_tau"])
    stau[0, N] = np.float32(np.asarray(inputs["L0_attn_log_dt"]))

    consts = _consts_row()
    cconst = np.tile(consts[:, 2048:2064], (128, 1)).astype(np.float32)
    idn = np.eye(128, dtype=np.float32)
    jdn = np.eye(128, dtype=np.float32)[:, ::-1].copy()
    tri = np.triu(np.ones((128, 128), np.float32))

    mixT = np.ascontiguousarray(f32(inputs["L0_attn_mix"]).T)
    fc1T = np.ascontiguousarray(f32(inputs["L0_attn_fc1_w"]).T)
    fc2T = np.ascontiguousarray(f32(inputs["L0_attn_fc2_w"]).T)
    WqT = np.ascontiguousarray(f32(inputs["L1_attn_Wq"]).T)
    WkT = np.ascontiguousarray(f32(inputs["L1_attn_Wk"]).T)
    WvT = np.ascontiguousarray(f32(inputs["L1_attn_Wv"]).T)
    WoT = np.ascontiguousarray(f32(inputs["L1_attn_Wo"]).T)
    WrT = np.ascontiguousarray(f32(inputs["L1_ffn_W_r"]).T)
    WkkT = np.ascontiguousarray(f32(inputs["L1_ffn_W_k"]).T)
    WvvT = np.ascontiguousarray(f32(inputs["L1_ffn_W_v"]).T)
    W1 = f32(inputs["L0_ffn_W1"])
    W2 = f32(inputs["L0_ffn_W2"])
    b1 = f32(inputs["L0_ffn_b1"])
    b2 = f32(inputs["L0_ffn_b2"])
    fc1_b = f32(inputs["L0_attn_fc1_b"])

    in_maps = []
    for k in range(8):
        srt = plans["tok_sorted"][k]
        xg = np.ascontiguousarray(xhead[srt])
        wvals = cwbase[srt, k].astype(np.float32)
        wg = np.ascontiguousarray(wvals.reshape(16, 128).T)
        sel = _build_sel(plans, k)
        fcb = np.stack([fc1_b, b1[k]], 1)
        prow = np.concatenate([f32(inputs["L0_attn_D"]), b2[k],
                               f32(inputs["L0_attn_norm1_w"])])[None, :]
        in_maps.append(dict(
            xT=np.ascontiguousarray(x[k].T),
            xg=xg, wg=wg, sel=sel,
            mixT=mixT, stau=stau, pp0=pp0, prow=prow, fcb=fcb,
            fc1T=fc1T, fc2T=fc2T,
            W1e=np.ascontiguousarray(W1[k]),
            W2e=np.ascontiguousarray(W2[k]),
            WqT=WqT, WkT=WkT, WvT=WvT, WoT=WoT,
            WrT=WrT, WkkT=WkkT, WvvT=WvvT,
            consts=consts, cconst=cconst, idn=idn, jdn=jdn, tri=tri,
        ))
    return in_maps


def kernel(**inputs):
    meta, plans = _plan(inputs)
    nc = build_program(meta)
    in_maps = make_in_maps(inputs, meta, plans)
    res = run_bass_kernel_spmd(nc, in_maps, core_ids=list(range(8)))
    out = np.stack([np.asarray(res.results[k]["yT"]).T for k in range(8)], axis=0)
    return np.ascontiguousarray(out)


# version 04
# speedup vs baseline: 1.0104x; 1.0104x over previous
import contextlib
import dataclasses

import ml_dtypes
import numpy as np

import concourse.bass as bass
import concourse.mybir as mybir
from concourse.bass_utils import run_bass_kernel_spmd

F32 = mybir.dt.float32
BF16 = mybir.dt.bfloat16
I16 = mybir.dt.int16
AF = mybir.ActivationFunctionType
OP = mybir.AluOpType
BF16NP = ml_dtypes.bfloat16

C = 1024
T = 2048
NCC = 8
NTT = 16
N = 64
FF = 4096
NE = 8
HEADS = 16
DH = 64
NCH = 16

LN_EPS = 1e-5
ATT_EPS = 1e-6


def _sl(ap_, offset, dims):
    return dataclasses.replace(ap_, offset=int(offset),
                               ap=mybir.VecI64Pair([[int(a), int(b)] for a, b in dims]))


class Sem:
    __slots__ = ("h", "n")

    def __init__(self, h):
        self.h = h
        self.n = 0

    def inc(self, inst, k=1):
        inst.then_inc(self.h, k)
        self.n += k
        return self.n


class Ctx:
    def __init__(self, nc, block, stack):
        self.nc = nc
        self.block = block
        self.stack = stack
        self.bar = self.sem("BAR")
        self._bar_pend = []
        self.t = {}
        self.s = {}
        self.sm = {}

    def sem(self, name):
        return Sem(self.stack.enter_context(self.nc.semaphore(name)))

    def sec(self, eng_name, fn):
        base = self.bar.n

        def wrapped(eng):
            eng.wait_ge(self.bar.h, base)
            fin = fn(eng)
            if fin is not None:
                sem, val = fin
                w = eng.wait_ge(sem.h, val)
                self._bar_pend.append(w)
        getattr(self.block, eng_name)(wrapped)

    def barrier(self):
        for inst in self._bar_pend:
            inst.then_inc(self.bar.h, 1)
            self.bar.n += 1
        self._bar_pend = []

    def chain(self, eng, sem, inst):
        sem.inc(inst)
        eng.wait_ge(sem.h, sem.n)




TRI0 = 1664


def build_program(meta):
    nc = bass.Bass("TRN2", target_bir_lowering=False)
    t = {}

    def din(name, shape, dt=F32):
        t[name] = nc.dram_tensor(name, list(shape), dt, kind="ExternalInput").ap()

    def dout(name, shape, dt=F32):
        t[name] = nc.dram_tensor(name, list(shape), dt, kind="ExternalOutput").ap()

    def dint(name, shape, dt=F32):
        t[name] = nc.dram_tensor(name, list(shape), dt).ap()

    din("xT", [C, T])
    din("xg", [T, C])
    din("wg", [128, 16])
    din("sel", [16 * NE * meta["nchunk"] * 128, 128], BF16)
    din("mixT", [N, C])
    din("stau", [1, N + 1])
    din("pp0", [C, 16])
    din("prow", [1, 3 * C])
    din("fcb", [FF, 2])
    din("fc1T", [C, FF])
    din("fc2T", [FF, C])
    din("W1e", [C, FF])
    din("W2e", [FF, C])
    din("WqT", [C, C]); din("WkT", [C, C]); din("WvT", [C, C]); din("WoT", [C, C])
    din("WrT", [C, C]); din("WkkT", [C, C]); din("WvvT", [C, C])
    din("consts", [1, 4096])
    din("cconst", [128, 16])
    din("idn", [128, 128])
    din("jdn", [128, 128])
    din("tri", [128, 128])
    dout("yT", [C, T])
    dint("kpad", [C, 256], BF16)
    dint("ag_in", [T, C])
    dint("ag_out", [NE * T, C])
    dint("acc_moe", [T, C])
    dint("xspill", [C, T])

    with contextlib.ExitStack() as st:
        def sb(name, shape, dt):
            return st.enter_context(nc.sbuf_tensor(name, shape, dt))

        S = dict(
            R=sb("R", [128, 16384], F32),
            BA=sb("BA", [128, 16384], BF16),
            BB=sb("BB", [128, 17664], BF16),
            BC=sb("BC", [128, 16384], BF16),
            W=sb("W", [128, 8192], BF16),
            RW=sb("RW", [128, 3312], F32),
            SC=sb("SC", [64, 1024], F32),

            CN=sb("CN", [128, 288], F32),
            CB=sb("CB", [128, 3652], BF16),
            SM=sb("SM", [128, 192], F32),
        )
        block = st.enter_context(nc.Block())
        ctx = Ctx(nc, block, st)
        ctx.t, ctx.s = t, S
        for nm in ("LD", "GD", "WD", "MM", "VE", "SE", "GE", "ST", "CCS", "TZ", "TA", "TB", "WA2", "WB2"):
            ctx.sm[nm] = ctx.sem(nm)
        emit(ctx, meta)
        build_program.dbg = dict(S=S, t=t)
    return nc


CONSTS0 = 2048


def crow(S, j, b=0, p=1):
    return S["CN"][b:b + p, 270 + j:270 + j + 1]


def emit(ctx, meta):
    nc, t, S = ctx.nc, ctx.t, ctx.s
    phases = meta.get("phases", 99)
    tap = meta.get("tap", None)
    ST = ctx.sm["ST"]

    emit_p0(ctx, meta)
    ctx.barrier()
    if phases >= 1 and not meta.get("skip_moe", False):
        emit_p1(ctx, meta)
        ctx.barrier()
    if phases >= 2:
        emit_p2(ctx, meta)
        ctx.barrier()
    if phases >= 3:
        emit_p3(ctx, meta)
        ctx.barrier()
    if phases >= 4 and not meta.get("skip_moe", False):
        emit_p4(ctx, meta)
        ctx.barrier()
    if phases >= 5:
        emit_p5(ctx, meta)
        ctx.barrier()

    def pf_sync(eng):
        i = eng.dma_start(
            out=_sl(t["yT"], 0, [[T, 128], [128 * T, NCC], [1, T]]),
            in_=_sl(S["R"].ap(), 0, [[16384, 128], [T, NCC], [1, T]]))
        ST.inc(i, 16)
        return (ST, ST.n)
    ctx.sec("sync", pf_sync)
    ctx.barrier()


def emit_p0(ctx, meta):
    nc, t, S = ctx.nc, ctx.t, ctx.s
    LD, GD, VE, SE, GE = (ctx.sm[k] for k in ("LD", "GD", "VE", "SE", "GE"))

    def p0_sync(eng):
        i = eng.dma_start(out=S["CN"][:, 0:128], in_=t["idn"][:, :])
        LD.inc(i, 16)
        for cc in range(NCC):
            i = eng.dma_start(out=S["SM"][:, 16 * cc:16 * cc + 16],
                              in_=t["pp0"][cc * 128:(cc + 1) * 128, :])
            LD.inc(i, 16)
        for fc in range(FF // 128):
            i = eng.dma_start(out=S["SM"][:, 128 + 2 * fc:128 + 2 * fc + 2],
                              in_=t["fcb"][fc * 128:(fc + 1) * 128, :])
            LD.inc(i, 16)
        i = eng.dma_start(out=S["RW"][0:1, 0:2064], in_=t["consts"][:, 0:2064])
        LD.inc(i, 16)
        i = eng.dma_start(out=S["CN"][:, 270:286], in_=t["cconst"][:, :])
        LD.inc(i, 16)
        i = eng.dma_start(out=S["RW"][32:33, 2112:2112 + N + 1], in_=t["stau"][:, :])
        LD.inc(i, 16)
        return (LD, LD.n)
    ctx.sec("sync", p0_sync)

    def p0_vec(eng):
        eng.memset(S["CN"][0:1, 140:268], 1.0)
        eng.memset(S["CN"][32:33, 140:268], 1.0)
        eng.memset(S["CN"][64:65, 140:268], 1.0)
        eng.memset(S["CN"][:, 129:130], 1.0)
        eng.memset(S["CB"][:, 128:129], 1.0)
        i = eng.memset(S["W"][:, :], 0.0)
        VE.inc(i)
        return (VE, VE.n)
    ctx.sec("vector", p0_vec)

    def p0_gp(eng):
        i = eng.iota(S["CN"][:, 128:129], pattern=[[0, 1]], base=0,
                     channel_multiplier=1, allow_small_or_imprecise_dtypes=True)
        GE.inc(i)
        i = eng.dma_start(out=S["CB"][:, 2240:2240 + 128], in_=t["tri"][:, :])
        GD.inc(i, 16)
        i = eng.dma_start(out=S["CB"][:, 3524:3524 + 128], in_=t["jdn"][:, :])
        GD.inc(i, 16)
        eng.wait_ge(ctx.sm["VE"].h, ctx.sm["VE"].n)
        Wf = S["W"].ap().bitcast(F32)
        for j in range(8):
            i = eng.dma_start(out=t["acc_moe"][j * 256:(j + 1) * 256, :],
                              in_=_sl(Wf, 0, [[4096, 128], [0, 2], [1, 1024]]))
            GD.inc(i, 16)
        return (GD, GD.n)
    ctx.sec("gpsimd", p0_gp)

    def p0_sc(eng):
        eng.wait_ge(LD.h, LD.n)
        i = eng.copy(out=S["CB"][:, 0:128], in_=S["CN"][:, 0:128])
        SE.inc(i)
        return (SE, SE.n)
    ctx.sec("scalar", p0_sc)


def emit_p1(ctx, meta):
    nc, t, S = ctx.nc, ctx.t, ctx.s
    LD, GD, MM, VE, SE, GE = (ctx.sm[k] for k in ("LD", "GD", "MM", "VE", "SE", "GE"))
    WA2, WB2, CCS = ctx.sm["WA2"], ctx.sm["WB2"], ctx.sm["CCS"]
    R, BA, BB, BC, W, RW, CN, CB, SM = (
        S[k] for k in ("R", "BA", "BB", "BC", "W", "RW", "CN", "CB", "SM"))
    onesF = CN[0:1, 140:268]
    WCOL = 3290
    BNC = 3270
    B2R = 2064

    def sy1(eng):
        i = eng.dma_start(
            out=_sl(R.ap(), 0, [[16384, 128], [1024, 16], [1, 1024]]),
            in_=_sl(t["xg"], 0, [[1024, 128], [131072, 16], [1, 1024]]))
        LD.inc(i, 16)
        i = eng.dma_start(out=RW[:, WCOL:WCOL + 16], in_=t["wg"][:, :])
        LD.inc(i, 16)
        i = eng.dma_start(out=RW[0:1, B2R:B2R + C], in_=t["prow"][:, C:2 * C])
        LD.inc(i, 16)
        return (LD, LD.n)
    ctx.sec("sync", sy1)
    ctx.barrier()

    _lnk = meta.get("p1_ln_k", 16)
    _lnm = meta.get("p1_ln_mode", "full")

    def vLN(eng):
        last = None
        for i_ in range(_lnk):
            last = eng.reduce_sum(RW[:, BNC:BNC + 1], R[:, i_ * 1024:(i_ + 1) * 1024],
                                  axis=mybir.AxisListType.X)
            ctx.chain(eng, VE, last)
            last = eng.tensor_scalar_mul(RW[:, BNC + 1:BNC + 2], RW[:, BNC:BNC + 1],
                                         -1.0 / 1024.0)
            VE.inc(last)
            eng.wait_ge(SE.h, ctx._p1_se_base + 2 * i_ + 2)
            last = eng.reciprocal(RW[:, BNC + 4:BNC + 5], RW[:, BNC + 3:BNC + 4])
            ctx.chain(eng, VE, last)
            if _lnm == "nots":
                last = eng.tensor_copy(out=BA[:, i_ * 1024:(i_ + 1) * 1024],
                                       in_=R[:, i_ * 1024:(i_ + 1) * 1024])
            else:
                last = eng.tensor_scalar(BA[:, i_ * 1024:(i_ + 1) * 1024],
                                         R[:, i_ * 1024:(i_ + 1) * 1024],
                                         RW[:, BNC + 1:BNC + 2],
                                         RW[:, BNC + 4:BNC + 5],
                                         OP.add, OP.mult)
            VE.inc(last)
        return (VE, VE.n)

    def sLN(eng):
        last = None
        for i_ in range(_lnk):
            eng.wait_ge(VE.h, ctx._p1_ve_base + i_ * 4 + 2)
            if _lnm == "nosq":
                last = eng.activation(RW[:, BNC + 2:BNC + 3], RW[:, BNC:BNC + 1],
                                      AF.Copy, bias=0.0)
            else:
                last = eng.activation(BA[:, i_ * 1024:(i_ + 1) * 1024],
                                      R[:, i_ * 1024:(i_ + 1) * 1024],
                                      AF.Square, bias=RW[:, BNC + 1:BNC + 2],
                                      accum_out=RW[:, BNC + 2:BNC + 3])
            ctx.chain(eng, SE, last)
            if _lnm == "nosqrt":
                last = eng.activation(RW[:, BNC + 3:BNC + 4], RW[:, BNC + 2:BNC + 3],
                                      AF.Copy, bias=0.0)
            else:
                last = eng.activation(RW[:, BNC + 3:BNC + 4], RW[:, BNC + 2:BNC + 3],
                                      AF.Sqrt, scale=crow(S, 2, 0, 128),
                                      bias=crow(S, 3, 0, 128))
            SE.inc(last)
        return (SE, SE.n)

    ctx._p1_ve_base = ctx.sm["VE"].n
    ctx._p1_se_base = ctx.sm["SE"].n
    if meta.get("p1_stage", 9) < 1:
        return
    ctx.sec("vector", vLN)
    ctx.sec("scalar", sLN)
    ctx.barrier()
    if meta.get("p1_stage", 9) < 2:
        return

    _tsw = 1024
    with nc.psum_tensor("psT1", [128, 2 * _tsw], BF16) as psT1:
        jobs = [(i_, cc) for i_ in range(16) for cc in range(NCC)]
        jobs = jobs[meta.get("p1_tr_lo", 0):meta.get("p1_tr_k", len(jobs))]
        mm_tr = MM.n
        se_tr = SE.n

        def tT(eng):
            last = None
            for j, (i_, cc) in enumerate(jobs):
                if j >= 2:
                    eng.wait_ge(SE.h, se_tr + j - 1)
                last = eng.transpose(psT1[:, (j % 2) * _tsw:(j % 2) * _tsw + 128],
                                     BA[:, i_ * 1024 + cc * 128: i_ * 1024 + cc * 128 + 128],
                                     CB[:, 0:128])
                MM.inc(last)
            return (MM, MM.n)
        ctx.sec("tensor", tT)

        def sT(eng):
            last = None
            for j, (i_, cc) in enumerate(jobs):
                eng.wait_ge(MM.h, mm_tr + j + 1)
                if meta.get("p1_tr_ev") == "vector":
                    last = eng.tensor_copy(
                        out=BC[:, cc * T + i_ * 128: cc * T + i_ * 128 + 128],
                        in_=psT1[:, (j % 2) * _tsw:(j % 2) * _tsw + 128])
                else:
                    last = eng.copy(
                        out=BC[:, cc * T + i_ * 128: cc * T + i_ * 128 + 128],
                        in_=psT1[:, (j % 2) * _tsw:(j % 2) * _tsw + 128])
                SE.inc(last)
            return (SE, SE.n)
        ctx.sec("vector" if meta.get("p1_tr_ev") == "vector" else "scalar", sT)
        ctx.barrier()
    if meta.get("p1_stage", 9) < 3:
        return

    with nc.psum_tensor("psM1", [128, 1024], F32) as psM1, \
         nc.psum_tensor("psM2", [128, 2048], F32) as psM2:
        mm0 = MM.n
        se0 = SE.n
        ve0 = VE.n
        wa0, wb0 = WA2.n, WB2.n

        def mmq(Q):
            return mm0 + Q * 48

        def gpW(eng):
            for Q in range(4):
                for e2 in range(2):
                    ge = Q * 2 + e2
                    if ge >= 1:
                        eng.wait_ge(MM.h, mmq((ge - 1) // 2) + ((ge - 1) % 2) * 16 + 16)
                        if e2 == 0:
                            eng.wait_ge(MM.h, mmq(Q))
                    i = eng.dma_start(
                        out=_sl(W.ap(), 0, [[8192, 128], [512, 8], [1, 512]]),
                        in_=_sl(t["W1e"], 512 * ge, [[4096, 128], [4096 * 128, 8], [1, 512]]))
                    WA2.inc(i, 16)
                eng.wait_ge(MM.h, mmq(Q) + 32)
                i = eng.dma_start(
                    out=_sl(W.ap(), 0, [[8192, 128], [1024, 8], [1, 1024]]),
                    in_=_sl(t["W2e"], 1024 * Q * 1024,
                            [[1024, 128], [128 * 1024, 8], [1, 1024]]))
                WB2.inc(i, 16)
            return (WB2, WB2.n)
        ctx.sec("gpsimd", gpW)

        def tM(eng):
            last = None
            for Q in range(4):
                for e2 in range(2):
                    ge = Q * 2 + e2
                    eng.wait_ge(WA2.h, wa0 + 16 * (ge + 1))
                    for j, (sq, fj) in enumerate([(a, b) for a in range(4) for b in range(4)]):
                        gj = ge * 16 + j
                        if gj >= 2:
                            eng.wait_ge(SE.h, se0 + gj - 1)
                        for cc in range(NCC):
                            last = eng.matmul(
                                psM1[:, (j % 2) * 512:(j % 2) * 512 + 512],
                                W[:, cc * 512 + fj * 128: cc * 512 + fj * 128 + 128],
                                BC[:, cc * T + sq * 512: cc * T + (sq + 1) * 512],
                                start=(cc == 0), stop=(cc == NCC - 1))
                        MM.inc(last)
                eng.wait_ge(WB2.h, wb0 + 16 * (Q + 1))
                eng.wait_ge(SE.h, se0 + (Q + 1) * 32)
                for st in range(16):
                    gs = Q * 16 + st
                    if gs >= 2:
                        eng.wait_ge(VE.h, ve0 + gs - 1)
                    addb2 = (Q == 0)
                    for fj8 in range(8):
                        lastf = (fj8 == 7) and not addb2
                        for hv in range(2):
                            last = eng.matmul(
                                psM2[:, (st % 2) * 1024 + hv * 512:(st % 2) * 1024 + (hv + 1) * 512],
                                BB[:, fj8 * 2048 + st * 128: fj8 * 2048 + st * 128 + 128],
                                W[:, fj8 * 1024 + hv * 512: fj8 * 1024 + (hv + 1) * 512],
                                start=(fj8 == 0), stop=lastf)
                    if addb2:
                        for hv in range(2):
                            last = eng.matmul(
                                psM2[:, (st % 2) * 1024 + hv * 512:(st % 2) * 1024 + (hv + 1) * 512],
                                onesF, RW[0:1, B2R + hv * 512:B2R + (hv + 1) * 512],
                                start=False, stop=True)
                    MM.inc(last)
            return (MM, MM.n)
        ctx.sec("tensor", tM)

        def sM(eng):
            last = None
            for Q in range(4):
                for e2 in range(2):
                    ge = Q * 2 + e2
                    for j, (sq, fj) in enumerate([(a, b) for a in range(4) for b in range(4)]):
                        eng.wait_ge(MM.h, mmq(Q) + e2 * 16 + j + 1)
                        fl = e2 * 4 + fj
                        fg = Q * 8 + fl
                        last = eng.activation(
                            BB[:, fl * 2048 + sq * 512: fl * 2048 + (sq + 1) * 512],
                            psM1[:, (j % 2) * 512:(j % 2) * 512 + 512],
                            AF.Gelu_apprx_tanh, bias=SM[:, 129 + 2 * fg:129 + 2 * fg + 1])
                        SE.inc(last)
            return (SE, SE.n)
        ctx.sec("scalar", sM)

        def vM(eng):
            last = None
            for Q in range(4):
                for st in range(16):
                    eng.wait_ge(MM.h, mmq(Q) + 32 + st + 1)
                    dst = R[:, st * 1024:(st + 1) * 1024]
                    srcp = psM2[:, (st % 2) * 1024:(st % 2) * 1024 + 1024]
                    if Q == 0:
                        last = eng.tensor_copy(out=dst, in_=srcp)
                    else:
                        last = eng.tensor_add(dst, dst, srcp)
                    VE.inc(last)
            return (VE, VE.n)
        ctx.sec("vector", vM)
        ctx.barrier()
    if meta.get("p1_stage", 9) < 4:
        return

    def sW(eng):
        last = None
        for st in range(16):
            last = eng.activation(R[:, st * 1024:(st + 1) * 1024],
                                  R[:, st * 1024:(st + 1) * 1024],
                                  AF.Copy, scale=RW[:, WCOL + st:WCOL + st + 1])
            SE.inc(last)
        return (SE, SE.n)
    ctx.sec("scalar", sW)
    ctx.barrier()

    def syA(eng):
        i = eng.dma_start(
            out=_sl(t["ag_in"], 0, [[1024, 128], [131072, 16], [1, 1024]]),
            in_=_sl(R.ap(), 0, [[16384, 128], [1024, 16], [1, 1024]]))
        LD.inc(i, 16)
        return (LD, LD.n)
    ctx.sec("sync", syA)

    def gpA(eng):
        eng.wait_ge(LD.h, LD.n)
        i = eng.collective_compute(
            "AllGather", mybir.AluOpType.bypass,
            replica_groups=[list(range(8))],
            ins=[t["ag_in"]], outs=[t["ag_out"]])
        CCS.inc(i)
        return None
    ctx.sec("gpsimd", gpA)


def emit_p3(ctx, meta):
    nc, t, S = ctx.nc, ctx.t, ctx.s
    LD, GD, MM, VE, SE = (ctx.sm[k] for k in ("LD", "GD", "MM", "VE", "SE"))
    WA2, WB2 = ctx.sm["WA2"], ctx.sm["WB2"]
    R, BA, BB, BC, W, RW, CN, CB, SM = (
        S[k] for k in ("R", "BA", "BB", "BC", "W", "RW", "CN", "CB", "SM"))
    ones_b = CB[:, 128:129]
    onesF32 = CN[32:33, 140:268]
    onesF64 = CN[64:65, 140:268]
    Rv = _sl(R.ap(), 0, [[16384, 128], [T, NCC], [1, T]])
    BCv = _sl(BC.ap(), 0, [[16384, 128], [T, NCC], [1, T]])

    def brc(ps, w):
        return _sl(ps.ap(), 0, [[2048, 128], [0, NCC], [1, w]])

    with nc.psum_tensor("psQ3", [1, 512], F32) as psQ, \
         nc.psum_tensor("psREP3", [128, 2048], F32) as psREP:

        def s1(eng):
            i = eng.copy(out=BC[:, :], in_=R[:, :])
            SE.inc(i)
            return (SE, SE.n)
        ctx.sec("scalar", s1)
        if meta.get("p5_ln", 9) < 2:
            return

        se_cast = SE.n
        mm_q = MM.n
        se_q = SE.n

        def t1(eng):
            eng.wait_ge(SE.h, se_cast)
            last = None
            for q in range(4):
                if q >= 1:
                    eng.wait_ge(SE.h, se_q + q)
                for cc in range(NCC):
                    last = eng.matmul(psQ[0:1, :], ones_b,
                                      BC[:, cc * T + q * 512: cc * T + (q + 1) * 512],
                                      start=(cc == 0), stop=(cc == NCC - 1))
                MM.inc(last)
            return (MM, MM.n)
        ctx.sec("tensor", t1)

        def s2(eng):
            last = None
            for q in range(4):
                eng.wait_ge(MM.h, mm_q + q + 1)
                last = eng.activation(RW[32:33, q * 512:(q + 1) * 512], psQ[0:1, :],
                                      AF.Copy, scale=crow(S, 2, 32))
                SE.inc(last)
            return (SE, SE.n)
        ctx.sec("scalar", s2)
        if meta.get("p5_ln", 9) < 3:
            return

        se_mu = SE.n

        def t2(eng):
            eng.wait_ge(SE.h, se_mu)
            last = None
            for q in range(4):
                last = eng.matmul(psREP[:, q * 512:(q + 1) * 512], onesF32,
                                  RW[32:33, q * 512:(q + 1) * 512], start=True, stop=True)
            MM.inc(last)
            return (MM, MM.n)
        ctx.sec("tensor", t2)

        mm_murep = MM.n

        def v1(eng):
            eng.wait_ge(MM.h, mm_murep)
            i = eng.tensor_sub(BCv, Rv, brc(psREP, 2048))
            VE.inc(i)
            return (VE, VE.n)
        ctx.sec("vector", v1)
        if meta.get("p5_ln", 9) < 4:
            return

        ve_xc = VE.n

        def s3(eng):
            eng.wait_ge(VE.h, ve_xc)
            i = eng.activation(BA[:, :], BC[:, :], AF.Square)
            SE.inc(i)
            return (SE, SE.n)
        ctx.sec("scalar", s3)

        se_sq = SE.n
        mm_q2 = MM.n
        se_q2 = SE.n

        def t3(eng):
            eng.wait_ge(SE.h, se_sq)
            last = None
            for q in range(4):
                if q >= 1:
                    eng.wait_ge(SE.h, se_q2 + q)
                for cc in range(NCC):
                    last = eng.matmul(psQ[0:1, :], ones_b,
                                      BA[:, cc * T + q * 512: cc * T + (q + 1) * 512],
                                      start=(cc == 0), stop=(cc == NCC - 1))
                MM.inc(last)
            return (MM, MM.n)
        ctx.sec("tensor", t3)

        def s4(eng):
            last = None
            for q in range(4):
                eng.wait_ge(MM.h, mm_q2 + q + 1)
                last = eng.activation(RW[96:97, q * 512:(q + 1) * 512], psQ[0:1, :],
                                      AF.Copy, scale=crow(S, 2, 96))
                SE.inc(last)
            eng.wait_ge(SE.h, SE.n)
            last = eng.activation(RW[64:65, 0:2048], RW[96:97, 0:2048], AF.Sqrt,
                                  bias=crow(S, 3, 64))
            SE.inc(last)
            return (SE, SE.n)
        ctx.sec("scalar", s4)
        if meta.get("p5_ln", 9) < 5:
            return

        se_s = SE.n

        def v2(eng):
            eng.wait_ge(SE.h, se_s)
            i = eng.reciprocal(RW[64:65, 0:2048], RW[64:65, 0:2048])
            VE.inc(i)
            return (VE, VE.n)
        ctx.sec("vector", v2)

        ve_rs = VE.n

        def t4(eng):
            eng.wait_ge(VE.h, ve_rs)
            last = None
            for q in range(4):
                last = eng.matmul(psREP[:, q * 512:(q + 1) * 512], onesF64,
                                  RW[64:65, q * 512:(q + 1) * 512], start=True, stop=True)
            MM.inc(last)
            return (MM, MM.n)
        ctx.sec("tensor", t4)

        mm_rsrep = MM.n

        def v3(eng):
            eng.wait_ge(MM.h, mm_rsrep)
            i = eng.tensor_mul(BCv, BCv, brc(psREP, 2048))
            VE.inc(i)
            return (VE, VE.n)
        ctx.sec("vector", v3)

        ve_norm = VE.n

        def s5(eng):
            eng.wait_ge(VE.h, ve_norm)
            last = None
            for cc in range(NCC):
                last = eng.activation(BA[:, cc * T:(cc + 1) * T], BC[:, cc * T:(cc + 1) * T],
                                      AF.Copy, scale=SM[:, 16 * cc + 3:16 * cc + 4])
                SE.inc(last)
            return (SE, SE.n)
        ctx.sec("scalar", s5)

        se_g = SE.n

        def v4(eng):
            eng.wait_ge(SE.h, se_g)
            last = None
            for cc in range(NCC):
                last = eng.tensor_scalar_add(BA[:, cc * T:(cc + 1) * T],
                                             BA[:, cc * T:(cc + 1) * T],
                                             SM[:, 16 * cc + 4:16 * cc + 5])
                VE.inc(last)
            return (VE, VE.n)
        ctx.sec("vector", v4)

        ctx.barrier()

    with nc.psum_tensor("psF1", [128, 1024], F32) as psF1, \
         nc.psum_tensor("psF2", [128, 1024], F32) as psF2:
        NE8 = 8
        mm0 = MM.n
        se0 = SE.n
        ve0 = VE.n
        wa0, wb0 = WA2.n, WB2.n

        def mm_counts(e):
            return mm0 + e * 48

        def gpW(eng):
            for e in range(NE8):
                if e >= 1:
                    eng.wait_ge(MM.h, mm_counts(e - 1) + 16)
                i = eng.dma_start(
                    out=_sl(W.ap(), 0, [[8192, 128], [512, 8], [1, 512]]),
                    in_=_sl(t["fc1T"], 512 * e, [[4096, 128], [4096 * 128, 8], [1, 512]]))
                WA2.inc(i, 16)
                if e >= 1:
                    eng.wait_ge(MM.h, mm_counts(e - 1) + 48)
                i = eng.dma_start(
                    out=_sl(W.ap(), 4096, [[8192, 128], [1024, 4], [1, 1024]]),
                    in_=_sl(t["fc2T"], 512 * e * 1024, [[1024, 128], [128 * 1024, 4], [1, 1024]]))
                WB2.inc(i, 16)
            return (WB2, WB2.n)
        ctx.sec("gpsimd", gpW)

        def tF(eng):
            last = None
            for e in range(NE8):
                eng.wait_ge(WA2.h, wa0 + 16 * (e + 1))
                for j, (tq, fj) in enumerate([(a, b) for a in range(4) for b in range(4)]):
                    gj = e * 16 + j
                    if gj >= 2:
                        eng.wait_ge(SE.h, se0 + gj - 1)
                    for cc in range(NCC):
                        last = eng.matmul(
                            psF1[:, (j % 2) * 512:(j % 2) * 512 + 512],
                            W[:, cc * 512 + fj * 128: cc * 512 + fj * 128 + 128],
                            BA[:, cc * T + tq * 512: cc * T + (tq + 1) * 512],
                            start=(cc == 0), stop=(cc == NCC - 1))
                    MM.inc(last)
                eng.wait_ge(WB2.h, wb0 + 16 * (e + 1))
                for j, (tq, cc) in enumerate([(a, b) for a in range(4) for b in range(NCC)]):
                    jj = e * 32 + j
                    if jj >= 2:
                        eng.wait_ge(VE.h, ve0 + jj - 1)
                    eng.wait_ge(SE.h, se0 + e * 16 + tq * 4 + 4)
                    for fj in range(4):
                        last = eng.matmul(
                            psF2[:, (j % 2) * 512:(j % 2) * 512 + 512],
                            W[:, 4096 + fj * 1024 + cc * 128: 4096 + fj * 1024 + cc * 128 + 128],
                            BB[:, fj * 2048 + tq * 512: fj * 2048 + (tq + 1) * 512],
                            start=(fj == 0), stop=(fj == 3))
                    MM.inc(last)
            return (MM, MM.n)
        ctx.sec("tensor", tF)

        def sF(eng):
            last = None
            for e in range(NE8):
                for j, (tq, fj) in enumerate([(a, b) for a in range(4) for b in range(4)]):
                    eng.wait_ge(MM.h, mm_counts(e) + j + 1)
                    fcg = 4 * e + fj
                    last = eng.activation(
                        BB[:, fj * 2048 + tq * 512: fj * 2048 + (tq + 1) * 512],
                        psF1[:, (j % 2) * 512:(j % 2) * 512 + 512],
                        AF.Gelu, bias=SM[:, 128 + 2 * fcg:128 + 2 * fcg + 1])
                    SE.inc(last)
            return (SE, SE.n)
        ctx.sec("scalar", sF)

        def vF(eng):
            last = None
            for e in range(NE8):
                for j, (tq, cc) in enumerate([(a, b) for a in range(4) for b in range(NCC)]):
                    eng.wait_ge(MM.h, mm_counts(e) + 16 + j + 1)
                    dst = R[:, cc * T + tq * 512: cc * T + (tq + 1) * 512]
                    if e == 0:
                        last = eng.scalar_tensor_tensor(
                            dst, psF2[:, (j % 2) * 512:(j % 2) * 512 + 512],
                            SM[:, 16 * cc + 5:16 * cc + 6], dst, OP.add, OP.add)
                    else:
                        last = eng.tensor_add(
                            dst, dst, psF2[:, (j % 2) * 512:(j % 2) * 512 + 512])
                    VE.inc(last)
            return (VE, VE.n)
        ctx.sec("vector", vF)

        ctx.barrier()

    def gpX(eng):
        i = eng.dma_start(out=_sl(R.ap(), 0, [[16384, 128], [T, NCC], [1, T]]),
                          in_=_sl(t["xT"], 0, [[T, 128], [128 * T, NCC], [1, T]]),
                          accum_op=OP.add)
        GD.inc(i, 16)
        return (GD, GD.n)
    ctx.sec("gpsimd", gpX)
    ctx.barrier()


def emit_p4(ctx, meta):
    nc, t, S = ctx.nc, ctx.t, ctx.s
    LD, GD, MM, VE, SE, CCS = (ctx.sm[k] for k in ("LD", "GD", "MM", "VE", "SE", "CCS"))
    TA, TB2 = ctx.sm["TA"], ctx.sm["TB"]
    WA2, WB2 = ctx.sm["WA2"], ctx.sm["WB2"]
    R, BA, BB, BC, W, CN, CB = (S[k] for k in ("R", "BA", "BB", "BC", "W", "CN", "CB"))
    NCH4 = meta["nchunk"]
    rowstart = meta["rowstart"]
    mm0 = MM.n
    ve0 = VE.n
    ld0 = LD.n
    mmT_pred = mm0 + 16 * NE * NCH4

    wdb = (2 * NE * NCH4 * 128 <= 8192)
    wsz = NE * NCH4 * 128

    def syS(eng):
        last = None
        per = NE * NCH4 * 128
        for ti in range(16):
            if wdb:
                if ti >= 2:
                    eng.wait_ge(MM.h, mm0 + (ti - 1) * NE * NCH4)
            elif ti >= 1:
                eng.wait_ge(MM.h, mm0 + ti * NE * NCH4)
            last = eng.dma_start(
                out=_sl(W.ap(), (ti % 2) * wsz if wdb else 0,
                        [[8192, 128], [128, NE * NCH4], [1, 128]]),
                in_=_sl(t["sel"], ti * per * 128,
                        [[NE * NCH4 * 128, 128], [128, NE * NCH4], [1, 128]]))
            (TA if ti % 2 == 0 else TB2).inc(last, 16)
        eng.wait_ge(TA.h, TA.n)
        return (TB2, TB2.n)
    ta0 = ctx.sm["TA"].n
    tb0 = ctx.sm["TB"].n
    ctx.sec("sync", syS)

    wa0, wb0 = WA2.n, WB2.n

    def gpS(eng):
        eng.wait_ge(CCS.h, CCS.n)
        last = None
        for ti in range(16):
            for e in range(NE):
                for ch in range(NCH4):
                    j = (ti * NE + e) * NCH4 + ch
                    if j >= 2:
                        eng.wait_ge(MM.h, mm0 + j - 1)
                    r0 = e * 2048 + int(rowstart[e][ti][ch])
                    last = eng.dma_start(
                        out=_sl(BC.ap(), (j % 2) * 1024, [[16384, 128], [1, 1024]]),
                        in_=t["ag_out"][r0:r0 + 128, :])
                    (WA2 if j % 2 == 0 else WB2).inc(last, 16)
        return (WA2 if j % 2 == 0 else WB2,
                (WA2 if j % 2 == 0 else WB2).n)
    ctx.sec("gpsimd", gpS)

    with nc.psum_tensor("psC4", [128, 2048], F32) as psC, \
         nc.psum_tensor("psT4", [128, 2048], BF16) as psT4:

        def tS(eng):
            last = None
            for ti in range(16):
                sem = TA if ti % 2 == 0 else TB2
                base = ta0 if ti % 2 == 0 else tb0
                eng.wait_ge(sem.h, base + 16 * (ti // 2 + 1))
                if ti >= 2:
                    eng.wait_ge(VE.h, ve0 + ti - 1)
                for e in range(NE):
                    for ch in range(NCH4):
                        j = (ti * NE + e) * NCH4 + ch
                        eng.wait_ge((WA2 if j % 2 == 0 else WB2).h,
                                    (wa0 if j % 2 == 0 else wb0) + 16 * (j // 2 + 1))
                        wof = (ti % 2) * wsz if wdb else 0
                        for hv in range(2):
                            last = eng.matmul(
                                psC[:, (ti % 2) * 1024 + hv * 512:(ti % 2) * 1024 + (hv + 1) * 512],
                                W[:, wof + ((e * NCH4) + ch) * 128:wof + ((e * NCH4) + ch) * 128 + 128],
                                _sl(BC.ap(), (j % 2) * 1024 + hv * 512, [[16384, 128], [1, 512]]),
                                start=(e == 0 and ch == 0), stop=(e == NE - 1 and ch == NCH4 - 1))
                        MM.inc(last)
            return (MM, MM.n)
        ctx.sec("tensor", tS)

        def vS(eng):
            last = None
            for ti in range(16):
                eng.wait_ge(MM.h, mm0 + (ti * NE + NE - 1) * NCH4 + NCH4)
                last = eng.tensor_copy(
                    out=_sl(BB.ap(), ti * 1024, [[17664, 128], [1, 1024]]),
                    in_=psC[:, (ti % 2) * 1024:(ti % 2) * 1024 + 1024])
                VE.inc(last)
            return (VE, VE.n)
        ctx.sec("vector", vS)

        mmT = MM.n
        veT = VE.n

        def tT(eng):
            last = None
            for ti in range(16):
                eng.wait_ge(VE.h, ve0 + ti + 1)
                for cc in range(NCC):
                    j = ti * NCC + cc
                    if j >= 2:
                        eng.wait_ge(VE.h, veT + j - 1)
                    last = eng.transpose(
                        psT4[:, (j % 2) * 1024:(j % 2) * 1024 + 128],
                        _sl(BB.ap(), ti * 1024 + cc * 128, [[17664, 128], [1, 128]]),
                        CB[:, 0:128])
                    MM.inc(last)
            return (MM, MM.n)
        ctx.sec("tensor", tT)

        def vT(eng):
            last = None
            for ti in range(16):
                for cc in range(NCC):
                    j = ti * NCC + cc
                    eng.wait_ge(MM.h, mmT + j + 1)
                    dst = R[:, cc * T + ti * 128: cc * T + ti * 128 + 128]
                    last = eng.tensor_add(dst, dst,
                                          psT4[:, (j % 2) * 1024:(j % 2) * 1024 + 128])
                    VE.inc(last)
            return (VE, VE.n)
        ctx.sec("vector", vT)
    ctx.barrier()


def emit_p5(ctx, meta):
    nc, t, S = ctx.nc, ctx.t, ctx.s
    LD, GD, MM, VE, SE = (ctx.sm[k] for k in ("LD", "GD", "MM", "VE", "SE"))
    WA2, WB2 = ctx.sm["WA2"], ctx.sm["WB2"]
    R, BA, BB, BC, W, RW, CN, CB, SM = (
        S[k] for k in ("R", "BA", "BB", "BC", "W", "RW", "CN", "CB", "SM"))
    ones_b = CB[:, 128:129]
    onesF32 = CN[32:33, 140:268]
    onesF64 = CN[64:65, 140:268]
    Rv = _sl(R.ap(), 0, [[16384, 128], [T, NCC], [1, T]])
    BAv = _sl(BA.ap(), 0, [[16384, 128], [T, NCC], [1, T]])
    BCv = _sl(BC.ap(), 0, [[16384, 128], [T, NCC], [1, T]])
    Rb = R.ap().bitcast(BF16)

    def brc(ps, w):
        return _sl(ps.ap(), 0, [[2048, 128], [0, NCC], [1, w]])

    with nc.psum_tensor("psQ5", [1, 512], F32) as psQ, \
         nc.psum_tensor("psREP5", [128, 2048], F32) as psREP:

        def s1(eng):
            i = eng.copy(out=BC[:, :], in_=R[:, :])
            SE.inc(i)
            return (SE, SE.n)
        ctx.sec("scalar", s1)
        if meta.get("p5_ln", 9) < 2:
            return

        se_cast = SE.n
        mm_q = MM.n
        se_q = SE.n

        def t1(eng):
            eng.wait_ge(SE.h, se_cast)
            last = None
            for q in range(4):
                if q >= 1:
                    eng.wait_ge(SE.h, se_q + q)
                for cc in range(NCC):
                    last = eng.matmul(psQ[0:1, :], ones_b,
                                      BC[:, cc * T + q * 512: cc * T + (q + 1) * 512],
                                      start=(cc == 0), stop=(cc == NCC - 1))
                MM.inc(last)
            return (MM, MM.n)
        ctx.sec("tensor", t1)

        def s2(eng):
            last = None
            for q in range(4):
                eng.wait_ge(MM.h, mm_q + q + 1)
                last = eng.activation(RW[32:33, q * 512:(q + 1) * 512], psQ[0:1, :],
                                      AF.Copy, scale=crow(S, 2, 32))
                SE.inc(last)
            return (SE, SE.n)
        ctx.sec("scalar", s2)
        if meta.get("p5_ln", 9) < 3:
            return

        se_mu = SE.n

        def t2(eng):
            eng.wait_ge(SE.h, se_mu)
            last = None
            for q in range(4):
                last = eng.matmul(psREP[:, q * 512:(q + 1) * 512], onesF32,
                                  RW[32:33, q * 512:(q + 1) * 512], start=True, stop=True)
            MM.inc(last)
            return (MM, MM.n)
        ctx.sec("tensor", t2)

        mm_murep = MM.n

        def v1(eng):
            eng.wait_ge(MM.h, mm_murep)
            i = eng.tensor_sub(BCv, Rv, brc(psREP, 2048))
            VE.inc(i)
            return (VE, VE.n)
        ctx.sec("vector", v1)
        if meta.get("p5_ln", 9) < 4:
            return

        ve_xc = VE.n

        def s3(eng):
            eng.wait_ge(VE.h, ve_xc)
            i = eng.activation(BA[:, :], BC[:, :], AF.Square)
            SE.inc(i)
            return (SE, SE.n)
        ctx.sec("scalar", s3)

        se_sq = SE.n
        mm_q2 = MM.n
        se_q2 = SE.n

        def t3(eng):
            eng.wait_ge(SE.h, se_sq)
            last = None
            for q in range(4):
                if q >= 1:
                    eng.wait_ge(SE.h, se_q2 + q)
                for cc in range(NCC):
                    last = eng.matmul(psQ[0:1, :], ones_b,
                                      BA[:, cc * T + q * 512: cc * T + (q + 1) * 512],
                                      start=(cc == 0), stop=(cc == NCC - 1))
                MM.inc(last)
            return (MM, MM.n)
        ctx.sec("tensor", t3)

        def s4(eng):
            last = None
            for q in range(4):
                eng.wait_ge(MM.h, mm_q2 + q + 1)
                last = eng.activation(RW[96:97, q * 512:(q + 1) * 512], psQ[0:1, :],
                                      AF.Copy, scale=crow(S, 2, 96))
                SE.inc(last)
            eng.wait_ge(SE.h, SE.n)
            last = eng.activation(RW[64:65, 0:2048], RW[96:97, 0:2048], AF.Sqrt,
                                  bias=crow(S, 3, 64))
            SE.inc(last)
            return (SE, SE.n)
        ctx.sec("scalar", s4)
        if meta.get("p5_ln", 9) < 5:
            return

        se_s = SE.n

        def v2(eng):
            eng.wait_ge(SE.h, se_s)
            i = eng.reciprocal(RW[64:65, 0:2048], RW[64:65, 0:2048])
            VE.inc(i)
            return (VE, VE.n)
        ctx.sec("vector", v2)

        ve_rs = VE.n

        def t4(eng):
            eng.wait_ge(VE.h, ve_rs)
            last = None
            for q in range(4):
                last = eng.matmul(psREP[:, q * 512:(q + 1) * 512], onesF64,
                                  RW[64:65, q * 512:(q + 1) * 512], start=True, stop=True)
            MM.inc(last)
            return (MM, MM.n)
        ctx.sec("tensor", t4)

        mm_rsrep = MM.n

        def v3(eng):
            eng.wait_ge(MM.h, mm_rsrep)
            i = eng.tensor_mul(BAv, BCv, brc(psREP, 2048))
            VE.inc(i)
            return (VE, VE.n)
        ctx.sec("vector", v3)
        if meta.get("p5_ln", 9) < 6:
            return

        def syS(eng):
            i = eng.dma_start(
                out=_sl(t["xspill"], 0, [[T, 128], [128 * T, NCC], [1, T]]),
                in_=_sl(R.ap(), 0, [[16384, 128], [T, NCC], [1, T]]))
            LD.inc(i, 16)
            return (LD, LD.n)
        ctx.sec("sync", syS)
        ctx.barrier()
        if meta.get("p5_stage", 99) < 2:
            return

    def proj(eng, src, psum, evac_engine_jobs, mm_base, ve_or_se, dst_cb):
        pass

    with nc.psum_tensor("psP5", [128, 1024], F32) as psF:
        def vxr(eng):
            last = None
            for cc in range(NCC):
                eng.tensor_sub(BC[:, cc * T + 1: cc * T + T],
                               BA[:, cc * T + 1: cc * T + T],
                               BA[:, cc * T: cc * T + T - 1])
                last = eng.tensor_copy(out=BC[:, cc * T: cc * T + 1],
                                       in_=BA[:, cc * T: cc * T + 1])
                VE.inc(last)
            return (VE, VE.n)
        ctx.sec("vector", vxr)

        ve_d = VE.n

        def sxr(eng):
            eng.wait_ge(VE.h, ve_d)
            last = None
            for cc in range(NCC):
                last = eng.activation(BC[:, cc * T:(cc + 1) * T], BC[:, cc * T:(cc + 1) * T],
                                      AF.Copy, scale=SM[:, 16 * cc + 6:16 * cc + 7])
                SE.inc(last)
            return (SE, SE.n)
        ctx.sec("scalar", sxr)

        se_m = SE.n

        def vxr2(eng):
            eng.wait_ge(SE.h, se_m)
            last = None
            for cc in range(NCC):
                last = eng.tensor_add(BC[:, cc * T + 1: cc * T + T],
                                      BC[:, cc * T + 1: cc * T + T],
                                      BA[:, cc * T: cc * T + T - 1])
                VE.inc(last)
            return (VE, VE.n)
        ctx.sec("vector", vxr2)
        ctx.barrier()

        def gpw1(eng):
            i = eng.dma_start(out=_sl(W.ap(), 0, [[8192, 128], [1024, 8], [1, 1024]]),
                              in_=_sl(t["WrT"], 0, [[1024, 128], [128 * 1024, 8], [1, 1024]]))
            WA2.inc(i, 16)
            return (WA2, WA2.n)
        ctx.sec("gpsimd", gpw1)

        mm_r = MM.n
        se_r = SE.n
        wa_r = WA2.n

        def tr(eng):
            eng.wait_ge(WA2.h, wa_r)
            last = None
            for j, (dc, tq) in enumerate([(a, b) for a in range(NCC) for b in range(4)]):
                if j >= 2:
                    eng.wait_ge(SE.h, se_r + j - 1)
                for cc in range(NCC):
                    last = eng.matmul(psF[:, (j % 2) * 512:(j % 2) * 512 + 512],
                                      W[:, cc * 1024 + dc * 128: cc * 1024 + dc * 128 + 128],
                                      BC[:, cc * T + tq * 512: cc * T + (tq + 1) * 512],
                                      start=(cc == 0), stop=(cc == NCC - 1))
                MM.inc(last)
            return (MM, MM.n)
        ctx.sec("tensor", tr)

        def sr(eng):
            last = None
            for j, (dc, tq) in enumerate([(a, b) for a in range(NCC) for b in range(4)]):
                eng.wait_ge(MM.h, mm_r + j + 1)
                last = eng.activation(BB[:, dc * T + tq * 512: dc * T + (tq + 1) * 512],
                                      psF[:, (j % 2) * 512:(j % 2) * 512 + 512],
                                      AF.Sigmoid)
                SE.inc(last)
            return (SE, SE.n)
        ctx.sec("scalar", sr)
        ctx.barrier()

        def vxk(eng):
            last = None
            for cc in range(NCC):
                eng.tensor_sub(BC[:, cc * T + 1: cc * T + T],
                               BA[:, cc * T + 1: cc * T + T],
                               BA[:, cc * T: cc * T + T - 1])
                last = eng.tensor_copy(out=BC[:, cc * T: cc * T + 1],
                                       in_=BA[:, cc * T: cc * T + 1])
                VE.inc(last)
            return (VE, VE.n)
        ctx.sec("vector", vxk)

        ve_d2 = VE.n

        def sxk(eng):
            eng.wait_ge(VE.h, ve_d2)
            last = None
            for cc in range(NCC):
                last = eng.activation(BC[:, cc * T:(cc + 1) * T], BC[:, cc * T:(cc + 1) * T],
                                      AF.Copy, scale=SM[:, 16 * cc + 7:16 * cc + 8])
                SE.inc(last)
            return (SE, SE.n)
        ctx.sec("scalar", sxk)

        se_m2 = SE.n

        def vxk2(eng):
            eng.wait_ge(SE.h, se_m2)
            last = None
            for cc in range(NCC):
                last = eng.tensor_add(BC[:, cc * T + 1: cc * T + T],
                                      BC[:, cc * T + 1: cc * T + T],
                                      BA[:, cc * T: cc * T + T - 1])
                VE.inc(last)
            return (VE, VE.n)
        ctx.sec("vector", vxk2)
        ctx.barrier()

        def gpw2(eng):
            i = eng.dma_start(out=_sl(W.ap(), 0, [[8192, 128], [1024, 8], [1, 1024]]),
                              in_=_sl(t["WkkT"], 0, [[1024, 128], [128 * 1024, 8], [1, 1024]]))
            WA2.inc(i, 16)
            return (WA2, WA2.n)
        ctx.sec("gpsimd", gpw2)

        mm_k = MM.n
        ve_k = VE.n
        se_k = SE.n
        wa_k = WA2.n

        def tk(eng):
            eng.wait_ge(WA2.h, wa_k)
            last = None
            for j, (dc, tq) in enumerate([(a, b) for a in range(NCC) for b in range(4)]):
                if j >= 2:
                    eng.wait_ge(VE.h, ve_k + j - 1)
                for cc in range(NCC):
                    last = eng.matmul(psF[:, (j % 2) * 512:(j % 2) * 512 + 512],
                                      W[:, cc * 1024 + dc * 128: cc * 1024 + dc * 128 + 128],
                                      BC[:, cc * T + tq * 512: cc * T + (tq + 1) * 512],
                                      start=(cc == 0), stop=(cc == NCC - 1))
                MM.inc(last)
            return (MM, MM.n)
        ctx.sec("tensor", tk)

        def vk(eng):
            last = None
            for j, (dc, tq) in enumerate([(a, b) for a in range(NCC) for b in range(4)]):
                eng.wait_ge(MM.h, mm_k + j + 1)
                last = eng.tensor_scalar_max(
                    _sl(Rb, dc * T + tq * 512, [[32768, 128], [1, 512]]),
                    psF[:, (j % 2) * 512:(j % 2) * 512 + 512], 0.0)
                VE.inc(last)
            return (VE, VE.n)
        ctx.sec("vector", vk)

        ve_relu = VE.n

        def sk(eng):
            eng.wait_ge(VE.h, ve_relu)
            last = None
            for dc in range(NCC):
                last = eng.activation(_sl(Rb, dc * T, [[32768, 128], [1, T]]),
                                      _sl(Rb, dc * T, [[32768, 128], [1, T]]),
                                      AF.Square)
                SE.inc(last)
            return (SE, SE.n)
        ctx.sec("scalar", sk)
        ctx.barrier()

        def gpw3(eng):
            i = eng.dma_start(out=_sl(W.ap(), 0, [[8192, 128], [1024, 8], [1, 1024]]),
                              in_=_sl(t["WvvT"], 0, [[1024, 128], [128 * 1024, 8], [1, 1024]]))
            WA2.inc(i, 16)
            return (WA2, WA2.n)
        ctx.sec("gpsimd", gpw3)

        mm_v = MM.n
        ve_v = VE.n
        wa_v = WA2.n

        def tv(eng):
            eng.wait_ge(WA2.h, wa_v)
            last = None
            for j, (dc, tq) in enumerate([(a, b) for a in range(NCC) for b in range(4)]):
                if j >= 2:
                    eng.wait_ge(VE.h, ve_v + j - 1)
                for cc in range(NCC):
                    last = eng.matmul(psF[:, (j % 2) * 512:(j % 2) * 512 + 512],
                                      W[:, cc * 1024 + dc * 128: cc * 1024 + dc * 128 + 128],
                                      _sl(Rb, cc * T + tq * 512, [[32768, 128], [1, 512]]),
                                      start=(cc == 0), stop=(cc == NCC - 1))
                MM.inc(last)
            return (MM, MM.n)
        ctx.sec("tensor", tv)

        def vv(eng):
            last = None
            for j, (dc, tq) in enumerate([(a, b) for a in range(NCC) for b in range(4)]):
                eng.wait_ge(MM.h, mm_v + j + 1)
                last = eng.tensor_mul(
                    _sl(Rb, 16384 + dc * T + tq * 512, [[32768, 128], [1, 512]]),
                    psF[:, (j % 2) * 512:(j % 2) * 512 + 512],
                    BB[:, dc * T + tq * 512: dc * T + (tq + 1) * 512])
                VE.inc(last)
            return (VE, VE.n)
        ctx.sec("vector", vv)
        ctx.barrier()

        def gpacc(eng):
            i = eng.dma_start(
                out=_sl(t["xspill"], 0, [[T, 128], [128 * T, NCC], [1, T]]),
                in_=_sl(Rb, 16384, [[32768, 128], [T, NCC], [1, T]]),
                accum_op=OP.add)
            GD.inc(i, 16)
            return (GD, GD.n)
        ctx.sec("gpsimd", gpacc)
    ctx.barrier()

    if meta.get("p5_stage", 99) < 3:
        return
    emit_p5_attn(ctx, meta)


def emit_p5_attn(ctx, meta):
    nc, t, S = ctx.nc, ctx.t, ctx.s
    LD, GD, MM, VE, SE = (ctx.sm[k] for k in ("LD", "GD", "MM", "VE", "SE"))
    WA2, WB2 = ctx.sm["WA2"], ctx.sm["WB2"]
    R, BA, BB, BC, W, RW, CN, CB, SM = (
        S[k] for k in ("R", "BA", "BB", "BC", "W", "RW", "CN", "CB", "SM"))
    Rb = R.ap().bitcast(BF16)
    onesF32 = CN[32:33, 140:268]
    V0 = 1040
    STG = 192
    TRI = 2240
    TRIH = 2368
    ONER = 2880
    PREF = 3392
    HFC = 3522
    RZC = 16640

    with nc.psum_tensor("psA5", [128, 2048], F32) as psF:
        mm0 = MM.n
        se0 = SE.n
        ve0 = VE.n
        wa0 = WA2.n

        def gpw(eng):
            for wi, wn in enumerate(["WqT", "WkT", "WvT"]):
                if wi >= 1:
                    eng.wait_ge(MM.h, mm0 + wi * 32)
                i = eng.dma_start(
                    out=_sl(W.ap(), 0, [[8192, 128], [1024, 8], [1, 1024]]),
                    in_=_sl(t[wn], 0, [[1024, 128], [128 * 1024, 8], [1, 1024]]))
                WA2.inc(i, 16)
            return (WA2, WA2.n)
        ctx.sec("gpsimd", gpw)

        def tqk(eng):
            last = None
            for wi in range(2):
                eng.wait_ge(WA2.h, wa0 + 16 * (wi + 1))
                for j, (dc, tq) in enumerate([(a_, b_) for a_ in range(NCC) for b_ in range(4)]):
                    gj = wi * 32 + j
                    if gj >= 2:
                        eng.wait_ge(SE.h, se0 + gj - 1)
                    for cc in range(NCC):
                        last = eng.matmul(psF[:, (j % 2) * 512:(j % 2) * 512 + 512],
                                          W[:, cc * 1024 + dc * 128: cc * 1024 + dc * 128 + 128],
                                          BA[:, cc * T + tq * 512: cc * T + (tq + 1) * 512],
                                          start=(cc == 0), stop=(cc == NCC - 1))
                    MM.inc(last)
            eng.wait_ge(WA2.h, wa0 + 48)
            for kt in range(16):
                if kt >= 2:
                    eng.wait_ge(SE.h, se0 + 64 + kt - 1)
                else:
                    eng.wait_ge(SE.h, se0 + 64)
                for cc in range(NCC):
                    for hv in range(2):
                        last = eng.matmul(
                            psF[:, (kt % 2) * 1024 + hv * 512:(kt % 2) * 1024 + (hv + 1) * 512],
                            BA[:, cc * T + kt * 128: cc * T + kt * 128 + 128],
                            W[:, cc * 1024 + hv * 512: cc * 1024 + (hv + 1) * 512],
                            start=(cc == 0), stop=(cc == NCC - 1))
                MM.inc(last)
            return (MM, MM.n)
        ctx.sec("tensor", tqk)

        def sqk(eng):
            last = None
            for wi in range(2):
                for j, (dc, tq) in enumerate([(a_, b_) for a_ in range(NCC) for b_ in range(4)]):
                    gj = wi * 32 + j
                    eng.wait_ge(MM.h, mm0 + gj + 1)
                    srcp = psF[:, (j % 2) * 512:(j % 2) * 512 + 512]
                    if wi == 0:
                        last = eng.activation(
                            BC[:, dc * T + tq * 512: dc * T + (tq + 1) * 512],
                            srcp, AF.Copy, scale=crow(S, 7, 0, 128))
                    else:
                        last = eng.activation(
                            _sl(Rb, dc * T + tq * 512, [[32768, 128], [1, 512]]),
                            srcp, AF.Copy)
                    SE.inc(last)
            for kt in range(16):
                eng.wait_ge(MM.h, mm0 + 64 + kt + 1)
                last = eng.activation(
                    _sl(BB.ap(), kt * V0, [[17664, 128], [65, 16], [1, 64]]),
                    _sl(psF.ap(), (kt % 2) * 1024, [[2048, 128], [64, 16], [1, 64]]),
                    AF.Copy)
                SE.inc(last)
            return (SE, SE.n)
        ctx.sec("scalar", sqk)

        def vpre(eng):
            last = None
            for kt in range(16):
                last = eng.memset(_sl(BB.ap(), kt * V0 + 64, [[17664, 128], [65, 16], [1, 1]]), 1.0)
            eng.memset(CB[0:1, ONER:ONER + 512], 1.0)
            last = eng.memset(CB[:, HFC:HFC + 1], 0.5)
            VE.inc(last)
            return (VE, VE.n)
        ctx.sec("vector", vpre)

        def spre(eng):
            i = eng.activation(CB[:, TRIH:TRIH + 128], CB[:, TRI:TRI + 128],
                               AF.Copy, scale=crow(S, 4, 0, 128))
            SE.inc(i)
            return (SE, SE.n)
        ctx.sec("scalar", spre)

        def vpre2(eng):
            i = eng.memset(CB[:, TRIH + 128:TRIH + 512], 0.5)
            VE.inc(i)
            return (VE, VE.n)
        ctx.sec("vector", vpre2)
        ctx.barrier()
    if meta.get("p5_stage", 99) < 4:
        return

    def build_sched():
        ev = []
        mm = se = ve = 0
        ktg = 0
        st_se_hist = {}
        av_mm_hist = {}
        for hp in range(8):
            for qc in range(4):
                e = dict(hp=hp, qc=qc, kts=[])
                if qc >= 1:
                    mm += 1; e["mm_psP"] = mm
                    ve += 1; e["ve_pref"] = ve
                for kt in range(4 * qc + 4):
                    w0 = max(0, kt * 128 - qc * 512)
                    d = kt >= 4 * qc
                    mm += 1; st_mm = mm
                    se += 1; st_se = se
                    tri_ve = None
                    if d:
                        ve += 1; tri_ve = ve
                    mm += 1; av_mm = mm
                    e["kts"].append(dict(kt=kt, ktg=ktg, w0=w0, d=d, st_mm=st_mm,
                                         st_se=st_se, tri_ve=tri_ve, av_mm=av_mm))
                    st_se_hist[ktg] = st_se
                    av_mm_hist[ktg] = av_mm
                    ktg += 1
                ve += 2; e["ve_rz"] = ve
                mm += 1; e["mm_rz"] = mm
                ve += 1; e["ve_rzc"] = ve
                ve += 1; e["ve_on"] = ve
                ev.append(e)
        return ev, st_se_hist, av_mm_hist

    sched, st_se_hist, av_mm_hist = build_sched()
    with nc.psum_tensor("psATT", [128, 3072], F32) as psA, \
         nc.psum_tensor("psO5", [128, 1024], F32) as psO:
        mmA = MM.n
        seA = SE.n
        veA = VE.n

        def slotS(ktg, hh):
            return (ktg % 2) * 1024 + hh * 512

        def stg(ktg, hh):
            return STG + ((ktg % 2) * 2 + hh) * 512

        def tATT(eng):
            last = None
            prev_pref_ve = None
            prev_on_ve = None
            prev_rzc_ve = None
            for e in sched:
                hp, qc = e["hp"], e["qc"]
                if qc >= 1:
                    if prev_pref_ve is not None:
                        eng.wait_ge(VE.h, veA + prev_pref_ve)
                    if prev_rzc_ve is not None:
                        eng.wait_ge(VE.h, veA + prev_rzc_ve)
                    for hh in range(2):
                        for kt in range(4 * qc):
                            last = eng.matmul(
                                psA[0:1, 2048 + hh * 512:2048 + hh * 512 + 65],
                                CB[:, HFC:HFC + 1],
                                _sl(BB.ap(), kt * V0 + (2 * hp + hh) * 65,
                                    [[17664, 128], [1, 65]]),
                                start=(kt == 0), stop=(kt == 4 * qc - 1))
                    MM.inc(last)
                    prev_pref_ve = e["ve_pref"]
                first = [True, True]
                if qc >= 1:
                    eng.wait_ge(VE.h, veA + e["ve_pref"])
                    if prev_on_ve is not None:
                        eng.wait_ge(VE.h, veA + prev_on_ve)
                    for hh in range(2):
                        last = eng.matmul(
                            psO[0:65, hh * 512 + 0:hh * 512 + 512],
                            CB[0:1, PREF + hh * 65:PREF + hh * 65 + 65],
                            CB[0:1, ONER:ONER + 512],
                            start=True, stop=False)
                        first[hh] = False
                elif prev_on_ve is not None:
                    eng.wait_ge(VE.h, veA + prev_on_ve)
                nkt = len(e["kts"])
                for ek in e["kts"]:
                    kt, ktg, w0, d = ek["kt"], ek["ktg"], ek["w0"], ek["d"]
                    ww = 512 - w0
                    if ktg >= 2:
                        eng.wait_ge(SE.h, seA + st_se_hist[ktg - 2])
                    for hh in range(2):
                        last = eng.matmul(
                            psA[:, slotS(ktg, hh) + w0:slotS(ktg, hh) + 512],
                            _sl(Rb, hp * T + kt * 128 + (hh * 64) * 32768,
                                [[32768, 64], [1, 128]]),
                            _sl(BC.ap(), hp * T + qc * 512 + w0 + (hh * 64) * 16384,
                                [[16384, 64], [1, ww]]),
                            tile_position=(hh * 64, 0),
                            start=True, stop=True)
                    MM.inc(last)
                    eng.wait_ge(SE.h, seA + ek["st_se"])
                    if d:
                        eng.wait_ge(VE.h, veA + ek["tri_ve"])
                    laststop = (kt == nkt - 1)
                    for hh in range(2):
                        vsl = _sl(BB.ap(), kt * V0 + (2 * hp + hh) * 65,
                                  [[17664, 128], [1, 65]])
                        last = eng.matmul(
                            psO[0:65, hh * 512 + w0:hh * 512 + 512],
                            vsl, CB[:, stg(ktg, hh) + w0:stg(ktg, hh) + 512],
                            start=first[hh], stop=laststop and not d)
                        first[hh] = False
                        if d:
                            last = eng.matmul(
                                psO[0:65, hh * 512 + w0:hh * 512 + 512],
                                vsl, CB[:, TRIH:TRIH + (512 - w0)],
                                start=False, stop=laststop)
                    MM.inc(last)
                eng.wait_ge(VE.h, veA + e["ve_rz"])
                for hv in range(2):
                    last = eng.matmul(psA[0:64, 2048 + hv * 512:2048 + (hv + 1) * 512],
                                      onesF32[0:1, 0:64],
                                      RW[32:33, 512 + hv * 512:512 + (hv + 1) * 512],
                                      start=True, stop=True)
                MM.inc(last)
                prev_rzc_ve = e["ve_rzc"]
            return (MM, MM.n)
        ctx.sec("tensor", tATT)

        def sATT(eng):
            last = None
            for e in sched:
                for ek in e["kts"]:
                    ktg, w0 = ek["ktg"], ek["w0"]
                    eng.wait_ge(MM.h, mmA + ek["st_mm"])
                    if ktg >= 2:
                        eng.wait_ge(MM.h, mmA + av_mm_hist[ktg - 2])
                    for hh in range(2):
                        last = eng.activation(
                            CB[:, stg(ktg, hh) + w0:stg(ktg, hh) + 512],
                            psA[:, slotS(ktg, hh) + w0:slotS(ktg, hh) + 512],
                            AF.Square, scale=crow(S, 5, 0, 128),
                            bias=crow(S, 5, 0, 128))
                    SE.inc(last)
            return (SE, SE.n)
        ctx.sec("scalar", sATT)

        def vATT(eng):
            last = None
            for e in sched:
                hp, qc = e["hp"], e["qc"]
                if qc >= 1:
                    eng.wait_ge(MM.h, mmA + e["mm_psP"])
                    eng.tensor_copy(out=CB[0:1, PREF:PREF + 65],
                                    in_=psA[0:1, 2048:2048 + 65])
                    last = eng.tensor_copy(out=CB[0:1, PREF + 65:PREF + 130],
                                           in_=psA[0:1, 2048 + 512:2048 + 512 + 65])
                    VE.inc(last)
                for ek in e["kts"]:
                    if ek["d"]:
                        ktg, w0 = ek["ktg"], ek["w0"]
                        eng.wait_ge(SE.h, seA + ek["st_se"])
                        for hh in range(2):
                            sl_ = CB[:, stg(ktg, hh) + w0:stg(ktg, hh) + w0 + 128]
                            last = eng.tensor_mul(sl_, sl_, CB[:, TRI:TRI + 128])
                        VE.inc(last)
                eng.wait_ge(MM.h, mmA + e["kts"][-1]["av_mm"])
                i = eng.tensor_scalar(RW[32:33, 512:1536], psO[64:65, 0:1024],
                                      ATT_EPS, None, OP.add)
                ctx.chain(eng, VE, i)
                i = eng.reciprocal(RW[32:33, 512:1536], RW[32:33, 512:1536])
                VE.inc(i)
                eng.wait_ge(MM.h, mmA + e["mm_rz"])
                i = eng.tensor_copy(out=BB[0:64, RZC:RZC + 1024],
                                    in_=psA[0:64, 2048:3072])
                ctx.chain(eng, VE, i)
                for hh in range(2):
                    last = eng.tensor_mul(
                        _sl(BA.ap(), hp * T + qc * 512 + (hh * 64) * 16384,
                            [[16384, 64], [1, 512]]),
                        psO[0:64, hh * 512:hh * 512 + 512],
                        BB[0:64, RZC + hh * 512:RZC + hh * 512 + 512])
                VE.inc(last)
            return (VE, VE.n)
        ctx.sec("vector", vATT)
        ctx.barrier()
    if meta.get("p5_stage", 99) < 5:
        return

    with nc.psum_tensor("psW5", [128, 1024], F32) as psF:
        def gpwo(eng):
            i = eng.dma_start(
                out=_sl(W.ap(), 0, [[8192, 128], [1024, 8], [1, 1024]]),
                in_=_sl(t["WoT"], 0, [[1024, 128], [128 * 1024, 8], [1, 1024]]))
            WA2.inc(i, 16)
            return (WA2, WA2.n)
        ctx.sec("gpsimd", gpwo)

        mmW = MM.n
        seW = SE.n
        waW = WA2.n

        def two(eng):
            eng.wait_ge(WA2.h, waW)
            last = None
            for j, (dc, tq) in enumerate([(a_, b_) for a_ in range(NCC) for b_ in range(4)]):
                if j >= 2:
                    eng.wait_ge(SE.h, seW + j - 1)
                for cc in range(NCC):
                    last = eng.matmul(psF[:, (j % 2) * 512:(j % 2) * 512 + 512],
                                      W[:, cc * 1024 + dc * 128: cc * 1024 + dc * 128 + 128],
                                      BA[:, cc * T + tq * 512: cc * T + (tq + 1) * 512],
                                      start=(cc == 0), stop=(cc == NCC - 1))
                MM.inc(last)
            return (MM, MM.n)
        ctx.sec("tensor", two)

        def swo(eng):
            last = None
            for j, (dc, tq) in enumerate([(a_, b_) for a_ in range(NCC) for b_ in range(4)]):
                eng.wait_ge(MM.h, mmW + j + 1)
                last = eng.activation(BC[:, dc * T + tq * 512: dc * T + (tq + 1) * 512],
                                      psF[:, (j % 2) * 512:(j % 2) * 512 + 512], AF.Copy)
                SE.inc(last)
            return (SE, SE.n)
        ctx.sec("scalar", swo)
        ctx.barrier()

    def syR(eng):
        i = eng.dma_start(out=_sl(R.ap(), 0, [[16384, 128], [T, NCC], [1, T]]),
                          in_=_sl(t["xspill"], 0, [[T, 128], [128 * T, NCC], [1, T]]))
        LD.inc(i, 16)
        return (LD, LD.n)
    ctx.sec("sync", syR)
    ctx.barrier()

    def vfin(eng):
        last = None
        for cc in range(NCC):
            last = eng.tensor_add(R[:, cc * T:(cc + 1) * T], R[:, cc * T:(cc + 1) * T],
                                  BC[:, cc * T:(cc + 1) * T])
            VE.inc(last)
        return (VE, VE.n)
    ctx.sec("vector", vfin)
    ctx.barrier()


def emit_p2(ctx, meta):
    nc, t, S = ctx.nc, ctx.t, ctx.s
    LD, GD, MM, VE, SE, TZ = (ctx.sm[k] for k in ("LD", "GD", "MM", "VE", "SE", "TZ"))
    R, BA, BB, BC, W, RW, SC, CN, CB, SM = (
        S[k] for k in ("R", "BA", "BB", "BC", "W", "RW", "SC", "CN", "CB", "SM"))
    ones_b = CB[:, 128:129]
    onesF = CN[0:1, 140:268]
    onesF32 = CN[32:33, 140:268]
    onesF64 = CN[64:65, 140:268]

    Rv = _sl(R.ap(), 0, [[16384, 128], [T, NCC], [1, T]])
    BAv = _sl(BA.ap(), 0, [[16384, 128], [T, NCC], [1, T]])

    def brc(ps, w):
        return _sl(ps.ap(), 0, [[2048, 128], [0, NCC], [1, w]])

    def sy0(eng):
        i = eng.dma_start(out=_sl(R.ap(), 0, [[16384, 128], [T, NCC], [1, T]]),
                          in_=_sl(t["xT"], 0, [[T, 128], [128 * T, NCC], [1, T]]))
        LD.inc(i, 16)
        return (LD, LD.n)
    ctx.sec("sync", sy0)

    def gp0(eng):
        i = eng.dma_start(out=BC[0:64, 0:1024], in_=t["mixT"][:, :])
        GD.inc(i, 16)
        i = eng.dma_start(out=BC[64:65, 0:1024], in_=t["prow"][:, 0:C])
        GD.inc(i, 16)
        return (GD, GD.n)
    ctx.sec("gpsimd", gp0)

    def syw1(eng):
        i = eng.dma_start(out=RW[64:65, 2112:2112 + C], in_=t["prow"][:, 2 * C:3 * C])
        LD.inc(i, 16)
        return (LD, LD.n)
    ctx.sec("sync", syw1)
    ctx.barrier()

    with nc.psum_tensor("psQ", [1, 512], F32) as psQ, \
         nc.psum_tensor("psREP", [128, 2048], F32) as psREP, \
         nc.psum_tensor("psTR", [128, 2048], BF16) as psTR, \
         nc.psum_tensor("psSML", [128, 256], F32) as psSML:

        def s1(eng):
            i = eng.copy(out=BA[:, :], in_=R[:, :])
            SE.inc(i)
            return (SE, SE.n)
        ctx.sec("scalar", s1)

        se_cast = SE.n
        mm_q_base = MM.n
        se_q_base = SE.n

        def stat_pass(src):
            pass

        def t1(eng):
            eng.wait_ge(SE.h, se_cast)
            last = None
            for q in range(4):
                if q >= 1:
                    eng.wait_ge(SE.h, se_q_base + q)
                for cc in range(NCC):
                    last = eng.matmul(psQ[0:1, :], ones_b,
                                      BA[:, cc * T + q * 512: cc * T + (q + 1) * 512],
                                      start=(cc == 0), stop=(cc == NCC - 1))
                MM.inc(last)
            return (MM, MM.n)
        ctx.sec("tensor", t1)

        def s2(eng):
            last = None
            for q in range(4):
                eng.wait_ge(MM.h, mm_q_base + q + 1)
                last = eng.activation(RW[32:33, q * 512:(q + 1) * 512], psQ[0:1, :],
                                      AF.Copy, scale=crow(S, 2, 32))
                SE.inc(last)
            return (SE, SE.n)
        ctx.sec("scalar", s2)
        if meta.get("p5_ln", 9) < 3:
            return

        se_mu = SE.n

        def t2(eng):
            eng.wait_ge(SE.h, se_mu)
            last = None
            for q in range(4):
                last = eng.matmul(psREP[:, q * 512:(q + 1) * 512], onesF32,
                                  RW[32:33, q * 512:(q + 1) * 512], start=True, stop=True)
            MM.inc(last)
            return (MM, MM.n)
        ctx.sec("tensor", t2)

        mm_murep = MM.n

        def v1(eng):
            eng.wait_ge(MM.h, mm_murep)
            i = eng.tensor_sub(Rv, Rv, brc(psREP, 2048))
            VE.inc(i)
            return (VE, VE.n)
        ctx.sec("vector", v1)

        ve_center = VE.n

        def s3(eng):
            eng.wait_ge(VE.h, ve_center)
            i = eng.activation(BA[:, :], R[:, :], AF.Square)
            SE.inc(i)
            return (SE, SE.n)
        ctx.sec("scalar", s3)

        se_sq = SE.n
        mm_q2_base = MM.n
        se_q2_base = SE.n

        def t3(eng):
            eng.wait_ge(SE.h, se_sq)
            last = None
            for q in range(4):
                if q >= 1:
                    eng.wait_ge(SE.h, se_q2_base + q)
                for cc in range(NCC):
                    last = eng.matmul(psQ[0:1, :], ones_b,
                                      BA[:, cc * T + q * 512: cc * T + (q + 1) * 512],
                                      start=(cc == 0), stop=(cc == NCC - 1))
                MM.inc(last)
            return (MM, MM.n)
        ctx.sec("tensor", t3)

        def s4(eng):
            last = None
            for q in range(4):
                eng.wait_ge(MM.h, mm_q2_base + q + 1)
                last = eng.activation(RW[96:97, q * 512:(q + 1) * 512], psQ[0:1, :],
                                      AF.Copy, scale=crow(S, 2, 96))
                SE.inc(last)
            eng.wait_ge(SE.h, SE.n)
            last = eng.activation(RW[64:65, 0:2048], RW[96:97, 0:2048], AF.Sqrt,
                                  bias=crow(S, 3, 64))
            SE.inc(last)
            return (SE, SE.n)
        ctx.sec("scalar", s4)
        if meta.get("p5_ln", 9) < 5:
            return

        se_s = SE.n

        def v2(eng):
            eng.wait_ge(SE.h, se_s)
            i = eng.reciprocal(RW[64:65, 0:2048], RW[64:65, 0:2048])
            VE.inc(i)
            return (VE, VE.n)
        ctx.sec("vector", v2)

        ve_rs = VE.n

        def s5(eng):
            eng.wait_ge(VE.h, ve_rs)
            i = eng.activation(RW[32:33, 0:2048], RW[64:65, 0:2048], AF.Square)
            SE.inc(i)
            return (SE, SE.n)
        ctx.sec("scalar", s5)

        se_rssq = SE.n

        def v3(eng):
            eng.wait_ge(SE.h, se_rssq)
            i = eng.tensor_scalar(RW[32:33, 0:2048], RW[32:33, 0:2048],
                                  -LN_EPS, 1.0, OP.mult, OP.add)
            VE.inc(i)
            return (VE, VE.n)
        ctx.sec("vector", v3)

        ve_varv = VE.n

        def s6(eng):
            eng.wait_ge(VE.h, ve_varv)
            i = eng.activation(RW[32:33, 0:2048], RW[32:33, 0:2048], AF.Sqrt,
                               bias=crow(S, 3, 32))
            SE.inc(i)
            return (SE, SE.n)
        ctx.sec("scalar", s6)

        se_s2 = SE.n

        def v4(eng):
            eng.wait_ge(SE.h, se_s2)
            i = eng.reciprocal(RW[32:33, 0:2048], RW[32:33, 0:2048])
            VE.inc(i)
            return (VE, VE.n)
        ctx.sec("vector", v4)

        ve_sv = VE.n

        def t4(eng):
            eng.wait_ge(VE.h, ve_sv)
            last = None
            for q in range(4):
                last = eng.matmul(psREP[:, q * 512:(q + 1) * 512], onesF64,
                                  RW[64:65, q * 512:(q + 1) * 512], start=True, stop=True)
            MM.inc(last)
            return (MM, MM.n)
        ctx.sec("tensor", t4)

        mm_rsrep = MM.n

        def v5a(eng):
            eng.wait_ge(MM.h, mm_rsrep)
            i = eng.tensor_mul(BAv, Rv, brc(psREP, 2048))
            VE.inc(i)
            return (VE, VE.n)
        ctx.sec("vector", v5a)

        ve_v1 = VE.n

        def t4b(eng):
            eng.wait_ge(VE.h, ve_v1)
            last = None
            for q in range(4):
                last = eng.matmul(psREP[:, q * 512:(q + 1) * 512], onesF32,
                                  RW[32:33, q * 512:(q + 1) * 512], start=True, stop=True)
            MM.inc(last)
            return (MM, MM.n)
        ctx.sec("tensor", t4b)

        mm_rs2rep = MM.n

        def v5(eng):
            eng.wait_ge(MM.h, mm_rs2rep)
            i = eng.tensor_mul(BAv, BAv, brc(psREP, 2048))
            VE.inc(i)
            return (VE, VE.n)
        ctx.sec("vector", v5)
        if meta.get("p2_stage", 99) < 3:
            return

        ve_v = VE.n

        jobs = [(i_, cc) for i_ in range(NTT) for cc in range(NCC)]
        mm_tr = MM.n
        se_tr = SE.n

        def t5(eng):
            eng.wait_ge(VE.h, ve_v)
            last = None
            for j, (i_, cc) in enumerate(jobs):
                if j >= 2:
                    eng.wait_ge(SE.h, se_tr + j - 1)
                last = eng.transpose(psTR[:, (j % 2) * 1024:(j % 2) * 1024 + 128],
                                     BA[:, cc * T + i_ * 128: cc * T + i_ * 128 + 128],
                                     CB[:, 0:128])
                MM.inc(last)
            return (MM, MM.n)
        ctx.sec("tensor", t5)

        def s7(eng):
            last = None
            for j, (i_, cc) in enumerate(jobs):
                eng.wait_ge(MM.h, mm_tr + j + 1)
                last = eng.copy(
                    out=BB[:, i_ * 1024 + cc * 128: i_ * 1024 + cc * 128 + 128],
                    in_=psTR[:, (j % 2) * 1024:(j % 2) * 1024 + 128])
                SE.inc(last)
            return (SE, SE.n)
        ctx.sec("scalar", s7)

        ctx.barrier()
        if meta.get("p2_stage", 99) < 4:
            return

        def s8(eng):
            i = eng.activation(RW[32:33, 2112:2112 + N + 1],
                               RW[32:33, 2112:2112 + N + 1], AF.Exp)
            ctx.chain(eng, SE, i)
            i = eng.activation(RW[32:33, 2112:2112 + N + 1],
                               RW[32:33, 2112:2112 + N + 1], AF.Ln,
                               bias=crow(S, 0, 32))
            SE.inc(i)
            return (SE, SE.n)
        ctx.sec("scalar", s8)

        se_tau = SE.n

        def t6(eng):
            eng.wait_ge(SE.h, se_tau)
            i = eng.matmul(psSML[0:1, 0:N], RW[32:33, 2112 + N:2112 + N + 1],
                           RW[32:33, 2112:2112 + N],
                           start=True, stop=True)
            ctx.chain(eng, MM, i)
            i = eng.matmul(psSML[0:64, 128:129], RW[32:33, 2112:2112 + N],
                           RW[32:33, 2112 + N:2112 + N + 1],
                           start=True, stop=True)
            MM.inc(i)
            return (MM, MM.n)
        ctx.sec("tensor", t6)

        mm_lam = MM.n

        def v6(eng):
            eng.wait_ge(MM.h, mm_lam)
            eng.tensor_copy(out=RW[32:33, 2048:2048 + N], in_=psSML[0:1, 0:N])
            i = eng.tensor_copy(out=CN[0:64, 130:131], in_=psSML[0:64, 128:129])
            ctx.chain(eng, VE, i)
            eng.tensor_scalar_mul(CN[0:64, 131:132], CN[0:64, 130:131], -1.0)
            eng.tensor_scalar_mul(CN[0:64, 132:133], CN[0:64, 130:131], -128.0)
            eng.tensor_scalar_mul(CN[0:64, 138:139], CN[0:64, 130:131], -127.0)
            i = eng.tensor_scalar(CN[:, 137:138], CN[:, 128:129], -127.0, None, OP.add)
            i = eng.tensor_scalar(RW[32:33, 0:2048], RW[0:1, 0:2048], 1.0, None, OP.add)
            VE.inc(i)
            return (VE, VE.n)
        ctx.sec("vector", v6)

        ve_lam = VE.n
        mm_er_base = MM.n

        def t7(eng):
            eng.wait_ge(VE.h, ve_lam)
            eng.matmul(psSML[0:64, 0:128], onesF[0:1, 0:64], RW[0:1, 0:128],
                       start=True, stop=True)
            i = eng.matmul(psREP[0:128, 0:64], onesF32, RW[32:33, 2048:2048 + N],
                           start=True, stop=True)
            for h in range(2):
                i = eng.matmul(psREP[0:64, 512 + h * 512:512 + (h + 1) * 512],
                               onesF64[0:1, 0:64],
                               RW[64:65, 2112 + h * 512:2112 + (h + 1) * 512],
                               start=True, stop=True)
            MM.inc(i)
            return (MM, MM.n)
        ctx.sec("tensor", t7)

        mm_emat = MM.n

        def s9(eng):
            eng.wait_ge(MM.h, mm_emat)
            eng.activation(CN[0:64, 133:134], CN[0:64, 131:132], AF.Exp)
            eng.activation(CN[0:64, 134:135], CN[0:64, 132:133], AF.Exp)
            eng.activation(CB[0:64, 192:320], psSML[0:64, 0:128], AF.Exp,
                           scale=CN[0:64, 130:131], bias=CN[0:64, 138:139])
            eng.activation(CB[0:64, 320:448], psSML[0:64, 0:128], AF.Exp,
                           scale=CN[0:64, 130:131], bias=CN[0:64, 132:133])
            i = eng.activation(CB[:, 448:512], psREP[0:128, 0:64], AF.Exp,
                               scale=CN[:, 137:138])
            SE.inc(i)
            return (SE, SE.n)
        ctx.sec("scalar", s9)
        if meta.get("p2_stage", 99) < 5:
            return

        se_emat = SE.n

        def v7(eng):
            eng.wait_ge(SE.h, se_emat)
            eng.tensor_mul(BC[0:64, 5120:6144], BC[0:64, 0:1024],
                           psREP[0:64, 512:512 + 1024])
            i = eng.tensor_scalar(CN[0:64, 135:136], CN[0:64, 133:134], -1.0, 1.0,
                                  OP.mult, OP.add)
            ctx.chain(eng, VE, i)
            i = eng.reciprocal(CN[0:64, 135:136], CN[0:64, 135:136])
            ctx.chain(eng, VE, i)
            i = eng.tensor_scalar_mul(CN[0:64, 136:137], CN[0:64, 135:136], -1.0)
            VE.inc(i)
            return (VE, VE.n)
        ctx.sec("vector", v7)

        ve_r1a = VE.n

        def t8(eng):
            eng.wait_ge(VE.h, ve_r1a)
            last = None
            for q in range(4):
                last = eng.matmul(psREP[0:64, q * 512:(q + 1) * 512], onesF32[0:1, 0:64],
                                  RW[32:33, q * 512:(q + 1) * 512], start=True, stop=True)
            MM.inc(last)
            return (MM, MM.n)
        ctx.sec("tensor", t8)

        mm_grep = MM.n

        def s10(eng):
            eng.wait_ge(MM.h, mm_grep)
            i = eng.activation(BC[0:64, 3072:5120], psREP[0:64, 0:2048], AF.Exp,
                               scale=CN[0:64, 131:132])
            SE.inc(i)
            return (SE, SE.n)
        ctx.sec("scalar", s10)

        se_w = SE.n

        def v8(eng):
            eng.wait_ge(SE.h, se_w)
            i = eng.tensor_scalar(BC[0:64, 1024:3072], BC[0:64, 3072:5120],
                                  CN[0:64, 136:137], CN[0:64, 135:136],
                                  OP.mult, OP.add)
            ctx.chain(eng, VE, i)
            i = eng.memset(BC[64:65, 1024:3072], 1.0)
            for cc in range(NCC):
                i = eng.tensor_mul(SM[:, 16 * cc + 8:16 * cc + 9],
                                   SM[:, 16 * cc + 2:16 * cc + 3],
                                   SM[:, 16 * cc + 0:16 * cc + 1])
            VE.inc(i)
            return (VE, VE.n)
        ctx.sec("vector", v8)
        if meta.get("p2_stage", 99) < 6:
            return

        ve_G = VE.n

        mm_kp = MM.n
        se_kp = SE.n
        ve_kp = VE.n
        ld_kp = LD.n

        def vz0(eng):
            i = eng.memset(CB[:, 512 + 128:512 + 256] if False else CB[:, 1536:1664], 0.0)
            VE.inc(i)
            return (VE, VE.n)
        ctx.sec("vector", vz0)

        ve_z0 = VE.n

        def t9(eng):
            eng.wait_ge(VE.h, ve_G)
            last = None
            for cc in range(NCC):
                if cc >= 1:
                    eng.wait_ge(SE.h, se_kp + cc)
                last = eng.matmul(psSML[:, (cc % 2) * 128:(cc % 2) * 128 + 128],
                                  BC[0:64, cc * 128:(cc + 1) * 128],
                                  CB[0:64, 192:320], start=True, stop=True)
                MM.inc(last)
            return (MM, MM.n)
        ctx.sec("tensor", t9)

        def s11(eng):
            last = None
            for cc in range(NCC):
                eng.wait_ge(MM.h, mm_kp + cc + 1)
                if cc >= 1:
                    eng.wait_ge(VE.h, ve_z0 + cc)
                    eng.wait_ge(LD.h, ld_kp + 16 * cc)
                last = eng.activation(CB[:, 512:640],
                                      psSML[:, (cc % 2) * 128:(cc % 2) * 128 + 128],
                                      AF.Copy, scale=SM[:, 16 * cc:16 * cc + 1])
                SE.inc(last)
            return (SE, SE.n)
        ctx.sec("scalar", s11)

        def v9(eng):
            last = None
            for cc in range(NCC):
                eng.wait_ge(SE.h, se_kp + cc + 1)
                last = eng.tensor_tensor(CB[:, 639:640], CB[:, 639:640],
                                         SM[:, 16 * cc + 8:16 * cc + 9], OP.add)
                VE.inc(last)
            return (VE, VE.n)
        ctx.sec("vector", v9)

        def sy1(eng):
            last = None
            for cc in range(NCC):
                eng.wait_ge(VE.h, ve_z0 + cc + 1)
                last = eng.dma_start(out=t["kpad"][cc * 128:(cc + 1) * 128, 0:128],
                                     in_=CB[:, 512:640])
                LD.inc(last, 16)
            eng.wait_ge(VE.h, ve_z0)
            last = eng.dma_start(
                out=_sl(t["kpad"], 128, [[256, 128], [128 * 256, 8], [1, 128]]),
                in_=_sl(CB.ap(), 1536, [[3652, 128], [0, 8], [1, 128]]))
            LD.inc(last, 16)
            return (LD, LD.n)
        ctx.sec("sync", sy1)

        ctx.barrier()

    if meta.get("p2_stage", 99) < 7:
        return
    emit_p2b(ctx, meta)


def emit_p2b(ctx, meta):
    nc, t, S = ctx.nc, ctx.t, ctx.s
    LD, GD, MM, VE, SE, TZ = (ctx.sm[k] for k in ("LD", "GD", "MM", "VE", "SE", "TZ"))
    R, BA, BB, BC, W, RW, SC, CN, CB, SM = (
        S[k] for k in ("R", "BA", "BB", "BC", "W", "RW", "SC", "CN", "CB", "SM"))
    onesF = CN[0:1, 140:268]
    onesF64 = CN[64:65, 140:268]

    with nc.psum_tensor("psSL", [64, 2048], F32) as psSL, \
         nc.psum_tensor("psEX", [128, 2048], F32) as psEX:

        def vz(eng):
            eng.memset(SC[:, :], 0.0)
            i = eng.memset(BA[:, :], 0.0)
            VE.inc(i)
            return (VE, VE.n)
        ctx.sec("vector", vz)

        ve0 = VE.n
        mm0 = MM.n

        def t10(eng):
            eng.wait_ge(VE.h, ve0)
            last = None
            for i_ in range(NCH):
                if i_ >= 2:
                    tgt = 2 if i_ - 2 == 0 else 3 * (i_ - 2) + 1
                    eng.wait_ge(VE.h, ve0 + tgt)
                for h in range(2):
                    last = eng.matmul(
                        psSL[:, (i_ % 2) * 1024 + h * 512:(i_ % 2) * 1024 + (h + 1) * 512],
                        CB[:, 448:512],
                        BB[:, i_ * 1024 + h * 512: i_ * 1024 + (h + 1) * 512],
                        start=True, stop=True)
                MM.inc(last)
                if i_ >= 1:
                    eng.wait_ge(VE.h, ve0 + 3 * i_)
                    if i_ >= 3:
                        eng.wait_ge(VE.h, ve0 + 3 * (i_ - 2) + 2)
                    for h in range(2):
                        last = eng.matmul(
                            psEX[:, (i_ % 2) * 1024 + h * 512:(i_ % 2) * 1024 + (h + 1) * 512],
                            CB[0:64, 320:448],
                            BC[0:64, 3072 + (i_ % 2) * 1024 + h * 512:
                               3072 + (i_ % 2) * 1024 + (h + 1) * 512],
                            start=True, stop=True)
                    MM.inc(last)
            return (MM, MM.n)
        ctx.sec("tensor", t10)

        def v10(eng):
            last = None
            for i_ in range(NCH):
                if i_ == 0:
                    i = eng.engine_nop()
                    VE.inc(i)
                    eng.wait_ge(MM.h, mm0 + 1)
                    last = eng.scalar_tensor_tensor(
                        SC[:, :], SC[:, :], CN[0:64, 134:135],
                        psSL[:, 0:1024], OP.mult, OP.add)
                    VE.inc(last)
                else:
                    eng.wait_ge(VE.h, ve0 + (2 if i_ == 1 else 3 * (i_ - 1) + 1))
                    if i_ >= 3:
                        eng.wait_ge(MM.h, mm0 + 2 * (i_ - 2) + 1)
                    last = eng.tensor_mul(
                        BC[0:64, 3072 + (i_ % 2) * 1024:3072 + (i_ % 2) * 1024 + 1024],
                        BC[0:64, 5120:6144], SC[:, :])
                    VE.inc(last)
                    eng.wait_ge(MM.h, mm0 + 2 * i_)
                    last = eng.scalar_tensor_tensor(
                        SC[:, :], SC[:, :], CN[0:64, 134:135],
                        psSL[:, (i_ % 2) * 1024:(i_ % 2) * 1024 + 1024],
                        OP.mult, OP.add)
                    VE.inc(last)
                    eng.wait_ge(MM.h, mm0 + 2 * i_ + 1)
                    last = eng.tensor_add(
                        BA[:, i_ * 1024:(i_ + 1) * 1024],
                        BA[:, i_ * 1024:(i_ + 1) * 1024],
                        psEX[:, (i_ % 2) * 1024:(i_ % 2) * 1024 + 1024])
                    VE.inc(last)
            return (VE, VE.n)
        ctx.sec("vector", v10)

        ctx.barrier()
    if meta.get("p2_stage", 99) < 8:
        return

    with nc.psum_tensor("psI", [128, 4096], F32) as psI:
        TA, TB2 = ctx.sm["TA"], ctx.sm["TB"]
        tpsem = [TA, TB2]
        mm1 = MM.n
        ve1 = VE.n
        tz0 = TZ.n
        ta0, tb0 = TA.n, TB2.n

        def syT(eng):
            last = None
            for g in range(32):
                if g >= 2:
                    eng.wait_ge(TZ.h, tz0 + g - 1)
                c0 = g * 32
                src = _sl(t["kpad"], c0 * 256, [[1, 128], [256, 32], [1, 128]])
                dst = _sl(W.ap(), (g % 2) * 4096, [[8192, 128], [128, 32], [1, 128]])
                last = eng.dma_start(out=dst, in_=src)
                tpsem[g % 2].inc(last, 16)
            eng.wait_ge(TA.h, TA.n)
            return (TB2, TB2.n)
        ctx.sec("sync", syT)

        _imode = meta.get("p2_intra", "full")

        def t11(eng):
            last = None
            base = [ta0, tb0]
            for g in range(32):
                eng.wait_ge(tpsem[g % 2].h, base[g % 2] + 16 * (g // 2 + 1))
                r = g // 4
                if g >= 8:
                    eng.wait_ge(VE.h, ve1 + (r - 1) * NCH)
                if _imode == "dma":
                    last = eng.wait_ge(MM.h, 0)
                    TZ.inc(last)
                    continue
                for cl in range(32):
                    c = g * 32 + cl
                    rhs = _sl(BB.ap(), c, [[17664, 128], [1024, 16]])
                    cloc = (g % 4) * 32 + cl
                    cb_ = (r % 2) * 2048 + (cloc % 4) * 512 + (cloc // 4) * 16
                    last = eng.matmul(
                        psI[:, cb_:cb_ + 16],
                        W[:, (g % 2) * 4096 + cl * 128:(g % 2) * 4096 + cl * 128 + 128],
                        rhs, start=True, stop=True)
                    if _imode == "ser" and cl < 31:
                        ctx.chain(eng, MM, last)
                TZ.inc(last)
            return (TZ, TZ.n)
        ctx.sec("tensor", t11)

        def v11(eng):
            last = None
            for r in range(8):
                eng.wait_ge(TZ.h, tz0 + 4 * (r + 1))
                for i_ in range(NCH):
                    src = _sl(psI.ap(), (r % 2) * 2048 + i_,
                              [[4096, 128], [512, 4], [16, 32]])
                    dstc = _sl(BA.ap(), i_ * 1024 + r * 128,
                               [[16384, 128], [1, 4], [4, 32]])
                    last = eng.tensor_add(dstc, dstc, src)
                    VE.inc(last)
            return (VE, VE.n)
        ctx.sec("vector", v11)

        ctx.barrier()
    if meta.get("p2_stage", 99) < 9:
        return

    with nc.psum_tensor("psREP2", [128, 2048], F32) as psREP2:
        def t12(eng):
            last = None
            for q in range(4):
                last = eng.matmul(psREP2[:, q * 512:(q + 1) * 512], onesF64,
                                  RW[64:65, q * 512:(q + 1) * 512], start=True, stop=True)
            MM.inc(last)
            return (MM, MM.n)
        ctx.sec("tensor", t12)

        mm_rs = MM.n

        def v12(eng):
            eng.wait_ge(MM.h, mm_rs)
            Rv = _sl(R.ap(), 0, [[16384, 128], [T, NCC], [1, T]])
            i = eng.tensor_mul(Rv, Rv,
                               _sl(psREP2.ap(), 0, [[2048, 128], [0, NCC], [1, T]]))
            VE.inc(i)
            return (VE, VE.n)
        ctx.sec("vector", v12)
        ctx.barrier()

    with nc.psum_tensor("psTR2", [128, 2048], BF16) as psTR2, \
         nc.psum_tensor("psKD", [128, 2048], F32) as psKD:
        jobs2 = [(i_, cc) for i_ in range(NTT) for cc in range(NCC)]
        mm_tb = MM.n
        ve_tb = VE.n

        def t13(eng):
            last = None
            for j, (i_, cc) in enumerate(jobs2):
                if j >= 2:
                    eng.wait_ge(VE.h, ve_tb + j - 1)
                last = eng.transpose(psTR2[:, (j % 2) * 1024:(j % 2) * 1024 + 128],
                                     BA[:, i_ * 1024 + cc * 128: i_ * 1024 + cc * 128 + 128],
                                     CB[:, 3524:3524 + 128])
                MM.inc(last)
            return (MM, MM.n)
        ctx.sec("tensor", t13)

        def v13(eng):
            last = None
            for j, (i_, cc) in enumerate(jobs2):
                eng.wait_ge(MM.h, mm_tb + j + 1)
                dst = R[:, cc * T + i_ * 128: cc * T + i_ * 128 + 128]
                last = eng.tensor_add(dst, dst,
                                      psTR2[:, (j % 2) * 1024:(j % 2) * 1024 + 128])
                VE.inc(last)
            return (VE, VE.n)
        ctx.sec("vector", v13)

        mm_kd = MM.n
        ve_after_tb = ve_tb + len(jobs2)

        def t14(eng):
            eng.wait_ge(VE.h, ve_after_tb)
            last = None
            for cc in range(NCC):
                if cc >= 1:
                    eng.wait_ge(VE.h, ve_after_tb + cc)
                for q in range(4):
                    last = eng.matmul(psKD[:, q * 512:(q + 1) * 512],
                                      BC[0:65, cc * 128:(cc + 1) * 128],
                                      BC[0:65, 1024 + q * 512:1024 + (q + 1) * 512],
                                      start=True, stop=True)
                MM.inc(last)
            return (MM, MM.n)
        ctx.sec("tensor", t14)

        def v14(eng):
            last = None
            for cc in range(NCC):
                eng.wait_ge(MM.h, mm_kd + cc + 1)
                last = eng.scalar_tensor_tensor(
                    R[:, cc * T:(cc + 1) * T], psKD[:, 0:2048],
                    SM[:, 16 * cc + 1:16 * cc + 2], R[:, cc * T:(cc + 1) * T],
                    OP.mult, OP.add)
                VE.inc(last)
            return (VE, VE.n)
        ctx.sec("vector", v14)



def _plan(inputs):
    top_idx = np.asarray(inputs["L0_ffn_top_idx"])
    tok = top_idx.T.astype(np.int64)
    T_ = T

    perms, tok_sorted = [], []
    for e in range(NE):
        p = np.argsort(tok[e] % T_, kind="stable")
        perms.append(p)
        tok_sorted.append(tok[e][p])

    nch_need = np.zeros((NE, 16), np.int64)
    chunk_starts = [[None] * 16 for _ in range(NE)]
    for e in range(NE):
        loc = tok_sorted[e] % T_
        for ti in range(16):
            a_ = int(np.searchsorted(loc, ti * 128, side="left"))
            b_ = int(np.searchsorted(loc, ti * 128 + 128, side="left"))
            c0 = a_ // 128
            c1 = max((b_ + 127) // 128, c0 + 1)
            nch_need[e, ti] = c1 - c0
            chunk_starts[e][ti] = c0
    nchunk = int(nch_need.max())
    grid = [[None] * nchunk for _ in range(16)]
    rowstart = np.zeros((NE, 16, nchunk), np.int64)
    for e in range(NE):
        for ti in range(16):
            for ch in range(nchunk):
                s = min(chunk_starts[e][ti] + ch, 2048 // 128 - 1) * 128
                rowstart[e, ti, ch] = s

    meta = dict(nchunk=nchunk, rowstart=rowstart)
    plans = dict(perms=perms, tok_sorted=tok_sorted, rowstart=rowstart,
                 nchunk=nchunk)
    return meta, plans


def _build_sel(plans, core):
    NCH4 = plans["nchunk"]
    sel = np.zeros((16 * NE * NCH4 * 128, 128), np.float32)
    for ti in range(16):
        for e in range(NE):
            ts = plans["tok_sorted"][e]
            for ch in range(NCH4):
                s = plans["rowstart"][e, ti, ch]
                if ch > 0 and s == plans["rowstart"][e, ti, ch - 1]:
                    continue
                blk = (ti * NE + e) * NCH4 + ch
                rows = ts[s:s + 128]
                tl = rows - (core * T + ti * 128)
                jj = np.nonzero((tl >= 0) & (tl < 128))[0]
                sel[blk * 128 + jj, tl[jj]] = 1.0
    sel = (sel.reshape(16, NE * NCH4, 128, 128)
              .transpose(0, 2, 1, 3)
              .reshape(16 * NE * NCH4 * 128, 128))
    return np.ascontiguousarray(sel).astype(BF16NP)


def _consts_row():
    c = np.zeros(4096, np.float32)
    c[0:2048] = np.arange(2048, dtype=np.float32)
    c[2048 + 0] = 1.0
    c[2048 + 1] = -1.0
    c[2048 + 2] = 1.0 / 1024.0
    c[2048 + 3] = LN_EPS
    c[2048 + 4] = 0.5
    c[2048 + 5] = 1.0 / np.sqrt(2.0)
    c[2048 + 6] = ATT_EPS
    c[2048 + 7] = DH ** -0.5
    return c[None, :]


def make_in_maps(inputs, meta, plans):
    f32 = lambda a: np.ascontiguousarray(np.asarray(a), dtype=np.float32)
    x = f32(inputs["L0_attn_x"])
    xhead = np.ascontiguousarray(x[0:4].reshape(4 * T, C))
    cwbase = f32(inputs["L0_ffn_combine_weights"])
    tokT = np.asarray(inputs["L0_ffn_top_idx"]).T.astype(np.int64)

    pp0 = np.zeros((C, 16), np.float32)
    pp0[:, 0] = f32(inputs["L0_attn_norm1_w"])
    pp0[:, 1] = f32(inputs["L0_attn_norm1_b"])
    pp0[:, 2] = f32(inputs["L0_attn_D"])
    pp0[:, 3] = f32(inputs["L0_attn_norm2_w"])
    pp0[:, 4] = f32(inputs["L0_attn_norm2_b"])
    pp0[:, 5] = f32(inputs["L0_attn_fc2_b"])
    pp0[:, 6] = f32(inputs["L1_ffn_mu_r"])
    pp0[:, 7] = f32(inputs["L1_ffn_mu_k"])

    stau = np.zeros((1, N + 1), np.float32)
    stau[0, 0:N] = f32(inputs["L0_attn_log_tau"])
    stau[0, N] = np.float32(np.asarray(inputs["L0_attn_log_dt"]))

    consts = _consts_row()
    cconst = np.tile(consts[:, 2048:2064], (128, 1)).astype(np.float32)
    idn = np.eye(128, dtype=np.float32)
    jdn = np.eye(128, dtype=np.float32)[:, ::-1].copy()
    tri = np.triu(np.ones((128, 128), np.float32))

    mixT = np.ascontiguousarray(f32(inputs["L0_attn_mix"]).T)
    fc1T = np.ascontiguousarray(f32(inputs["L0_attn_fc1_w"]).T)
    fc2T = np.ascontiguousarray(f32(inputs["L0_attn_fc2_w"]).T)
    WqT = np.ascontiguousarray(f32(inputs["L1_attn_Wq"]).T)
    WkT = np.ascontiguousarray(f32(inputs["L1_attn_Wk"]).T)
    WvT = np.ascontiguousarray(f32(inputs["L1_attn_Wv"]).T)
    WoT = np.ascontiguousarray(f32(inputs["L1_attn_Wo"]).T)
    WrT = np.ascontiguousarray(f32(inputs["L1_ffn_W_r"]).T)
    WkkT = np.ascontiguousarray(f32(inputs["L1_ffn_W_k"]).T)
    WvvT = np.ascontiguousarray(f32(inputs["L1_ffn_W_v"]).T)
    W1 = f32(inputs["L0_ffn_W1"])
    W2 = f32(inputs["L0_ffn_W2"])
    b1 = f32(inputs["L0_ffn_b1"])
    b2 = f32(inputs["L0_ffn_b2"])
    fc1_b = f32(inputs["L0_attn_fc1_b"])

    in_maps = []
    for k in range(8):
        srt = plans["tok_sorted"][k]
        xg = np.ascontiguousarray(xhead[srt])
        wvals = cwbase[srt, k].astype(np.float32)
        wg = np.ascontiguousarray(wvals.reshape(16, 128).T)
        sel = _build_sel(plans, k)
        fcb = np.stack([fc1_b, b1[k]], 1)
        prow = np.concatenate([f32(inputs["L0_attn_D"]), b2[k],
                               f32(inputs["L0_attn_norm1_w"])])[None, :]
        in_maps.append(dict(
            xT=np.ascontiguousarray(x[k].T),
            xg=xg, wg=wg, sel=sel,
            mixT=mixT, stau=stau, pp0=pp0, prow=prow, fcb=fcb,
            fc1T=fc1T, fc2T=fc2T,
            W1e=np.ascontiguousarray(W1[k]),
            W2e=np.ascontiguousarray(W2[k]),
            WqT=WqT, WkT=WkT, WvT=WvT, WoT=WoT,
            WrT=WrT, WkkT=WkkT, WvvT=WvvT,
            consts=consts, cconst=cconst, idn=idn, jdn=jdn, tri=tri,
        ))
    return in_maps


def kernel(**inputs):
    meta, plans = _plan(inputs)
    nc = build_program(meta)
    in_maps = make_in_maps(inputs, meta, plans)
    res = run_bass_kernel_spmd(nc, in_maps, core_ids=list(range(8)))
    out = np.stack([np.asarray(res.results[k]["yT"]).T for k in range(8)], axis=0)
    return np.ascontiguousarray(out)
